# version 02
# baseline (speedup 1.0000x reference)
import math
from collections import defaultdict
from contextlib import ExitStack

import numpy as np
import ml_dtypes

import concourse.bass as bass
import concourse.mybir as mybir
from concourse.bass_utils import run_bass_kernel_spmd

F32 = mybir.dt.float32
BF16 = mybir.dt.bfloat16
AF = mybir.ActivationFunctionType
ALU = mybir.AluOpType
AX = mybir.AxisListType

NCORES = 8
B_PER_CORE = 2
T_BATCH = 2048
T_CORE = B_PER_CORE * T_BATCH
D = 1024
NCH = D // 128
NT = 512
NTT = T_CORE // NT
HEADS = 16
HD = 64
BLK = 256
NEG = np.float32(-1e38)

PERM = np.argsort(np.array([0, 3, 6, 1, 4, 7, 2, 5]))



def _rope_perm64():
    return np.concatenate([np.arange(0, 64, 2), np.arange(1, 64, 2)])


def _prep_qkvT(wqkv):
    p64 = _rope_perm64()
    newrows = np.arange(3072)
    for c in range(2):
        for h in range(HEADS):
            base = c * 1024 + h * 64
            newrows[base:base + 64] = base + p64
    return np.ascontiguousarray(wqkv[newrows].T)


def _vprep(v):
    v = np.asarray(v)
    return np.ascontiguousarray(v.reshape(-1, 128).T)


def _rope_tables():
    inv_freq = 1.0 / (10000.0 ** (np.arange(0, HD, 2, dtype=np.float64) / HD))
    t = np.arange(T_BATCH, dtype=np.float64)
    fr = inv_freq[:, None] * t[None, :]
    cos = np.cos(fr)
    sin = np.sin(fr)
    c1 = np.empty((128, T_BATCH), np.float64)
    c2 = np.empty((128, T_BATCH), np.float64)
    for half in range(2):
        base = half * 64
        c1[base:base + 32] = cos
        c1[base + 32:base + 64] = cos
        c2[base:base + 32] = -sin
        c2[base + 32:base + 64] = sin
    return c1.astype(np.float32), c2.astype(np.float32)


def _causal_masks():
    j = np.arange(BLK)[None, :]
    i0 = np.arange(128)[:, None]
    m0 = np.where(j <= i0, np.float32(0), NEG).astype(np.float32)
    m1 = np.where(j <= i0 + 128, np.float32(0), NEG).astype(np.float32)
    return m0, m1


def _noise_constant():
    import jax
    with jax.default_device(jax.devices("cpu")[0]):
        noise = np.asarray(jax.random.normal(jax.random.key(1), (16, T_BATCH, 8)),
                           dtype=np.float32)
    return noise



ENGS = ("sync", "gpsimd", "tensor", "vector", "scalar")


class B:
    def __init__(self, nc, sems):
        self.nc = nc
        self.sems = sems
        self.ops = {e: [] for e in ENGS}
        self.cnt = defaultdict(int)

    def op(self, eng, fn, sem=None, n=1):
        if sem is None:
            self.ops[eng].append(lambda q, fn=fn: fn(q))
        else:
            s = self.sems[sem]
            self.ops[eng].append(lambda q, fn=fn, s=s, n=n: fn(q).then_inc(s, n))
            self.cnt[sem] += n

    def opf(self, eng, fn, sem, n=1):
        self.ops[eng].append(lambda q, fn=fn: fn(q))
        s = self.sems[sem]
        self.ops[eng].append(lambda q, s=s, n=n: q.drain().then_inc(s, n))
        self.cnt[sem] += n

    def fence(self, eng):
        self.ops[eng].append(lambda q: q.drain())

    def wait(self, eng, sem, val=None):
        if val is None:
            val = self.cnt[sem]
        if val <= 0:
            return
        s = self.sems[sem]
        self.ops[eng].append(lambda q, s=s, v=val: q.wait_ge(s, v))

    def val(self, sem):
        return self.cnt[sem]



def build_program(phases=(1, 2, 3, 4), dbg=False):
    nc = bass.Bass("TRN2", target_bir_lowering=False)
    es = ExitStack()

    di = lambda nm, shape, dt=F32: nc.dram_tensor(nm, list(shape), dt, kind="ExternalInput").ap()
    do = lambda nm, shape, dt=F32: nc.dram_tensor(nm, list(shape), dt, kind="ExternalOutput").ap()

    inp = {}
    inp["x0"] = di("x0", [D, T_CORE])
    xout = do("xout", [D, T_CORE])

    last = max(phases)
    xbuf = {0: inp["x0"]}
    for p in phases:
        if p == last:
            xbuf[p] = xout
        else:
            xbuf[p] = nc.dram_tensor(f"xb{p}", [D, T_CORE], F32, kind="Internal").ap()

    dbgo = {}
    if dbg:
        for nm, shape in [("d_xln", [128, NCH * NT]), ("d_qkv", [128, 3 * NCH * NT]),
                          ("d_att", [128, NCH * NT]), ("d_o2", [128, NCH * NT]),
                          ("d_vT", [128, 4 * NCH * 128])]:
            dbgo[nm] = do(nm, shape)
        for nm, shape in [("d4_xe", [128, NCH * NT]), ("d4_hs", [128, 16 * NT]),
                          ("d4_hm", [128, 16 * NT]), ("d4_acc0", [128, NCH * NT]),
                          ("d4_xf", [128, NCH * NT]), ("d4_w", [8, T_CORE]),
                          ("d4_acc", [128, NCH * NT]), ("d4_o2", [128, NCH * NT]),
                          ("d4_hid", [128, NCH * NT]),
                          ("d_hi0", [128, 1]), ("d_cnt0", [128, 1]), ("d_dsq0", [128, 1]),
                          ("d_impg", [128, 256]),
                          ("d_vstar", [1, 1]), ("d_temp", [1, 1]), ("d_flag", [1, 1]),
                          ("d_srow", [1, NT]), ("d_imp", [1, NT]), ("d_xs", [128, NCH * NT]),
                          ("d_lg", [8, NT]), ("d_gall", [8, T_CORE]),
                          ("d_fin", [128, NCH * NT]), ("d_acc", [128, NCH * NT])]:
            dbgo[nm] = do(nm, shape)

    inp["rc1"] = di("rc1", [128, T_BATCH])
    inp["rc2"] = di("rc2", [128, T_BATCH])
    inp["mask0"] = di("mask0", [128, BLK])
    inp["mask1"] = di("mask1", [128, BLK])
    inp["ident"] = di("ident", [128, 128])

    if 1 in phases:
        inp["wqkvT"] = di("wqkvT", [D, 3 * D])
        inp["woutT"] = di("woutT", [D, D])
    if 3 in phases:
        inp["l1_wqkvT"] = di("l1_wqkvT", [D, 3 * D])
        inp["l1_woutT"] = di("l1_woutT", [D, D])

    E_SPECS = [
        [("w_a%d_l0" % a, 1024, 4096, "gelu"), ("w_a%d_l1" % a, 4096, 1024, None)]
        for a in range(3)
    ] + [
        [("w_a%d_l0" % a, 1024, 2048, "silu"), ("w_a%d_l1" % a, 2048, 2048, "silu"),
         ("w_a%d_l2" % a, 2048, 1024, None)]
        for a in range(3, 6)
    ] + [
        [("w_a%d_l0" % a, 1024, 3072, "ss"), ("w_a%d_l1" % a, 3072, 1024, None)]
        for a in range(6, 8)
    ]
    if 4 in phases:
        inp["l1_ln_w"] = di("l1_ln_w", [128, 8])
        inp["l1_ln_b"] = di("l1_ln_b", [128, 8])
        inp["l1_router"] = di("l1_router", [D, 8])
        inp["l1_gate"] = di("l1_gate", [8, D, 2 * D])
        inp["l1_up"] = di("l1_up", [8, D, 2 * D])
        inp["l1_down"] = di("l1_down", [8, 2 * D, D])
        inp["l1_shg"] = di("l1_shg", [D, 2 * D])
        inp["l1_shu"] = di("l1_shu", [D, 2 * D])
        inp["l1_shd"] = di("l1_shd", [2 * D, D])
    scr = {}
    if 2 in phases:
        for a in range(8):
            for (wn, K, M, act) in E_SPECS[a]:
                inp[wn] = di(wn, [K, M])
                if act is not None:
                    inp["b_" + wn] = di("b_" + wn, [128, M // 128])
        inp["gate_w1"] = di("gate_w1", [D, D])
        inp["gate_w2"] = di("gate_w2", [D, D // 2])
        inp["gate_w3p"] = di("gate_w3p", [D // 2, 8])
        inp["aff_wp"] = di("aff_wp", [D, 8])
        inp["gate_b1"] = di("gate_b1", [128, 8])
        inp["ln_scale"] = di("ln_scale", [128, 8])
        inp["ln_offset"] = di("ln_offset", [128, 8])
        inp["gate_b2"] = di("gate_b2", [128, 4])
        inp["gate_b3p"] = di("gate_b3p", [8, 1])
        inp["aff_bp"] = di("aff_bp", [8, 1])
        inp["b2stack"] = di("b2stack", [8, D])
        inp["usage"] = di("usage", [1, 8])
        inp["noise_p"] = di("noise_p", [8, T_CORE])
        inp["gnan"] = di("gnan", [128, 8])
        scr["xc"] = nc.dram_tensor("xc_scr", [D, T_CORE], BF16, kind="Internal").ap()
        scr["xs"] = nc.dram_tensor("xs_scr", [D, T_CORE], BF16, kind="Internal").ap()
        scr["r"] = nc.dram_tensor("r_scr", [1, T_CORE], F32, kind="Internal").ap()
        scr["s3"] = nc.dram_tensor("s3_scr", [1, T_CORE], F32, kind="Internal").ap()
        scr["lg"] = nc.dram_tensor("lg_scr", [8, T_CORE], F32, kind="Internal").ap()
        cc_in_t = nc.dram_tensor("cc_in", [16, 256], F32)
        cc_out_t = nc.dram_tensor("cc_out", [128, 256], F32)
        scr["cc_in"] = cc_in_t.ap()
        scr["cc_out"] = cc_out_t.ap()
    if 3 in phases or 4 in phases:
        scr["qf"] = nc.dram_tensor("qf_scr", [D, T_CORE], BF16, kind="Internal").ap()
        scr["hid"] = nc.dram_tensor("hid_scr", [D, T_CORE], BF16, kind="Internal").ap()
        scr["xf"] = nc.dram_tensor("xf_scr", [D, T_CORE], BF16, kind="Internal").ap()

    sems = {}
    for s in ("LD", "WT", "PE", "ACT", "DVE", "ST", "CC", "PL"):
        sems[s] = es.enter_context(nc.semaphore(s))

    b = B(nc, sems)

    sb = lambda nm, shape, dt=F32: es.enter_context(nc.sbuf_tensor(nm, list(shape), dt))

    class TV:
        def __init__(self, t, p0, npart, off, ncols):
            self.t, self.p0, self.np, self.off, self.nc = t, p0, npart, off, ncols

        def _sl(self, s, base, n):
            if isinstance(s, slice):
                a = 0 if s.start is None else s.start
                bnd = n if s.stop is None else s.stop
                assert 0 <= a <= bnd <= n, (a, bnd, n)
                return base + a, base + bnd
            raise TypeError(s)

        def __getitem__(self, idx):
            if idx == slice(None):
                return self.t[self.p0:self.p0 + self.np, self.off:self.off + self.nc]
            pa, pb = self._sl(idx[0], self.p0, self.np)
            ca, cb = self._sl(idx[1], self.off, self.nc)
            return self.t[pa:pb, ca:cb]

    POOL16_COLS = 74112
    POOL32_COLS = 14336
    pool16 = sb("pool16", [128, POOL16_COLS], BF16)
    pool32 = sb("pool32", [128, POOL32_COLS])
    _off16 = [0]
    _off32 = [0]

    def carve16(ncols, npart=128, p0=0):
        off = _off16[0]; _off16[0] += ncols
        assert _off16[0] <= POOL16_COLS, ("pool16 overflow", _off16[0])
        return TV(pool16, p0, npart, off, ncols)

    def carve32(ncols, npart=128, p0=0):
        off = _off32[0]; _off32[0] += ncols
        assert _off32[0] <= POOL32_COLS, ("pool32 overflow", _off32[0])
        return TV(pool32, p0, npart, off, ncols)

    from contextlib import contextmanager

    @contextmanager
    def scope():
        m16, m32 = _off16[0], _off32[0]
        yield
        _off16[0], _off32[0] = m16, m32

    S = {}
    S["x"] = carve32(NCH * NT)
    S["o2"] = carve32(NCH * NT)
    S["t32a"] = carve32(NT)
    S["bf_a"] = carve16(NCH * NT)
    S["bf_b"] = carve16(NCH * NT)
    S["bf_c"] = carve16(NCH * NT)

    R = {}
    for nm in ("mu", "r", "s2", "tmp1", "tmp2", "imp", "s3"):
        R[nm] = carve32(NT, npart=1)
    R["r_bf"] = carve16(NT, npart=1)
    R["mu_bf"] = carve16(NT, npart=1)

    C = {}
    C["ones128_bf"] = sb("ones128_bf", [128, 1], BF16)
    C["ones1x128_bf"] = sb("ones1x128_bf", [1, 128], BF16)
    C["ones1x128_f"] = sb("ones1x128_f", [1, 128])
    C["ones128_f"] = sb("ones128_f", [128, 1])
    C["negmax"] = sb("negmax", [128, 1])
    C["z"] = sb("z", [128, 1])
    C["rz"] = sb("rz", [128, 1])
    C["eps"] = sb("c_eps", [1, 1])

    W = {}
    W["ident"] = carve16(128)
    W["identf"] = carve32(128)

    ps = {}
    for nm in ("A", "B", "C", "D", "E", "F", "G"):
        ps[nm] = es.enter_context(nc.psum_tensor("ps" + nm, [128, NT], F32))
    ps["T"] = es.enter_context(nc.psum_tensor("psT", [128, 1024], BF16))


    def fm_slice(ap_, k, c0, n=NT):
        return bass.AP(ap_.tensor, k * 128 * T_CORE + c0, [[T_CORE, 128], [1, n]])

    def load_consts():
        b.op("vector", lambda q: q.memset(C["ones128_bf"][:], 1.0))
        b.op("vector", lambda q: q.memset(C["ones1x128_bf"][:], 1.0))
        b.op("vector", lambda q: q.memset(C["ones1x128_f"][:], 1.0))
        b.op("vector", lambda q: q.memset(C["ones128_f"][:], 1.0))
        b.opf("vector", lambda q: q.memset(C["eps"][:], 1e-5), "DVE")
        b.op("gpsimd", lambda q: q.dma_start(out=W["ident"][:], in_=inp["ident"]), "WT", 16)
        b.op("sync", lambda q: q.dma_start(out=W["identf"][:], in_=inp["ident"]), "LD", 16)

    def load_attn_weights(wq_name, wo_name):
        for k in range(NCH):
            src = bass.AP(inp[wq_name].tensor, k * 128 * (3 * D),
                          [[3 * D, 128], [1, 3 * D]])
            b.op("gpsimd", lambda q, s=src, k=k:
                 q.dma_start(out=W["qkv"][:, k * 3 * D:(k + 1) * 3 * D], in_=s), "WT", 16)
        for k in range(NCH):
            src = bass.AP(inp[wo_name].tensor, k * 128 * D, [[D, 128], [1, D]])
            b.op("gpsimd", lambda q, s=src, k=k:
                 q.dma_start(out=W["out"][:, k * D:(k + 1) * D], in_=s), "WT", 16)

    def stats_rows(bf_val, bf_sq, tt):
        for k in range(NCH):
            b.op("tensor", lambda q, k=k: q.matmul(
                ps["F"][0:1, :], C["ones128_bf"][:], bf_val[:, k * NT:(k + 1) * NT],
                start=(k == 0), stop=(k == NCH - 1)))
        for k in range(NCH):
            b.op("tensor", lambda q, k=k: q.matmul(
                ps["G"][0:1, :], C["ones128_bf"][:], bf_sq[:, k * NT:(k + 1) * NT],
                start=(k == 0), stop=(k == NCH - 1)),
                "PE" if k == NCH - 1 else None)
        b.wait("vector", "PE")
        mu = R["mu"][:]
        e2 = R["tmp1"][:]
        musq = R["tmp2"][:]
        var = R["s2"][:]
        b.op("vector", lambda q, mu=mu: q.tensor_scalar(mu, ps["F"][0:1, :], 1.0 / D,
                                                        None, ALU.mult))
        b.op("vector", lambda q, e2=e2: q.tensor_scalar(e2, ps["G"][0:1, :], 1.0 / D,
                                                        None, ALU.mult))
        b.fence("vector")
        b.op("vector", lambda q, musq=musq, mu=mu: q.tensor_tensor(musq, mu, mu, ALU.mult))
        b.opf("vector", lambda q, var=var, e2=e2, musq=musq:
              q.tensor_tensor(var, e2, musq, ALU.subtract), "DVE")
        b.wait("scalar", "DVE")
        sd = R["tmp1"][:]
        b.opf("scalar", lambda q, sd=sd, var=var:
              q.activation(sd, var, AF.Sqrt, bias=C["eps"][:]), "ACT")
        b.wait("vector", "ACT")
        rr = R["r"][:]
        b.op("vector", lambda q, rr=rr, sd=sd: q.reciprocal(rr, sd))
        b.fence("vector")
        b.op("vector", lambda q, rr=rr: q.tensor_copy(R["r_bf"][:], rr))
        b.opf("vector", lambda q, mu=mu:
              q.tensor_copy(R["mu_bf"][:], mu), "DVE")

    def bcast_row(row_bf_slice, bank):
        b.op("tensor", lambda q, r=row_bf_slice, p=ps[bank]:
             q.matmul(p[:], C["ones1x128_bf"][:], r, start=True, stop=True), "PE")

    def load_x_tile(tt, xb_ap):
        c0 = tt * NT
        b.wait("sync", "ST")
        for k in range(NCH):
            b.op("sync", lambda q, s=fm_slice(xb_ap, k, c0), k=k:
                 q.dma_start(out=S["x"][:, k * NT:(k + 1) * NT], in_=s), "LD", 16)

    def ln_of_x_tile(tt, out_bf):
        c0 = tt * NT
        b.wait("scalar", "LD")
        b.wait("vector", "LD")
        for k in range(NCH):
            sl = slice(k * NT, (k + 1) * NT)
            b.op("scalar", lambda q, sl=sl: q.activation(
                S["bf_b"][:, sl], S["x"][:, sl], AF.Square), "ACT")
            b.op("vector", lambda q, sl=sl:
                 q.tensor_copy(S["bf_a"][:, sl], S["x"][:, sl]), "DVE")
        b.wait("tensor", "ACT")
        b.wait("tensor", "DVE")
        stats_rows(S["bf_a"], S["bf_b"], tt)
        b.wait("tensor", "DVE")
        bcast_row(R["mu_bf"][:], "C")
        bcast_row(R["r_bf"][:], "D")
        b.wait("vector", "PE")
        for k in range(NCH):
            sl = slice(k * NT, (k + 1) * NT)
            b.op("vector", lambda q, sl=sl: q.tensor_tensor(
                S["t32a"][:], S["x"][:, sl], ps["C"][:], ALU.subtract))
            b.op("vector", lambda q, sl=sl, ob=out_bf: q.tensor_tensor(
                ob[:, sl], S["t32a"][:], ps["D"][:], ALU.mult), "DVE")

    def post_ln_residual(tt, xb_in_ap, xb_out_ap, branch32):
        c0 = tt * NT
        for k in range(NCH):
            sl = slice(k * NT, (k + 1) * NT)
            b.op("scalar", lambda q, sl=sl, br=branch32: q.activation(
                S["bf_b"][:, sl], br[:, sl], AF.Square), "ACT")
            b.op("vector", lambda q, sl=sl, br=branch32:
                 q.tensor_copy(S["bf_a"][:, sl], br[:, sl]), "DVE")
        load_x_tile(tt, xb_in_ap)
        b.wait("tensor", "ACT")
        b.wait("tensor", "DVE")
        stats_rows(S["bf_a"], S["bf_b"], tt)
        b.wait("tensor", "DVE")
        bcast_row(R["mu_bf"][:], "C")
        bcast_row(R["r_bf"][:], "D")
        b.wait("vector", "PE")
        b.wait("vector", "LD")
        for k in range(NCH):
            sl = slice(k * NT, (k + 1) * NT)
            b.op("vector", lambda q, sl=sl, br=branch32: q.tensor_tensor(
                S["t32a"][:], br[:, sl], ps["C"][:], ALU.subtract))
            b.op("vector", lambda q, sl=sl: q.tensor_tensor(
                S["t32a"][:], S["t32a"][:], ps["D"][:], ALU.mult))
            b.op("vector", lambda q, sl=sl: q.tensor_tensor(
                S["x"][:, sl], S["x"][:, sl], S["t32a"][:], ALU.add), "DVE")
        b.wait("sync", "DVE")
        for k in range(NCH):
            b.op("sync", lambda q, d=fm_slice(xb_out_ap, k, c0), k=k:
                 q.dma_start(out=d, in_=S["x"][:, k * NT:(k + 1) * NT]), "ST", 16)

    def mm_sweep_evac(nm_chunks, lhsT_fn, rhs_fn, evac_fn, nk=NCH):
        marks = []
        for m in range(nm_chunks):
            bank = "A" if m % 2 == 0 else "B"
            if m >= 2:
                b.wait("tensor", "ACT", marks[m - 2])
            for k in range(nk):
                b.op("tensor", lambda q, m=m, k=k, bank=bank: q.matmul(
                    ps[bank][:], lhsT_fn(m, k), rhs_fn(m, k),
                    start=(k == 0), stop=(k == nk - 1)),
                    "PE" if k == nk - 1 else None)
            b.wait("scalar", "PE")
            evac_fn(m, bank)
            marks.append(b.val("ACT"))


    DBG_TT = 0

    def dump(nm, tv):
        if not dbg or nm not in dbgo:
            return
        b.wait("gpsimd", "DVE")
        b.wait("gpsimd", "ACT")
        b.op("gpsimd", lambda q, nm=nm, tv=tv: q.dma_start(out=dbgo[nm], in_=tv[:]),
             "ST", 16)
        b.wait("vector", "ST")
        b.wait("scalar", "ST")

    def attn_phase(xb_in, xb_out, wq_name, wo_name, kind):
        S["qkv"] = carve16(3 * NCH * NT)
        S["qsw"] = carve16(NCH * NT)
        S["vT"] = carve16(4 * NCH * 128)
        S["attb"] = carve16(BLK)
        S["attT"] = carve16(BLK)
        W["qkv"] = carve16(NCH * 3 * D)
        W["out"] = carve16(NCH * D)
        W["rc1"] = carve16(T_BATCH)
        W["rc2"] = carve16(T_BATCH)
        W["mask0"] = carve32(BLK)
        W["mask1"] = carve32(BLK)
        b.op("gpsimd", lambda q: q.dma_start(out=W["rc1"][:], in_=inp["rc1"]), "WT", 16)
        b.op("gpsimd", lambda q: q.dma_start(out=W["rc2"][:], in_=inp["rc2"]), "WT", 16)
        b.op("sync", lambda q: q.dma_start(out=W["mask0"][:], in_=inp["mask0"]), "LD", 16)
        b.op("sync", lambda q: q.dma_start(out=W["mask1"][:], in_=inp["mask1"]), "LD", 16)
        load_attn_weights(wq_name, wo_name)
        for tt in range(NTT):
            p0 = (tt % (T_BATCH // NT)) * NT
            load_x_tile(tt, xb_in)
            ln_of_x_tile(tt, S["bf_c"])
            if tt == DBG_TT:
                dump("d_xln", S["bf_c"])
            b.wait("tensor", "WT")
            b.wait("tensor", "DVE")
            mm_sweep_evac(
                3 * NCH,
                lambda m, k: W["qkv"][:, k * 3 * D + m * 128: k * 3 * D + (m + 1) * 128],
                lambda m, k: S["bf_c"][:, k * NT:(k + 1) * NT],
                lambda m, bank: b.op("scalar", lambda q, m=m, bank=bank: q.activation(
                    S["qkv"][:, m * NT:(m + 1) * NT], ps[bank][:], AF.Copy), "ACT"))
            b.wait("gpsimd", "ACT")
            for half in range(2):
                hoff = half * NCH * NT
                if half == 1:
                    b.wait("gpsimd", "DVE")
                for blk_i in range(4):
                    srcb = blk_i ^ 1
                    b.op("gpsimd", lambda q, d=blk_i, s=srcb, hoff=hoff: q.dma_start(
                        out=S["qsw"][d * 32:(d + 1) * 32, :],
                        in_=S["qkv"][s * 32:(s + 1) * 32, hoff:hoff + NCH * NT]),
                        "WT", 16)
                b.wait("vector", "WT")
                for m in range(NCH):
                    sl = slice(hoff + m * NT, hoff + (m + 1) * NT)
                    sw = slice(m * NT, (m + 1) * NT)
                    csl = slice(p0, p0 + NT)
                    b.op("vector", lambda q, sl=sl, csl=csl: q.tensor_tensor(
                        S["bf_a"][:, 0:NT], S["qkv"][:, sl], W["rc1"][:, csl], ALU.mult))
                    b.op("vector", lambda q, sw=sw, csl=csl: q.tensor_tensor(
                        S["bf_b"][:, 0:NT], S["qsw"][:, sw], W["rc2"][:, csl], ALU.mult))
                    b.op("vector", lambda q, sl=sl: q.tensor_tensor(
                        S["qkv"][:, sl], S["bf_a"][:, 0:NT], S["bf_b"][:, 0:NT],
                        ALU.add), "DVE")
            if tt == DBG_TT:
                b.wait("gpsimd", "DVE")
                dump("d_qkv", S["qkv"])
            if kind == "l0":
                l0_attention_core(tt)
            else:
                l1_attention_core(tt)
            if tt == DBG_TT:
                dump("d_att", S["bf_c"])
                dump("d_vT", S["vT"])
            mm_sweep_evac(
                NCH,
                lambda m, k: W["out"][:, k * D + m * 128: k * D + (m + 1) * 128],
                lambda m, k: S["bf_c"][:, k * NT:(k + 1) * NT],
                lambda m, bank: b.op("scalar", lambda q, m=m, bank=bank: q.activation(
                    S["o2"][:, m * NT:(m + 1) * NT], ps[bank][:], AF.Copy), "ACT"))
            b.wait("vector", "ACT")
            if tt == DBG_TT:
                b.wait("gpsimd", "ACT")
                dump("d_o2", S["o2"])
            post_ln_residual(tt, xb_in, xb_out, S["o2"])

    def l0_attention_core(tt):
        b.wait("tensor", "DVE")
        for ch in range(NCH):
            for st in range(4):
                slot = st % 2
                b.wait("tensor", "DVE")
                b.op("tensor", lambda q, ch=ch, st=st, slot=slot: q.transpose(
                    ps["T"][:, slot * 128:(slot + 1) * 128],
                    S["qkv"][:, (2 * NCH + ch) * NT + st * 128:
                             (2 * NCH + ch) * NT + (st + 1) * 128],
                    W["ident"][:]), "PE")
                b.wait("vector", "PE")
                dst = S["vT"][:, (st * NCH + ch) * 128:(st * NCH + ch + 1) * 128]
                b.op("vector", lambda q, d=dst, slot=slot: q.tensor_copy(
                    d, ps["T"][:, slot * 128:(slot + 1) * 128]), "DVE")
        scale = float(HD ** -0.5)
        for ch in range(NCH):
            for sub in range(2):
                pbase = 64 * sub
                for blk_i in range(2):
                    for qt in range(2):
                        qcol = ch * NT + blk_i * BLK + qt * 128
                        kcol = (NCH + ch) * NT + blk_i * BLK
                        sbank = "A" if (blk_i * 2 + qt) % 2 == 0 else "B"
                        b.wait("tensor", "DVE")
                        b.op("tensor", lambda q, qcol=qcol, kcol=kcol, sbank=sbank,
                             pbase=pbase: q.matmul(
                                 ps[sbank][:, 0:BLK],
                                 S["qkv"][pbase:pbase + 64, qcol:qcol + 128],
                                 S["qkv"][pbase:pbase + 64, kcol:kcol + BLK],
                                 start=True, stop=True), "PE")
                        b.wait("vector", "PE")
                        mask = W["mask0"] if qt == 0 else W["mask1"]
                        b.op("vector", lambda q, sbank=sbank, mask=mask:
                             q.tensor_tensor(ps[sbank][:, 0:BLK], ps[sbank][:, 0:BLK],
                                             mask[:], ALU.add))
                        b.op("vector", lambda q, sbank=sbank: q.tensor_reduce(
                            C["negmax"][:], ps[sbank][:, 0:BLK], AX.X, ALU.max,
                            negate=True))
                        b.fence("vector")
                        b.opf("vector", lambda q: q.tensor_scalar(
                            C["negmax"][:], C["negmax"][:], scale, None, ALU.mult),
                            "DVE")
                        b.wait("scalar", "DVE")
                        b.opf("scalar", lambda q, sbank=sbank: q.activation(
                            S["attb"][:], ps[sbank][:, 0:BLK], AF.Exp,
                            bias=C["negmax"][:], scale=scale,
                            accum_out=C["z"][:]), "ACT")
                        b.wait("vector", "ACT")
                        b.op("vector", lambda q: q.reciprocal(C["rz"][:], C["z"][:]))
                        b.fence("vector")
                        b.opf("vector", lambda q: q.tensor_scalar(
                            S["attb"][:], S["attb"][:], C["rz"][:], None, ALU.mult),
                            "DVE")
                        b.wait("tensor", "DVE")
                        for jt in range(2):
                            b.op("tensor", lambda q, jt=jt: q.transpose(
                                ps["T"][:, jt * 128:(jt + 1) * 128],
                                S["attb"][:, jt * 128:(jt + 1) * 128],
                                W["ident"][:]), "PE" if jt == 1 else None)
                        b.wait("vector", "PE")
                        for jt in range(2):
                            b.op("vector", lambda q, jt=jt: q.tensor_copy(
                                S["attT"][:, jt * 128:(jt + 1) * 128],
                                ps["T"][:, jt * 128:(jt + 1) * 128]),
                                "DVE" if jt == 1 else None)
                        b.wait("tensor", "DVE")
                        ocol = blk_i * BLK + qt * 128
                        for jt in range(2):
                            vcol = ((blk_i * 2 + jt) * NCH + ch) * 128 + pbase
                            b.op("tensor", lambda q, vcol=vcol, jt=jt, ocol=ocol,
                                 pbase=pbase: q.matmul(
                                     ps["E"][pbase:pbase + 64, ocol:ocol + 128],
                                     S["vT"][:, vcol:vcol + 64],
                                     S["attT"][:, jt * 128:(jt + 1) * 128],
                                     start=(jt == 0), stop=(jt == 1)),
                                 "PE" if jt == 1 else None)
            b.wait("scalar", "PE")
            b.op("scalar", lambda q, ch=ch: q.activation(
                S["bf_c"][:, ch * NT:(ch + 1) * NT], ps["E"][:], AF.Copy), "ACT")
            b.wait("tensor", "ACT")


    def l1_attn_phase(xb_in, xb_out):
        S["qkv"] = carve16(3 * NCH * NT)
        S["vT"] = carve16(4 * NCH * 128)
        S["kT"] = carve16(4 * NCH * 128)
        S["qsw"] = S["kT"]
        S["kvbf"] = carve16(8 * 72)
        S["ksumbf"] = carve16(8)
        W["qkv"] = carve16(NCH * 3 * D)
        W["out"] = carve16(NCH * D)
        W["rc1"] = carve16(T_BATCH)
        W["rc2"] = carve16(T_BATCH)
        KV = carve32(8 * 72)
        KS = carve32(8)
        DN = carve32(NT, npart=1)
        b.op("gpsimd", lambda q: q.dma_start(out=W["rc1"][:], in_=inp["rc1"]), "WT", 16)
        b.op("gpsimd", lambda q: q.dma_start(out=W["rc2"][:], in_=inp["rc2"]), "WT", 16)
        load_attn_weights("l1_wqkvT", "l1_woutT")
        scale = float(HD ** -0.5)
        TPB = T_BATCH // NT
        for bb in range(B_PER_CORE):
            b.wait("vector", "PE")
            b.op("vector", lambda q: q.memset(KV[:], 0.0))
            b.opf("vector", lambda q: q.memset(KS[:], 0.0), "DVE")
            for ti in range(TPB):
                tt = bb * TPB + ti
                p0 = ti * NT
                load_x_tile(tt, xb_in)
                ln_of_x_tile(tt, S["bf_c"])
                b.wait("tensor", "WT")
                b.wait("tensor", "DVE")
                mm_sweep_evac(
                    3 * NCH,
                    lambda m, k: W["qkv"][:, k * 3 * D + m * 128: k * 3 * D + (m + 1) * 128],
                    lambda m, k: S["bf_c"][:, k * NT:(k + 1) * NT],
                    lambda m, bank: b.op("scalar", lambda q, m=m, bank=bank: q.activation(
                        S["qkv"][:, m * NT:(m + 1) * NT], ps[bank][:], AF.Copy), "ACT"))
                b.wait("gpsimd", "ACT")
                for half in range(2):
                    hoff = half * NCH * NT
                    if half == 1:
                        b.wait("gpsimd", "DVE")
                    for blk_i in range(4):
                        srcb = blk_i ^ 1
                        b.op("gpsimd", lambda q, d=blk_i, sgm=srcb, hoff=hoff: q.dma_start(
                            out=S["qsw"][d * 32:(d + 1) * 32, :],
                            in_=S["qkv"][sgm * 32:(sgm + 1) * 32, hoff:hoff + NCH * NT]),
                            "WT", 16)
                    b.wait("vector", "WT")
                    for m in range(NCH):
                        sl = slice(hoff + m * NT, hoff + (m + 1) * NT)
                        sw = slice(m * NT, (m + 1) * NT)
                        csl = slice(p0, p0 + NT)
                        b.op("vector", lambda q, sl=sl, csl=csl: q.tensor_tensor(
                            S["bf_a"][:, 0:NT], S["qkv"][:, sl], W["rc1"][:, csl],
                            ALU.mult))
                        b.op("vector", lambda q, sw=sw, csl=csl: q.tensor_tensor(
                            S["bf_b"][:, 0:NT], S["qsw"][:, sw], W["rc2"][:, csl],
                            ALU.mult))
                        b.op("vector", lambda q, sl=sl: q.tensor_tensor(
                            S["qkv"][:, sl], S["bf_a"][:, 0:NT], S["bf_b"][:, 0:NT],
                            ALU.add), "DVE")
                for half, sc in ((0, scale), (1, 1.0)):
                    ho = half * NCH * NT
                    reg = slice(ho, ho + NCH * NT)
                    b.wait("vector", "ACT")
                    b.op("vector", lambda q, reg=reg, sc=sc: q.tensor_scalar(
                        S["bf_a"][:], S["qkv"][:, reg], sc, 0.0, ALU.mult, ALU.min))
                    b.opf("vector", lambda q, reg=reg, sc=sc: q.tensor_scalar(
                        S["bf_b"][:], S["qkv"][:, reg], sc, 0.0, ALU.mult, ALU.max),
                        "DVE")
                    b.wait("scalar", "DVE")
                    b.opf("scalar", lambda q: q.activation(
                        S["bf_c"][:], S["bf_a"][:], AF.Exp), "ACT")
                    b.wait("vector", "ACT")
                    b.opf("vector", lambda q, reg=reg: q.tensor_tensor(
                        S["qkv"][:, reg], S["bf_c"][:], S["bf_b"][:], ALU.add), "DVE")
                b.wait("sync", "DVE")
                c0 = tt * NT
                for k in range(NCH):
                    dst = bass.AP(scr["qf"].tensor, k * 128 * T_CORE + c0,
                                  [[T_CORE, 128], [1, NT]])
                    b.op("sync", lambda q, d=dst, k=k:
                         q.dma_start(out=d, in_=S["qkv"][:, k * NT:(k + 1) * NT]),
                         "ST", 16)
                for k in range(NCH):
                    sl = slice(NCH * NT + k * NT, NCH * NT + (k + 1) * NT)
                    b.opf("vector", lambda q, sl=sl: q.tensor_scalar(
                        S["t32a"][:], S["qkv"][:, sl], 1.0, 0.0, ALU.mult, ALU.add,
                        accum_out=C["negmax"][:]), "DVE")
                    b.op("vector", lambda q, k=k: q.tensor_tensor(
                        KS[:, k:k + 1], KS[:, k:k + 1], C["negmax"][:], ALU.add),
                        "DVE")
                b.wait("tensor", "DVE")
                for src_half, dst_tv in ((1, S["kT"]), (2, S["vT"])):
                    for ch in range(NCH):
                        for st in range(4):
                            slot = st % 2
                            b.wait("tensor", "DVE")
                            b.op("tensor", lambda q, ch=ch, st=st, slot=slot,
                                 src_half=src_half: q.transpose(
                                     ps["T"][:, slot * 128:(slot + 1) * 128],
                                     S["qkv"][:, (src_half * NCH + ch) * NT + st * 128:
                                              (src_half * NCH + ch) * NT + (st + 1) * 128],
                                     W["ident"][:]), "PE")
                            b.wait("vector", "PE")
                            dst = dst_tv[:, (st * NCH + ch) * 128:(st * NCH + ch + 1) * 128]
                            b.op("vector", lambda q, d=dst, slot=slot: q.tensor_copy(
                                d, ps["T"][:, slot * 128:(slot + 1) * 128]), "DVE")
                for ch in range(NCH):
                    b.wait("tensor", "DVE")
                    for sub in range(2):
                        pbase = 64 * sub
                        for st in range(4):
                            col = (st * NCH + ch) * 128 + pbase
                            b.op("tensor", lambda q, col=col, st=st, pbase=pbase:
                                 q.matmul(ps["E"][pbase:pbase + 64, 0:64],
                                          S["kT"][:, col:col + 64],
                                          S["vT"][:, col:col + 64],
                                          start=(st == 0), stop=(st == 3)),
                                 "PE" if st == 3 else None)
                    b.wait("vector", "PE")
                    b.op("vector", lambda q, ch=ch: q.tensor_tensor(
                        KV[:, ch * 72:ch * 72 + 64], KV[:, ch * 72:ch * 72 + 64],
                        ps["E"][:, 0:64], ALU.add), "DVE")
                    b.wait("tensor", "DVE")
            b.op("vector", lambda q: q.tensor_copy(S["kvbf"][:], KV[:]))
            b.opf("vector", lambda q: q.tensor_copy(S["ksumbf"][:], KS[:]), "DVE")
            for ti in range(TPB):
                tt = bb * TPB + ti
                c0 = tt * NT
                b.wait("sync", "PE")
                b.wait("sync", "DVE")
                b.wait("sync", "ST")
                for k in range(NCH):
                    src = bass.AP(scr["qf"].tensor, k * 128 * T_CORE + c0,
                                  [[T_CORE, 128], [1, NT]])
                    b.op("sync", lambda q, sgm=src, k=k:
                         q.dma_start(out=S["qkv"][:, k * NT:(k + 1) * NT], in_=sgm),
                         "LD", 16)
                b.wait("tensor", "LD")
                b.wait("tensor", "DVE")
                b.wait("tensor", "ACT")
                for ch in range(NCH):
                    for sub in range(2):
                        pbase = 64 * sub
                        b.op("tensor", lambda q, ch=ch, pbase=pbase: q.matmul(
                            ps["F"][0:1, :],
                            S["ksumbf"][pbase:pbase + 64, ch:ch + 1],
                            S["qkv"][pbase:pbase + 64, ch * NT:(ch + 1) * NT],
                            start=True, stop=True), "PE")
                        b.wait("vector", "PE")
                        b.op("vector", lambda q: q.tensor_scalar(
                            DN[:], ps["F"][0:1, :], 1e-6, None, ALU.max))
                        b.fence("vector")
                        b.opf("vector", lambda q: q.reciprocal(R["tmp1"][:], DN[:]),
                              "DVE")
                        b.wait("tensor", "DVE")
                        b.op("tensor", lambda q, pbase=pbase: q.matmul(
                            ps["D"][pbase:pbase + 64, :],
                            C["ones1x128_f"][0:1, 0:64], R["tmp1"][:],
                            start=True, stop=True))
                        b.op("tensor", lambda q, ch=ch, pbase=pbase: q.matmul(
                            ps["A"][pbase:pbase + 64, :],
                            S["kvbf"][pbase:pbase + 64, ch * 72:ch * 72 + 64],
                            S["qkv"][pbase:pbase + 64, ch * NT:(ch + 1) * NT],
                            start=True, stop=True), "PE")
                    b.wait("scalar", "PE")
                    b.opf("scalar", lambda q: q.activation(
                        S["bf_a"][:, 0:NT], ps["A"][:], AF.Copy), "ACT")
                    b.wait("vector", "ACT")
                    b.op("vector", lambda q, ch=ch: q.tensor_tensor(
                        S["bf_c"][:, ch * NT:(ch + 1) * NT], S["bf_a"][:, 0:NT],
                        ps["D"][:], ALU.mult), "DVE")
                    b.wait("tensor", "DVE")
                b.wait("tensor", "DVE")
                mm_sweep_evac(
                    NCH,
                    lambda m, k: W["out"][:, k * D + m * 128: k * D + (m + 1) * 128],
                    lambda m, k: S["bf_c"][:, k * NT:(k + 1) * NT],
                    lambda m, bank: b.op("scalar", lambda q, m=m, bank=bank:
                                         q.activation(S["o2"][:, m * NT:(m + 1) * NT],
                                                      ps[bank][:], AF.Copy), "ACT"))
                b.wait("vector", "ACT")
                post_ln_residual(tt, xb_in, xb_out, S["o2"])


    def ffn1_phase(xb_in, xb_out):
        S["xf"] = carve16(NCH * NT)
        S["h"] = carve16(32 * NT)
        S["wb0"] = carve16(8192)
        S["wb1"] = carve16(8192)
        S["gall"] = carve16(T_CORE, npart=8)
        S["grow"] = carve16(NT, npart=1)
        S["acc"] = carve16(NCH * NT)
        S["gfin"] = carve16(8)
        W["router"] = carve16(NCH * 8)
        V = {}
        V["lgt"] = carve32(8)
        V["lg"] = carve32(NT, npart=8)
        B4 = {}
        B4["lnw"] = carve32(8)
        B4["lnb"] = carve32(8)
        T4 = {}
        T4["max1"] = sb("c4_max1", [128, 1])
        T4["score"] = sb("c4_score", [128, 1])
        for k in range(NCH):
            src = bass.AP(inp["l1_router"].tensor, k * 128 * 8, [[8, 128], [1, 8]])
            b.op("gpsimd", lambda q, sgm=src, k=k:
                 q.dma_start(out=W["router"][:, k * 8:(k + 1) * 8], in_=sgm), "WT", 16)
        b.op("sync", lambda q: q.dma_start(out=B4["lnw"][:], in_=inp["l1_ln_w"]), "LD", 16)
        b.op("sync", lambda q: q.dma_start(out=B4["lnb"][:], in_=inp["l1_ln_b"]), "LD", 16)

        for tt in range(NTT):
            c0 = tt * NT
            load_x_tile(tt, xb_in)
            ln_of_x_tile(tt, S["bf_c"])
            b.wait("sync", "DVE")
            for k in range(NCH):
                dst = bass.AP(scr["hid"].tensor, k * 128 * T_CORE + c0,
                              [[T_CORE, 128], [1, NT]])
                b.op("sync", lambda q, d=dst, k=k:
                     q.dma_start(out=d, in_=S["bf_c"][:, k * NT:(k + 1) * NT]), "ST", 16)
            b.wait("scalar", "DVE")
            for k in range(NCH):
                sl = slice(k * NT, (k + 1) * NT)
                b.op("scalar", lambda q, sl=sl: q.activation(
                    S["bf_b"][:, sl], S["bf_c"][:, sl], AF.Square), "ACT")
            b.wait("tensor", "ACT")
            b.wait("tensor", "DVE")
            stats_rows(S["bf_c"], S["bf_b"], tt)
            b.wait("tensor", "DVE")
            bcast_row(R["mu_bf"][:], "C")
            bcast_row(R["r_bf"][:], "D")
            b.wait("vector", "PE")
            for k in range(NCH):
                sl = slice(k * NT, (k + 1) * NT)
                b.op("vector", lambda q, sl=sl: q.tensor_tensor(
                    S["t32a"][:], S["bf_c"][:, sl], ps["C"][:], ALU.subtract))
                b.op("vector", lambda q, sl=sl: q.tensor_tensor(
                    S["t32a"][:], S["t32a"][:], ps["D"][:], ALU.mult))
                b.op("vector", lambda q, sl=sl, k=k: q.tensor_scalar(
                    S["xf"][:, sl], S["t32a"][:], B4["lnw"][:, k:k + 1],
                    B4["lnb"][:, k:k + 1], ALU.mult, ALU.add),
                    "DVE" if k == NCH - 1 else None)
            b.wait("sync", "DVE")
            for k in range(NCH):
                dst = bass.AP(scr["xf"].tensor, k * 128 * T_CORE + c0,
                              [[T_CORE, 128], [1, NT]])
                b.op("sync", lambda q, d=dst, k=k:
                     q.dma_start(out=d, in_=S["xf"][:, k * NT:(k + 1) * NT]), "ST", 16)
            if dbg and tt == 0:
                b.wait("gpsimd", "DVE")
                b.op("gpsimd", lambda q: q.dma_start(out=dbgo["d4_xf"], in_=S["xf"][:]), "ST", 16)
                b.op("gpsimd", lambda q: q.dma_start(out=dbgo["d4_hid"], in_=S["bf_c"][:]), "ST", 16)
                b.wait("vector", "ST")
                b.wait("scalar", "ST")
            b.wait("tensor", "WT")
            b.wait("tensor", "DVE")
            b.wait("tensor", "ACT")
            for k in range(NCH):
                b.op("tensor", lambda q, k=k: q.matmul(
                    ps["E"][0:8, :], W["router"][:, k * 8:(k + 1) * 8],
                    S["xf"][:, k * NT:(k + 1) * NT],
                    start=(k == 0), stop=(k == NCH - 1)), "PE" if k == NCH - 1 else None)
            b.wait("vector", "PE")
            b.opf("vector", lambda q: q.tensor_copy(V["lg"][:], ps["E"][0:8, :]), "DVE")
            for sub in range(4):
                scol = sub * 128
                b.wait("tensor", "DVE")
                b.op("tensor", lambda q, scol=scol: q.transpose(
                    ps["E"][:, 0:128], V["lg"][0:8, scol:scol + 128],
                    W["identf"][0:8, :]), "PE")
                b.wait("vector", "PE")
                b.op("vector", lambda q: q.tensor_copy(V["lgt"][:], ps["E"][:, 0:8]))
                b.fence("vector")
                b.op("vector", lambda q: q.tensor_reduce(
                    T4["max1"][:], V["lgt"][:], AX.X, ALU.max))
                b.fence("vector")
                b.opf("vector", lambda q: q.tensor_scalar(
                    V["lgt"][:], V["lgt"][:], T4["max1"][:], None, ALU.is_ge), "DVE")
                b.wait("scalar", "DVE")
                b.opf("scalar", lambda q: q.activation(
                    T4["score"][:], T4["max1"][:], AF.Sigmoid), "ACT")
                b.wait("vector", "ACT")
                b.op("vector", lambda q: q.tensor_scalar(
                    V["lgt"][:], V["lgt"][:], T4["score"][:], None, ALU.mult))
                b.fence("vector")
                b.opf("vector", lambda q: q.tensor_copy(S["gfin"][:], V["lgt"][:]),
                      "DVE")
                b.wait("tensor", "DVE")
                b.op("tensor", lambda q: q.transpose(
                    ps["T"][0:8, 0:128], S["gfin"][:], W["ident"][:]), "PE")
                b.wait("vector", "PE")
                b.opf("vector", lambda q, c0=c0, scol=scol: q.tensor_copy(
                    S["gall"][0:8, c0 + scol:c0 + scol + 128], ps["T"][0:8, 0:128]),
                    "DVE")

        if dbg:
            b.wait("gpsimd", "DVE")
            b.op("gpsimd", lambda q: q.dma_start(out=dbgo["d4_w"], in_=S["gall"][:]), "ST", 16)
            b.wait("vector", "ST")
        SER = _os.environ.get("P4_SERIAL", "0") == "1"

        def stream_mm(base_off, K, M, rhs_tv, evac_fn):
            nk = K // 128
            mg = min(1024, (8192 // nk) // 128 * 128)
            b.wait("tensor", "ACT")
            b.wait("tensor", "DVE")
            b.wait("gpsimd", "PE")
            pe_marks = {}
            evac_marks = []
            for gi in range(M // mg):
                buf = S["wb0"] if gi % 2 == 0 else S["wb1"]
                if SER:
                    b.wait("gpsimd", "PE")
                    b.wait("gpsimd", "ACT")
                    b.wait("gpsimd", "DVE")
                elif gi >= 2:
                    b.wait("gpsimd", "PE", pe_marks[gi - 2])
                src = bass.AP(inp[stream_mm.wname].tensor, base_off + gi * mg,
                              [[M, 128], [128 * M, nk], [1, mg]])
                b.op("gpsimd", lambda q, sgm=src, buf=buf, nk=nk, mg=mg:
                     q.dma_start(out=buf[:, 0:nk * mg], in_=sgm), "WT", 16)
                b.wait("tensor", "WT")
                for mc in range(mg // 128):
                    mi = (gi * mg) // 128 + mc
                    bank = "A" if mi % 2 == 0 else "B"
                    if SER:
                        b.wait("tensor", "ACT")
                        b.wait("tensor", "DVE")
                    elif len(evac_marks) >= 2:
                        b.wait("tensor", evac_marks[-2][0], evac_marks[-2][1])
                    for k in range(nk):
                        b.op("tensor", lambda q, k=k, mc=mc, bank=bank, buf=buf, mg=mg:
                             q.matmul(ps[bank][:],
                                      buf[:, k * mg + mc * 128:k * mg + (mc + 1) * 128],
                                      rhs_tv[:, k * NT:(k + 1) * NT],
                                      start=(k == 0), stop=(k == nk - 1)),
                             "PE" if k == nk - 1 else None)
                    evac_marks.append(evac_fn(mi, bank))
                pe_marks[gi] = b.val("PE")

        for tt in range(NTT):
            c0 = tt * NT
            b.wait("sync", "PE")
            b.wait("sync", "DVE")
            b.wait("sync", "ST")
            for k in range(NCH):
                src = bass.AP(scr["xf"].tensor, k * 128 * T_CORE + c0,
                              [[T_CORE, 128], [1, NT]])
                b.op("sync", lambda q, sgm=src, k=k:
                     q.dma_start(out=S["xf"][:, k * NT:(k + 1) * NT], in_=sgm), "LD", 16)
            b.wait("vector", "LD")
            b.wait("vector", "DVE")
            b.opf("vector", lambda q: q.memset(S["acc"][:], 0.0), "DVE")
            b.wait("tensor", "LD")

            def evac_silu(mi, bank):
                b.wait("scalar", "PE")
                b.op("scalar", lambda q, mi=mi, bank=bank: q.activation(
                    S["h"][:, mi * NT:(mi + 1) * NT], ps[bank][:], AF.Silu), "ACT")
                return ("ACT", b.val("ACT"))

            def evac_hmid(mi, bank):
                b.wait("vector", "PE")
                b.op("vector", lambda q, mi=mi, bank=bank: q.tensor_tensor(
                    S["h"][:, (16 + mi) * NT:(16 + mi + 1) * NT],
                    S["h"][:, mi * NT:(mi + 1) * NT], ps[bank][:], ALU.mult), "DVE")
                return ("DVE", b.val("DVE"))

            def evac_acc(mi, bank):
                b.wait("vector", "PE")
                b.op("vector", lambda q, mi=mi, bank=bank: q.tensor_tensor(
                    S["acc"][:, mi * NT:(mi + 1) * NT],
                    S["acc"][:, mi * NT:(mi + 1) * NT], ps[bank][:], ALU.add), "DVE")
                return ("DVE", b.val("DVE"))

            hmid_tv = TV(pool16, 0, 128, S["h"].off + 16 * NT, 16 * NT)

            for e in range(8):
                b.wait("gpsimd", "PE")
                b.wait("gpsimd", "DVE")
                b.op("gpsimd", lambda q, e=e, c0=c0: q.dma_start(
                    out=S["grow"][:], in_=S["gall"][e:e + 1, c0:c0 + NT]), "WT", 16)
                b.wait("tensor", "WT")
                b.wait("tensor", "DVE")
                b.wait("tensor", "ACT")
                b.op("tensor", lambda q: q.matmul(
                    ps["C"][:], C["ones1x128_bf"][:], S["grow"][:],
                    start=True, stop=True), "PE")
                b.wait("vector", "PE")
                for k in range(NCH):
                    sl = slice(k * NT, (k + 1) * NT)
                    b.op("vector", lambda q, sl=sl: q.tensor_tensor(
                        S["bf_a"][:, sl], S["xf"][:, sl], ps["C"][:], ALU.mult),
                        "DVE" if k == NCH - 1 else None)
                if dbg and tt == 0 and e == 0:
                    b.wait("gpsimd", "DVE")
                    b.op("gpsimd", lambda q: q.dma_start(out=dbgo["d4_xe"], in_=S["bf_a"][:]), "ST", 16)
                    b.wait("vector", "ST")
                stream_mm.wname = "l1_gate"
                stream_mm(e * D * 2 * D, D, 2 * D, S["bf_a"], evac_silu)
                stream_mm.wname = "l1_up"
                stream_mm(e * D * 2 * D, D, 2 * D, S["bf_a"], evac_hmid)
                if dbg and tt == 0 and e == 0:
                    b.wait("gpsimd", "ACT")
                    b.wait("gpsimd", "DVE")
                    b.op("gpsimd", lambda q: q.dma_start(out=dbgo["d4_hs"], in_=S["h"][:, 0:16 * NT]), "ST", 16)
                    b.op("gpsimd", lambda q: q.dma_start(out=dbgo["d4_hm"], in_=S["h"][:, 16 * NT:32 * NT]), "ST", 16)
                    b.wait("vector", "ST")
                    b.wait("scalar", "ST")
                stream_mm.wname = "l1_down"
                stream_mm(e * 2 * D * D, 2 * D, D, hmid_tv, evac_acc)
                if dbg and tt == 0 and e == 0:
                    b.wait("gpsimd", "DVE")
                    b.op("gpsimd", lambda q: q.dma_start(out=dbgo["d4_acc0"], in_=S["acc"][:]), "ST", 16)
                    b.wait("vector", "ST")
            stream_mm.wname = "l1_shg"
            stream_mm(0, D, 2 * D, S["xf"], evac_silu)
            stream_mm.wname = "l1_shu"
            stream_mm(0, D, 2 * D, S["xf"], evac_hmid)
            stream_mm.wname = "l1_shd"
            stream_mm(0, 2 * D, D, hmid_tv, evac_acc)
            if dbg and tt == 0:
                b.wait("gpsimd", "DVE")
                b.op("gpsimd", lambda q: q.dma_start(out=dbgo["d4_acc"], in_=S["acc"][:]), "ST", 16)
                b.wait("vector", "ST")
                b.wait("scalar", "ST")
            b.wait("sync", "ST")
            b.wait("sync", "DVE")
            for k in range(NCH):
                src = bass.AP(scr["hid"].tensor, k * 128 * T_CORE + c0,
                              [[T_CORE, 128], [1, NT]])
                b.op("sync", lambda q, sgm=src, k=k:
                     q.dma_start(out=S["bf_c"][:, k * NT:(k + 1) * NT], in_=sgm),
                     "LD", 16)
            b.wait("vector", "LD")
            b.wait("vector", "DVE")
            for k in range(NCH):
                sl = slice(k * NT, (k + 1) * NT)
                b.op("vector", lambda q, sl=sl: q.tensor_tensor(
                    S["o2"][:, sl], S["bf_c"][:, sl], S["acc"][:, sl], ALU.add),
                    "DVE" if k == NCH - 1 else None)
            b.wait("scalar", "DVE")
            post_ln_residual(tt, xb_in, xb_out, S["o2"])



    import os as _os
    P2_STAGES = _os.environ.get("P2_STAGES", "ABCDE")

    def ffn0_phase(xb_in, xb_out):
        S["xs"] = carve16(NCH * NT)
        S["h"] = carve16(32 * NT)
        S["wb0"] = carve16(8192)
        S["wb1"] = carve16(8192)
        W["g1"] = carve16(NCH * D)
        W["g2"] = carve16(NCH * (D // 2))
        W["g3"] = carve16(4 * 8)
        W["aff"] = carve16(NCH * 8)
        W["b2s"] = carve16(D, npart=8)
        S["gall"] = carve16(T_CORE, npart=8)
        S["srow"] = carve16(NT, npart=1)
        S["gb"] = carve16(NT)
        S["acc"] = carve16(NCH * NT)
        S["gfin"] = carve16(8)
        S["grow"] = carve16(NT, npart=1)
        V32 = {}
        V32["noise"] = carve32(NT, npart=8)
        V32["lg"] = carve32(NT, npart=8)
        V32["impg"] = carve32(256)
        V32["scrap"] = carve32(256)
        V32["gnan"] = carve32(8)
        V32["xsum"] = carve32(16)
        V32["ge"] = carve32(8)
        V32["lgt"] = carve32(8)
        V32["t8a"] = carve32(8)
        V32["t8b"] = carve32(8)
        BIA = {}
        for a in range(8):
            for (wn, K, M, act) in E_SPECS[a]:
                if act is not None:
                    BIA[wn] = carve32(M // 128)
        for nm in ("gate_b1", "ln_scale", "ln_offset"):
            BIA[nm] = carve32(8)
        BIA["gate_b2"] = carve32(4)
        T1 = {}
        for nm in ("lo", "hi", "mid", "ff", "cnt", "omf", "vstar_c", "dsq"):
            T1[nm] = sb("c2_" + nm, [128, 1])
        for nm in ("vstar", "temp", "flag", "ns", "gm", "sval"):
            T1[nm] = sb("c2r_" + nm, [1, 1])
        T1["u8"] = sb("c2_u8", [1, 8])
        T1["u8b"] = sb("c2_u8b", [1, 8])
        T1["r128"] = sb("c2_r128", [1, 128])
        T1["b3"] = sb("c2_b3", [8, 1])
        T1["affb"] = sb("c2_affb", [8, 1])
        T1["aff03"] = sb("c2_aff03", [8, 2])
        T1["ns8"] = sb("c2_ns8", [8, 1])
        T1["fl8"] = sb("c2_fl8", [8, 1])
        T1["ones128f"] = sb("c2_o128f", [128, 1])
        T1["ones1x8f"] = sb("c2_o1x8f", [1, 8])

        b.op("vector", lambda q: q.memset(T1["ones128f"][:], 1.0))
        b.op("vector", lambda q: q.memset(T1["ones1x8f"][:], 1.0))
        b.opf("vector", lambda q: q.memset(V32["xsum"][:], 0.0), "DVE")
        for k in range(NCH):
            src = bass.AP(inp["gate_w1"].tensor, k * 128 * D, [[D, 128], [1, D]])
            b.op("gpsimd", lambda q, s=src, k=k:
                 q.dma_start(out=W["g1"][:, k * D:(k + 1) * D], in_=s), "WT", 16)
        for k in range(NCH):
            src = bass.AP(inp["gate_w2"].tensor, k * 128 * (D // 2),
                          [[D // 2, 128], [1, D // 2]])
            b.op("gpsimd", lambda q, s=src, k=k:
                 q.dma_start(out=W["g2"][:, k * (D // 2):(k + 1) * (D // 2)], in_=s),
                 "WT", 16)
        for k in range(4):
            src = bass.AP(inp["gate_w3p"].tensor, k * 128 * 8, [[8, 128], [1, 8]])
            b.op("gpsimd", lambda q, s=src, k=k:
                 q.dma_start(out=W["g3"][:, k * 8:(k + 1) * 8], in_=s), "WT", 16)
        for k in range(NCH):
            src = bass.AP(inp["aff_wp"].tensor, k * 128 * 8, [[8, 128], [1, 8]])
            b.op("gpsimd", lambda q, s=src, k=k:
                 q.dma_start(out=W["aff"][:, k * 8:(k + 1) * 8], in_=s), "WT", 16)
        b.op("gpsimd", lambda q: q.dma_start(out=W["b2s"][:], in_=inp["b2stack"]), "WT", 16)
        for nm in ("gate_b1", "ln_scale", "ln_offset", "gate_b2"):
            b.op("sync", lambda q, nm=nm: q.dma_start(out=BIA[nm][:], in_=inp[nm]),
                 "LD", 16)
        for a in range(8):
            for (wn, K, M, act) in E_SPECS[a]:
                if act is not None:
                    b.op("sync", lambda q, wn=wn:
                         q.dma_start(out=BIA[wn][:], in_=inp["b_" + wn]), "LD", 16)
        b.op("sync", lambda q: q.dma_start(out=T1["b3"][:], in_=inp["gate_b3p"]), "LD", 16)
        b.op("sync", lambda q: q.dma_start(out=T1["affb"][:], in_=inp["aff_bp"]), "LD", 16)
        b.op("sync", lambda q: q.dma_start(out=T1["u8"][:], in_=inp["usage"]), "LD", 16)
        b.op("sync", lambda q: q.dma_start(out=V32["gnan"][:], in_=inp["gnan"]), "LD", 16)

        row_dma = lambda row_tv, dram_ap, c0, n=NT: None

        for tt in range(NTT):
            c0 = tt * NT
            load_x_tile(tt, xb_in)
            b.wait("scalar", "LD")
            b.wait("scalar", "DVE")
            for k in range(NCH):
                sl = slice(k * NT, (k + 1) * NT)
                b.op("scalar", lambda q, sl=sl: q.activation(
                    S["o2"][:, sl], S["x"][:, sl], AF.Square), "ACT")
            b.wait("tensor", "ACT")
            b.wait("tensor", "DVE")
            b.wait("tensor", "LD")
            for k in range(NCH):
                b.op("tensor", lambda q, k=k: q.matmul(
                    ps["F"][0:1, :], C["ones128_f"][:], S["x"][:, k * NT:(k + 1) * NT],
                    start=(k == 0), stop=(k == NCH - 1)))
            for k in range(NCH):
                b.op("tensor", lambda q, k=k: q.matmul(
                    ps["G"][0:1, :], C["ones128_f"][:], S["o2"][:, k * NT:(k + 1) * NT],
                    start=(k == 0), stop=(k == NCH - 1)),
                    "PE" if k == NCH - 1 else None)
            b.wait("vector", "PE")
            b.op("vector", lambda q: q.tensor_scalar(
                R["mu"][:], ps["F"][0:1, :], 1.0 / D, None, ALU.mult))
            b.op("vector", lambda q: q.tensor_scalar(
                R["tmp1"][:], ps["G"][0:1, :], 1.0 / D, None, ALU.mult))
            b.fence("vector")
            b.op("vector", lambda q: q.tensor_tensor(
                R["tmp2"][:], R["mu"][:], R["mu"][:], ALU.mult))
            b.fence("vector")
            b.op("vector", lambda q: q.tensor_tensor(
                R["s2"][:], R["tmp1"][:], R["tmp2"][:], ALU.subtract))
            b.fence("vector")
            b.opf("vector", lambda q: q.tensor_scalar(
                R["tmp1"][:], R["s2"][:], 1e-5, None, ALU.add), "DVE")
            b.wait("scalar", "DVE")
            b.opf("scalar", lambda q: q.activation(
                R["tmp2"][:], R["tmp1"][:], AF.Sqrt), "ACT")
            b.wait("vector", "ACT")
            b.op("vector", lambda q: q.reciprocal(R["r"][:], R["tmp2"][:]))
            b.fence("vector")
            b.op("vector", lambda q: q.tensor_tensor(
                R["tmp2"][:], R["r"][:], R["r"][:], ALU.mult))
            b.fence("vector")
            b.op("vector", lambda q: q.tensor_tensor(
                R["tmp2"][:], R["tmp2"][:], R["tmp1"][:], ALU.mult))
            b.fence("vector")
            b.op("vector", lambda q: q.tensor_scalar(
                R["tmp2"][:], R["tmp2"][:], -0.5, 1.5, ALU.mult, ALU.add))
            b.fence("vector")
            b.opf("vector", lambda q: q.tensor_tensor(
                R["r"][:], R["r"][:], R["tmp2"][:], ALU.mult), "DVE")
            b.wait("tensor", "DVE")
            b.op("tensor", lambda q: q.matmul(
                ps["C"][:], C["ones1x128_f"][:], R["mu"][:], start=True, stop=True))
            b.op("tensor", lambda q: q.matmul(
                ps["D"][:], C["ones1x128_f"][:], R["r"][:], start=True, stop=True), "PE")
            b.wait("vector", "PE")
            for k in range(NCH):
                sl = slice(k * NT, (k + 1) * NT)
                b.op("vector", lambda q, sl=sl: q.tensor_tensor(
                    S["t32a"][:], S["x"][:, sl], ps["C"][:], ALU.subtract))
                b.op("vector", lambda q, sl=sl: q.tensor_copy(
                    S["xs"][:, sl], S["t32a"][:]))
                b.op("vector", lambda q, sl=sl: q.tensor_tensor(
                    S["x"][:, sl], S["t32a"][:], ps["D"][:], ALU.mult),
                    "DVE" if k == NCH - 1 else None)
            b.wait("scalar", "DVE")
            for k in range(NCH):
                sl = slice(k * NT, (k + 1) * NT)
                b.op("scalar", lambda q, sl=sl: q.activation(
                    S["o2"][:, sl], S["x"][:, sl], AF.Square), "ACT")
                b.op("scalar", lambda q, sl=sl: q.activation(
                    S["bf_b"][:, sl], S["xs"][:, sl], AF.Abs),
                    "ACT" if k == NCH - 1 else None)
            b.wait("tensor", "ACT")
            for k in range(NCH):
                b.op("tensor", lambda q, k=k: q.matmul(
                    ps["E"][0:1, :], C["ones128_f"][:], S["o2"][:, k * NT:(k + 1) * NT],
                    start=(k == 0), stop=(k == NCH - 1)))
            for k in range(NCH):
                b.op("tensor", lambda q, k=k: q.matmul(
                    ps["F"][0:1, :], C["ones128_bf"][:], S["bf_b"][:, k * NT:(k + 1) * NT],
                    start=(k == 0), stop=(k == NCH - 1)),
                    "PE" if k == NCH - 1 else None)
            b.wait("vector", "PE")
            b.opf("vector", lambda q: q.tensor_copy(R["s3"][:], ps["F"][0:1, :]), "DVE")
            b.wait("scalar", "PE")
            b.wait("scalar", "DVE")
            b.opf("scalar", lambda q: q.activation(
                R["imp"][:], ps["E"][0:1, :], AF.Sqrt), "ACT")
            b.wait("vector", "ACT")
            b.fence("vector")
            b.wait("sync", "DVE")
            b.wait("sync", "ACT")
            b.wait("sync", "PE")
            for k in range(NCH):
                dst = bass.AP(scr["xc"].tensor, k * 128 * T_CORE + c0,
                              [[T_CORE, 128], [1, NT]])
                b.op("sync", lambda q, d=dst, k=k:
                     q.dma_start(out=d, in_=S["xs"][:, k * NT:(k + 1) * NT]), "ST", 16)
            b.op("sync", lambda q, tt=tt: q.dma_start(
                out=bass.AP(scr["cc_in"].tensor, tt * NT, [[NT, 1], [1, NT]]),
                in_=R["imp"][:]), "ST", 16)
            b.op("sync", lambda q, c0=c0: q.dma_start(
                out=bass.AP(scr["r"].tensor, c0, [[NT, 1], [1, NT]]),
                in_=R["r"][:]), "ST", 16)
            b.op("sync", lambda q, c0=c0: q.dma_start(
                out=bass.AP(scr["s3"].tensor, c0, [[NT, 1], [1, NT]]),
                in_=R["s3"][:]), "ST", 16)

        if "B" not in P2_STAGES:
            return
        b.wait("gpsimd", "ST")
        b.op("gpsimd", lambda q: q.collective_compute(
            "AllGather", ALU.bypass, replica_groups=[list(range(NCORES))],
            ins=[cc_in_t.ap().opt()], outs=[cc_out_t.ap().opt()]), "CC", 1)
        b.wait("sync", "CC")
        b.op("sync", lambda q: q.dma_start(out=V32["impg"][:], in_=scr["cc_out"]),
             "LD", 16)
        b.wait("vector", "LD")

        b.opf("vector", lambda q: q.tensor_scalar(
            V32["scrap"][:], V32["impg"][:], 1.0 / 32768.0, 0.0, ALU.mult,
            ALU.add, accum_out=T1["dsq"][:]), "DVE")
        b.wait("tensor", "DVE")
        b.op("tensor", lambda q: q.matmul(ps["G"][0:1, 0:1], T1["ones128f"][:],
                                          T1["dsq"][:], start=True, stop=True), "PE")
        b.wait("vector", "PE")
        b.opf("vector", lambda q: q.tensor_copy(T1["gm"][:], ps["G"][0:1, 0:1]), "DVE")
        b.wait("tensor", "DVE")
        b.op("tensor", lambda q: q.matmul(ps["G"][:, 0:1], C["ones1x128_f"][:],
                                          T1["gm"][:], start=True, stop=True), "PE")
        b.wait("vector", "PE")
        b.opf("vector", lambda q: q.tensor_copy(T1["vstar_c"][:], ps["G"][:, 0:1]), "DVE")
        b.opf("vector", lambda q: q.tensor_scalar(
            V32["scrap"][:], V32["impg"][:], T1["vstar_c"][:], None, ALU.subtract),
            "DVE")
        b.wait("scalar", "DVE")
        b.opf("scalar", lambda q: q.activation(
            V32["scrap"][:], V32["scrap"][:], AF.Square, accum_out=T1["dsq"][:]), "ACT")
        b.wait("tensor", "ACT")
        b.op("tensor", lambda q: q.matmul(ps["G"][0:1, 0:1], T1["ones128f"][:],
                                          T1["dsq"][:], start=True, stop=True), "PE")
        b.wait("vector", "PE")
        b.opf("vector", lambda q: q.tensor_scalar(
            T1["gm"][:], ps["G"][0:1, 0:1], 1.0 / 32768.0, None, ALU.mult), "DVE")
        b.wait("scalar", "DVE")
        b.opf("scalar", lambda q: q.activation(T1["sval"][:], T1["gm"][:], AF.Sqrt),
              "ACT")
        b.wait("vector", "ACT")
        b.opf("vector", lambda q: q.tensor_scalar(
            T1["temp"][:], T1["sval"][:], 0.1, 1.0, ALU.mult, ALU.add), "DVE")

        b.op("vector", lambda q: q.memset(T1["lo"][:], 0.0))
        b.op("vector", lambda q: q.tensor_reduce(
            T1["hi"][:], V32["impg"][:], AX.X, ALU.max))
        b.fence("vector")
        b.opf("vector", lambda q: q.tensor_copy(T1["mid"][:], T1["hi"][:]), "DVE")
        b.wait("tensor", "DVE")
        b.op("tensor", lambda q: q.transpose(
            ps["E"][0:1, 0:128], T1["mid"][:], W["identf"][:]), "PE")
        b.wait("vector", "PE")
        b.op("vector", lambda q: q.tensor_reduce(
            T1["gm"][:], ps["E"][0:1, 0:128], AX.X, ALU.max))
        b.fence("vector")
        b.opf("vector", lambda q: q.tensor_copy(T1["r128"][0:1, 0:1], T1["gm"][:]),
              "DVE")
        b.wait("tensor", "DVE")
        b.op("tensor", lambda q: q.matmul(ps["G"][:, 0:1], C["ones1x128_f"][:],
                                          T1["gm"][:], start=True, stop=True), "PE")
        b.wait("vector", "PE")
        b.opf("vector", lambda q: q.tensor_copy(T1["hi"][:], ps["G"][:, 0:1]), "DVE")
        if dbg:
            b.wait("gpsimd", "DVE")
            b.op("gpsimd", lambda q: q.dma_start(out=dbgo["d_hi0"], in_=T1["hi"][:]), "ST", 16)
            b.wait("vector", "ST")
        K_SEL = float(int(0.9 * (32768 - 1)) + 1)
        for it in range(60):
            b.op("vector", lambda q: q.tensor_tensor(
                T1["mid"][:], T1["lo"][:], T1["hi"][:], ALU.add))
            b.fence("vector")
            b.op("vector", lambda q: q.tensor_scalar(
                T1["mid"][:], T1["mid"][:], 0.5, None, ALU.mult))
            b.fence("vector")
            b.opf("vector", lambda q: q.tensor_scalar(
                V32["scrap"][:], V32["impg"][:], T1["mid"][:], 0.0, ALU.is_le,
                ALU.add, accum_out=T1["dsq"][:]), "DVE")
            b.wait("tensor", "DVE")
            b.op("tensor", lambda q: q.matmul(ps["G"][0:1, 0:1], T1["ones128f"][:],
                                              T1["dsq"][:], start=True, stop=True),
                 "PE")
            b.wait("vector", "PE")
            b.opf("vector", lambda q: q.tensor_copy(T1["gm"][:], ps["G"][0:1, 0:1]),
                  "DVE")
            b.wait("tensor", "DVE")
            b.op("tensor", lambda q: q.matmul(ps["G"][:, 0:1], C["ones1x128_f"][:],
                                              T1["gm"][:], start=True, stop=True), "PE")
            b.wait("vector", "PE")
            if dbg and it == 0:
                b.op("vector", lambda q: q.tensor_copy(T1["vstar_c"][:], ps["G"][:, 0:1]))
                b.fence("vector")
                b.opf("vector", lambda q: q.tensor_copy(T1["dsq"][:], T1["dsq"][:]), "DVE")
                b.wait("gpsimd", "DVE")
                b.op("gpsimd", lambda q: q.dma_start(out=dbgo["d_cnt0"], in_=T1["vstar_c"][:]), "ST", 16)
                b.op("gpsimd", lambda q: q.dma_start(out=dbgo["d_dsq0"], in_=T1["dsq"][:]), "ST", 16)
                b.wait("vector", "ST")
            b.op("vector", lambda q: q.tensor_scalar(
                T1["ff"][:], ps["G"][:, 0:1], K_SEL, None, ALU.is_ge))
            b.fence("vector")
            b.op("vector", lambda q: q.tensor_tensor(
                T1["cnt"][:], T1["mid"][:], T1["hi"][:], ALU.subtract))
            b.fence("vector")
            b.op("vector", lambda q: q.tensor_tensor(
                T1["cnt"][:], T1["cnt"][:], T1["ff"][:], ALU.mult))
            b.fence("vector")
            b.op("vector", lambda q: q.tensor_tensor(
                T1["hi"][:], T1["hi"][:], T1["cnt"][:], ALU.add))
            b.op("vector", lambda q: q.tensor_scalar(
                T1["omf"][:], T1["ff"][:], -1.0, 1.0, ALU.mult, ALU.add))
            b.fence("vector")
            b.op("vector", lambda q: q.tensor_tensor(
                T1["cnt"][:], T1["mid"][:], T1["lo"][:], ALU.subtract))
            b.fence("vector")
            b.op("vector", lambda q: q.tensor_tensor(
                T1["cnt"][:], T1["cnt"][:], T1["omf"][:], ALU.mult))
            b.fence("vector")
            b.opf("vector", lambda q: q.tensor_tensor(
                T1["lo"][:], T1["lo"][:], T1["cnt"][:], ALU.add), "DVE")
        b.op("vector", lambda q: q.tensor_scalar(
            V32["scrap"][:], V32["impg"][:], T1["lo"][:], None, ALU.is_le))
        b.fence("vector")
        b.op("vector", lambda q: q.scalar_tensor_tensor(
            V32["scrap"][:], V32["scrap"][:], 1e38, V32["impg"][:], ALU.mult, ALU.add))
        b.fence("vector")
        b.op("vector", lambda q: q.tensor_reduce(
            T1["mid"][:], V32["scrap"][:], AX.X, ALU.min))
        b.fence("vector")
        b.opf("vector", lambda q: q.tensor_copy(T1["cnt"][:], T1["mid"][:]), "DVE")
        b.wait("tensor", "DVE")
        b.op("tensor", lambda q: q.transpose(
            ps["E"][0:1, 0:128], T1["mid"][:], W["identf"][:]), "PE")
        b.wait("vector", "PE")
        b.op("vector", lambda q: q.tensor_reduce(
            T1["vstar"][:], ps["E"][0:1, 0:128], AX.X, ALU.min))
        b.fence("vector")
        b.opf("vector", lambda q: q.tensor_copy(T1["r128"][0:1, 0:1], T1["vstar"][:]),
              "DVE")

        b.op("vector", lambda q: q.tensor_scalar(
            T1["u8b"][:], T1["u8"][:], 1e-8, None, ALU.add))
        b.fence("vector")
        b.op("vector", lambda q: q.tensor_reduce(
            T1["sval"][:], T1["u8b"][:], AX.X, ALU.min))
        b.fence("vector")
        b.op("vector", lambda q: q.tensor_scalar(
            T1["flag"][:], T1["sval"][:], 0.0, None, ALU.is_lt))
        b.fence("vector")
        b.opf("vector", lambda q: q.tensor_scalar(
            T1["u8b"][:], T1["u8b"][:], 1e-30, None, ALU.max), "DVE")
        b.wait("scalar", "DVE")
        b.opf("scalar", lambda q: q.activation(T1["u8b"][:], T1["u8b"][:], AF.Ln),
              "ACT")
        b.wait("vector", "ACT")
        b.op("vector", lambda q: q.tensor_tensor(
            T1["u8b"][:], T1["u8"][:], T1["u8b"][:], ALU.mult))
        b.fence("vector")
        b.op("vector", lambda q: q.tensor_reduce(
            T1["sval"][:], T1["u8b"][:], AX.X, ALU.add, negate=True))
        b.fence("vector")
        b.op("vector", lambda q: q.tensor_scalar(
            T1["sval"][:], T1["sval"][:], -0.01, 0.02, ALU.mult, ALU.add))
        b.fence("vector")
        b.op("vector", lambda q: q.tensor_scalar(
            T1["sval"][:], T1["sval"][:], 1e-4, None, ALU.max))
        b.fence("vector")
        b.op("vector", lambda q: q.tensor_scalar(
            T1["gm"][:], T1["flag"][:], -1.0, 1.0, ALU.mult, ALU.add))
        b.fence("vector")
        b.opf("vector", lambda q: q.tensor_tensor(
            T1["ns"][:], T1["sval"][:], T1["gm"][:], ALU.mult), "DVE")
        b.wait("tensor", "DVE")
        b.op("tensor", lambda q: q.matmul(ps["G"][0:8, 0:1], T1["ones1x8f"][:],
                                          T1["ns"][:], start=True, stop=True), "PE")
        b.wait("vector", "PE")
        b.opf("vector", lambda q: q.tensor_copy(T1["ns8"][:], ps["G"][0:8, 0:1]), "DVE")
        b.wait("tensor", "DVE")
        b.op("tensor", lambda q: q.matmul(ps["G"][0:8, 0:1], T1["ones1x8f"][:],
                                          T1["flag"][:], start=True, stop=True), "PE")
        b.wait("vector", "PE")
        b.opf("vector", lambda q: q.tensor_copy(T1["fl8"][:], ps["G"][0:8, 0:1]), "DVE")
        b.wait("tensor", "DVE")
        b.op("tensor", lambda q: q.matmul(ps["G"][:, 0:1], C["ones1x128_f"][:],
                                          T1["flag"][:], start=True, stop=True), "PE")
        b.wait("vector", "PE")
        b.op("vector", lambda q: q.tensor_copy(T1["ff"][:], ps["G"][:, 0:1]))
        b.fence("vector")
        b.opf("vector", lambda q: q.tensor_scalar(
            T1["omf"][:], T1["ff"][:], -1.0, 1.0, ALU.mult, ALU.add), "DVE")

        if "C" not in P2_STAGES:
            return
        if dbg:
            b.wait("gpsimd", "DVE")
            b.op("gpsimd", lambda q: q.dma_start(out=dbgo["d_vstar"], in_=T1["vstar"][:]), "ST", 16)
            b.op("gpsimd", lambda q: q.dma_start(out=dbgo["d_impg"], in_=V32["impg"][:]), "ST", 16)
            b.op("gpsimd", lambda q: q.dma_start(out=dbgo["d_temp"], in_=T1["temp"][:]), "ST", 16)
            b.op("gpsimd", lambda q: q.dma_start(out=dbgo["d_flag"], in_=T1["flag"][:]), "ST", 16)
        for tt in range(NTT):
            c0 = tt * NT
            b.wait("sync", "PE")
            b.wait("sync", "DVE")
            b.wait("sync", "ST")
            for k in range(NCH):
                src = bass.AP(scr["xc"].tensor, k * 128 * T_CORE + c0,
                              [[T_CORE, 128], [1, NT]])
                b.op("sync", lambda q, s=src, k=k:
                     q.dma_start(out=S["xs"][:, k * NT:(k + 1) * NT], in_=s), "LD", 16)
            b.op("sync", lambda q, tt=tt: q.dma_start(
                out=R["imp"][:],
                in_=bass.AP(scr["cc_in"].tensor, tt * NT, [[NT, 1], [1, NT]])), "LD", 16)
            b.op("sync", lambda q, c0=c0: q.dma_start(
                out=R["r"][:], in_=bass.AP(scr["r"].tensor, c0, [[NT, 1], [1, NT]])),
                "LD", 16)
            b.op("sync", lambda q, c0=c0: q.dma_start(
                out=R["s3"][:], in_=bass.AP(scr["s3"].tensor, c0, [[NT, 1], [1, NT]])),
                "LD", 16)
            b.wait("vector", "LD")
            b.op("vector", lambda q: q.tensor_scalar(
                R["tmp1"][:], R["imp"][:], T1["vstar"][:], None, ALU.is_gt))
            b.op("vector", lambda q: q.tensor_tensor(
                R["tmp2"][:], R["r"][:], R["s3"][:], ALU.mult))
            b.fence("vector")
            b.op("vector", lambda q: q.tensor_tensor(
                R["tmp2"][:], R["tmp2"][:], R["tmp1"][:], ALU.mult))
            b.fence("vector")
            b.op("vector", lambda q: q.tensor_scalar(
                R["tmp2"][:], R["tmp2"][:], 1e-8, None, ALU.add))
            b.fence("vector")
            b.op("vector", lambda q: q.tensor_scalar(
                R["tmp2"][:], R["tmp2"][:], T1["temp"][:], None, ALU.mult))
            b.fence("vector")
            b.op("vector", lambda q: q.reciprocal(R["s2"][:], R["tmp2"][:]))
            b.fence("vector")
            b.op("vector", lambda q: q.tensor_tensor(
                R["s2"][:], R["s2"][:], R["r"][:], ALU.mult))
            b.fence("vector")
            b.op("vector", lambda q: q.tensor_tensor(
                R["s2"][:], R["s2"][:], R["tmp1"][:], ALU.mult))
            b.fence("vector")
            b.opf("vector", lambda q: q.tensor_copy(S["srow"][:], R["s2"][:]), "DVE")
            b.wait("tensor", "DVE")
            bcast_row(S["srow"][:], "D")
            b.wait("vector", "PE")
            for k in range(NCH):
                sl = slice(k * NT, (k + 1) * NT)
                b.op("vector", lambda q, sl=sl: q.tensor_tensor(
                    S["xs"][:, sl], S["xs"][:, sl], ps["D"][:], ALU.mult),
                    "DVE" if k == NCH - 1 else None)
            if dbg and tt == 0:
                b.wait("gpsimd", "DVE")
                b.op("gpsimd", lambda q: q.dma_start(out=dbgo["d_xs"], in_=S["xs"][:]), "ST", 16)
                b.op("gpsimd", lambda q: q.dma_start(out=dbgo["d_srow"], in_=S["srow"][:]), "ST", 16)
                b.op("gpsimd", lambda q: q.dma_start(out=dbgo["d_imp"], in_=R["imp"][:]), "ST", 16)
                b.wait("vector", "ST")
            b.wait("sync", "DVE")
            for k in range(NCH):
                dst = bass.AP(scr["xs"].tensor, k * 128 * T_CORE + c0,
                              [[T_CORE, 128], [1, NT]])
                b.op("sync", lambda q, d=dst, k=k:
                     q.dma_start(out=d, in_=S["xs"][:, k * NT:(k + 1) * NT]), "ST", 16)
            bb = tt // (T_BATCH // NT)
            for k in range(NCH):
                sl = slice(k * NT, (k + 1) * NT)
                b.opf("vector", lambda q, sl=sl: q.tensor_scalar(
                    S["t32a"][:], S["xs"][:, sl], 1.0 / T_BATCH, 0.0, ALU.mult,
                    ALU.add, accum_out=T1["dsq"][:]), "DVE")
                col = bb * 8 + k
                b.op("vector", lambda q, col=col: q.tensor_tensor(
                    V32["xsum"][:, col:col + 1], V32["xsum"][:, col:col + 1],
                    T1["dsq"][:], ALU.add), "DVE")
            b.wait("tensor", "DVE")
            mm_sweep_evac(
                NCH,
                lambda m, k: W["g1"][:, k * D + m * 128:k * D + (m + 1) * 128],
                lambda m, k: S["xs"][:, k * NT:(k + 1) * NT],
                lambda m, bank: b.op("scalar", lambda q, m=m, bank=bank: q.activation(
                    S["bf_a"][:, m * NT:(m + 1) * NT], ps[bank][:], AF.Silu,
                    bias=BIA["gate_b1"][:, m:m + 1]), "ACT"))
            b.wait("scalar", "ACT")
            for k in range(NCH):
                sl = slice(k * NT, (k + 1) * NT)
                b.op("scalar", lambda q, sl=sl: q.activation(
                    S["bf_b"][:, sl], S["bf_a"][:, sl], AF.Square), "ACT")
            b.wait("tensor", "ACT")
            b.wait("tensor", "DVE")
            stats_rows(S["bf_a"], S["bf_b"], tt)
            b.wait("tensor", "DVE")
            bcast_row(R["mu_bf"][:], "C")
            bcast_row(R["r_bf"][:], "D")
            b.wait("vector", "PE")
            for k in range(NCH):
                sl = slice(k * NT, (k + 1) * NT)
                b.op("vector", lambda q, sl=sl: q.tensor_tensor(
                    S["t32a"][:], S["bf_a"][:, sl], ps["C"][:], ALU.subtract))
                b.op("vector", lambda q, sl=sl: q.tensor_tensor(
                    S["t32a"][:], S["t32a"][:], ps["D"][:], ALU.mult))
                b.op("vector", lambda q, sl=sl, k=k: q.tensor_scalar(
                    S["bf_a"][:, sl], S["t32a"][:], BIA["ln_scale"][:, k:k + 1],
                    BIA["ln_offset"][:, k:k + 1], ALU.mult, ALU.add),
                    "DVE" if k == NCH - 1 else None)
            b.wait("tensor", "DVE")
            mm_sweep_evac(
                4,
                lambda m, k: W["g2"][:, k * (D // 2) + m * 128:k * (D // 2) + (m + 1) * 128],
                lambda m, k: S["bf_a"][:, k * NT:(k + 1) * NT],
                lambda m, bank: b.op("scalar", lambda q, m=m, bank=bank: q.activation(
                    S["bf_c"][:, m * NT:(m + 1) * NT], ps[bank][:], AF.Silu,
                    bias=BIA["gate_b2"][:, m:m + 1]), "ACT"))
            b.wait("tensor", "ACT")
            b.wait("tensor", "DVE")
            for k in range(4):
                b.op("tensor", lambda q, k=k: q.matmul(
                    ps["E"][0:8, :], W["g3"][:, k * 8:(k + 1) * 8],
                    S["bf_c"][:, k * NT:(k + 1) * NT],
                    start=(k == 0), stop=(k == 3)), "PE" if k == 3 else None)
            b.wait("vector", "PE")
            b.opf("vector", lambda q: q.tensor_scalar(
                V32["lg"][:], ps["E"][0:8, :], T1["b3"][:], None, ALU.add), "DVE")
            b.wait("sync", "DVE")
            b.op("sync", lambda q, c0=c0: q.dma_start(
                out=bass.AP(scr["lg"].tensor, c0, [[T_CORE, 8], [1, NT]]),
                in_=V32["lg"][:]), "ST", 16)
            if dbg and tt == 0:
                b.wait("gpsimd", "DVE")
                b.op("gpsimd", lambda q: q.dma_start(out=dbgo["d_lg"], in_=V32["lg"][:]), "ST", 16)
                b.wait("vector", "ST")

        if "D" not in P2_STAGES:
            return
        b.wait("vector", "DVE")
        for bb in range(2):
            b.opf("vector", lambda q, bb=bb: q.tensor_copy(
                S["gfin"][:], V32["xsum"][:, bb * 8:bb * 8 + 8]), "DVE")
            b.wait("tensor", "DVE")
            for k in range(NCH):
                b.op("tensor", lambda q, k=k: q.matmul(
                    ps["F"][0:8, 0:1], W["aff"][:, k * 8:(k + 1) * 8],
                    S["gfin"][:, k:k + 1], start=(k == 0), stop=(k == NCH - 1)),
                    "PE" if k == NCH - 1 else None)
            b.wait("vector", "PE")
            b.opf("vector", lambda q, bb=bb: q.tensor_scalar(
                T1["aff03"][:, bb:bb + 1], ps["F"][0:8, 0:1], T1["affb"][:], 0.3,
                ALU.add, ALU.mult), "DVE")

        for tt in range(NTT):
            c0 = tt * NT
            bb = tt // (T_BATCH // NT)
            b.wait("sync", "ST")
            b.op("sync", lambda q, c0=c0: q.dma_start(
                out=V32["lg"][:],
                in_=bass.AP(scr["lg"].tensor, c0, [[T_CORE, 8], [1, NT]])), "LD", 16)
            b.op("sync", lambda q, c0=c0: q.dma_start(
                out=V32["noise"][:],
                in_=bass.AP(inp["noise_p"].tensor, c0, [[T_CORE, 8], [1, NT]])),
                "LD", 16)
            b.wait("vector", "LD")
            b.op("vector", lambda q, bb=bb: q.tensor_scalar(
                V32["lg"][:], V32["lg"][:], T1["aff03"][:, bb:bb + 1], None, ALU.add))
            b.fence("vector")
            b.op("vector", lambda q: q.scalar_tensor_tensor(
                V32["lg"][:], V32["noise"][:], T1["ns8"][:], V32["lg"][:],
                ALU.mult, ALU.add))
            b.fence("vector")
            b.opf("vector", lambda q: q.tensor_scalar(
                V32["lg"][:], V32["lg"][:], 1.0 / 1.5, None, ALU.mult), "DVE")
            for sub in range(4):
                scol = sub * 128
                b.wait("tensor", "DVE")
                b.wait("tensor", "ACT")
                b.op("tensor", lambda q, scol=scol: q.transpose(
                    ps["E"][:, 0:128], V32["lg"][0:8, scol:scol + 128],
                    W["identf"][0:8, :]), "PE")
                b.wait("vector", "PE")
                b.op("vector", lambda q: q.tensor_copy(
                    V32["lgt"][:], ps["E"][:, 0:8]))
                b.fence("vector")
                b.op("vector", lambda q: q.tensor_reduce(
                    T1["mid"][:], V32["lgt"][:], AX.X, ALU.max))
                b.fence("vector")
                b.op("vector", lambda q: q.tensor_scalar(
                    V32["t8a"][:], V32["lgt"][:], T1["mid"][:], None, ALU.is_ge))
                b.fence("vector")
                b.op("vector", lambda q: q.scalar_tensor_tensor(
                    V32["t8b"][:], V32["t8a"][:], -1e38, V32["lgt"][:],
                    ALU.mult, ALU.add))
                b.fence("vector")
                b.op("vector", lambda q: q.tensor_reduce(
                    T1["mid"][:], V32["t8b"][:], AX.X, ALU.max))
                b.fence("vector")
                b.op("vector", lambda q: q.tensor_scalar(
                    V32["t8a"][:], V32["lgt"][:], T1["mid"][:], None, ALU.is_ge))
                b.fence("vector")
                b.op("vector", lambda q: q.tensor_tensor(
                    V32["t8b"][:], V32["lgt"][:], V32["t8a"][:], ALU.mult))
                b.op("vector", lambda q: q.tensor_scalar(
                    V32["t8a"][:], V32["t8a"][:], 1.0, 1e9, ALU.subtract, ALU.mult))
                b.fence("vector")
                b.op("vector", lambda q: q.tensor_tensor(
                    V32["t8b"][:], V32["t8b"][:], V32["t8a"][:], ALU.add))
                b.fence("vector")
                b.op("vector", lambda q: q.tensor_reduce(
                    T1["mid"][:], V32["t8b"][:], AX.X, ALU.max, negate=True))
                b.fence("vector")
                b.opf("vector", lambda q: q.tensor_copy(T1["cnt"][:], T1["mid"][:]),
                      "DVE")
                b.wait("scalar", "DVE")
                b.opf("scalar", lambda q: q.activation(
                    V32["ge"][:], V32["t8b"][:], AF.Exp, bias=T1["mid"][:],
                    accum_out=T1["dsq"][:]), "ACT")
                b.wait("vector", "ACT")
                b.op("vector", lambda q: q.reciprocal(T1["cnt"][:], T1["dsq"][:]))
                b.fence("vector")
                b.op("vector", lambda q: q.tensor_scalar(
                    V32["ge"][:], V32["ge"][:], T1["cnt"][:], None, ALU.mult))
                b.fence("vector")
                b.op("vector", lambda q: q.tensor_scalar(
                    V32["ge"][:], V32["ge"][:], T1["omf"][:], None, ALU.mult))
                b.op("vector", lambda q: q.tensor_scalar(
                    V32["t8a"][:], V32["gnan"][:], T1["ff"][:], None, ALU.mult))
                b.fence("vector")
                b.op("vector", lambda q: q.tensor_tensor(
                    V32["ge"][:], V32["ge"][:], V32["t8a"][:], ALU.add))
                b.fence("vector")
                b.opf("vector", lambda q: q.tensor_copy(S["gfin"][:], V32["ge"][:]),
                      "DVE")
                b.wait("tensor", "DVE")
                b.op("tensor", lambda q: q.transpose(
                    ps["T"][0:8, 0:128], S["gfin"][:], W["ident"][:]), "PE")
                b.wait("vector", "PE")
                b.opf("vector", lambda q, c0=c0, scol=scol: q.tensor_copy(
                    S["gall"][0:8, c0 + scol:c0 + scol + 128], ps["T"][0:8, 0:128]),
                    "DVE")

        if dbg:
            b.wait("gpsimd", "DVE")
            b.op("gpsimd", lambda q: q.dma_start(out=dbgo["d_gall"], in_=S["gall"][:]), "ST", 16)
        if "E" not in P2_STAGES:
            return
        def stream_layer(wn, K, M, rhs_tv, rhs_cols, out_tv, act, bias_tv):
            nk = K // 128
            mg = min(1024, (8192 // (K // 128)) // 128 * 128)
            wsrc = inp[wn]
            b.wait("tensor", "ACT")
            b.wait("tensor", "DVE")
            b.wait("gpsimd", "PE")
            pe_marks = {}
            evac_marks = []
            nmg = M // mg
            for gi in range(nmg):
                buf = S["wb0"] if gi % 2 == 0 else S["wb1"]
                if gi >= 2:
                    b.wait("gpsimd", "PE", pe_marks[gi - 2])
                src = bass.AP(wsrc.tensor, gi * mg, [[M, 128], [128 * M, nk], [1, mg]])
                b.op("gpsimd", lambda q, s=src, buf=buf, nk=nk, mg=mg:
                     q.dma_start(out=buf[:, 0:nk * mg], in_=s), "WT", 16)
                b.wait("tensor", "WT")
                for mc in range(mg // 128):
                    m_abs = gi * mg + mc * 128
                    mi = m_abs // 128
                    bank = "A" if mi % 2 == 0 else "B"
                    if len(evac_marks) >= 2:
                        b.wait("tensor", "ACT", evac_marks[-2])
                    for k in range(nk):
                        b.op("tensor", lambda q, k=k, mc=mc, bank=bank, buf=buf, mg=mg:
                             q.matmul(ps[bank][:],
                                      buf[:, k * mg + mc * 128:k * mg + (mc + 1) * 128],
                                      rhs_tv[:, k * rhs_cols:k * rhs_cols + NT],
                                      start=(k == 0), stop=(k == nk - 1)),
                             "PE" if k == nk - 1 else None)
                    b.wait("scalar", "PE")
                    af = {"gelu": AF.Gelu, "silu": AF.Silu}.get(act)
                    if act == "ss":
                        b.wait("scalar", "DVE")
                        b.op("scalar", lambda q, mi=mi, bank=bank: q.activation(
                            S["bf_b"][:, 0:NT], ps[bank][:], AF.Silu,
                            bias=bias_tv[:, mi:mi + 1]), "ACT")
                        b.op("scalar", lambda q, mi=mi, bank=bank: q.activation(
                            S["bf_c"][:, 0:NT], ps[bank][:], AF.Sigmoid,
                            bias=bias_tv[:, mi:mi + 1]), "ACT")
                        b.wait("vector", "ACT")
                        b.op("vector", lambda q, mi=mi: q.tensor_tensor(
                            out_tv[:, mi * NT:(mi + 1) * NT], S["bf_b"][:, 0:NT],
                            S["bf_c"][:, 0:NT], ALU.mult), "DVE")
                    else:
                        b.op("scalar", lambda q, mi=mi, bank=bank, af=af: q.activation(
                            out_tv[:, mi * NT:(mi + 1) * NT], ps[bank][:], af,
                            bias=bias_tv[:, mi:mi + 1]), "ACT")
                    evac_marks.append(b.val("ACT"))
                pe_marks[gi] = b.val("PE")

        def final_layer(wn, K, rhs_tv, rhs_cols, a, tt):
            nk = K // 128
            mg = min(1024, (8192 // (K // 128)) // 128 * 128)
            wsrc = inp[wn]
            b.wait("gpsimd", "PE")
            c0 = tt * NT
            b.wait("gpsimd", "PE")
            b.op("gpsimd", lambda q, a=a, c0=c0: q.dma_start(
                out=S["grow"][:], in_=S["gall"][a:a + 1, c0:c0 + NT]), "WT", 16)
            b.wait("tensor", "ACT")
            b.wait("tensor", "DVE")
            b.wait("tensor", "WT")
            b.op("tensor", lambda q: q.matmul(
                ps["C"][:], C["ones1x128_bf"][:], S["grow"][:],
                start=True, stop=True), "PE")
            b.wait("scalar", "PE")
            b.opf("scalar", lambda q: q.activation(S["gb"][:, 0:NT], ps["C"][:],
                                                   AF.Copy), "ACT")
            pe_marks = {}
            evac_marks = []
            nmg = D // mg
            for gi in range(nmg):
                buf = S["wb0"] if gi % 2 == 0 else S["wb1"]
                if gi >= 2:
                    b.wait("gpsimd", "PE", pe_marks[gi - 2])
                src = bass.AP(wsrc.tensor, gi * mg, [[D, 128], [128 * D, nk], [1, mg]])
                b.op("gpsimd", lambda q, s=src, buf=buf, nk=nk, mg=mg:
                     q.dma_start(out=buf[:, 0:nk * mg], in_=s), "WT", 16)
                b.wait("tensor", "WT")
                for mc in range(mg // 128):
                    m_abs = gi * mg + mc * 128
                    mi = m_abs // 128
                    bank = "A" if mi % 2 == 0 else "B"
                    if len(evac_marks) >= 2:
                        b.wait("tensor", "DVE", evac_marks[-2])
                    for k in range(nk):
                        b.op("tensor", lambda q, k=k, mc=mc, bank=bank, buf=buf, mg=mg,
                             last=(k == nk - 1 and a != 7): q.matmul(
                                 ps[bank][:],
                                 buf[:, k * mg + mc * 128:k * mg + (mc + 1) * 128],
                                 rhs_tv[:, k * rhs_cols:k * rhs_cols + NT],
                                 start=(k == 0), stop=last),
                             "PE" if (k == nk - 1 and a != 7) else None)
                    if a == 7:
                        b.op("tensor", lambda q, mi=mi, bank=bank, c0=c0: q.matmul(
                            ps[bank][:], W["b2s"][0:8, mi * 128:(mi + 1) * 128],
                            S["gall"][0:8, c0:c0 + NT], start=False, stop=True), "PE")
                    b.wait("vector", "PE")
                    b.op("vector", lambda q, bank=bank: q.tensor_tensor(
                        S["t32a"][:], ps[bank][:], S["gb"][:, 0:NT], ALU.mult))
                    b.op("vector", lambda q, mi=mi: q.tensor_tensor(
                        S["acc"][:, mi * NT:(mi + 1) * NT],
                        S["acc"][:, mi * NT:(mi + 1) * NT], S["t32a"][:], ALU.add),
                        "DVE")
                    evac_marks.append(b.val("DVE"))
                pe_marks[gi] = b.val("PE")

        for tt in range(NTT):
            c0 = tt * NT
            b.wait("sync", "PE")
            b.wait("sync", "DVE")
            b.wait("sync", "ST")
            for k in range(NCH):
                src = bass.AP(scr["xs"].tensor, k * 128 * T_CORE + c0,
                              [[T_CORE, 128], [1, NT]])
                b.op("sync", lambda q, s=src, k=k:
                     q.dma_start(out=S["xs"][:, k * NT:(k + 1) * NT], in_=s), "LD", 16)
            b.wait("vector", "LD")
            b.wait("vector", "DVE")
            b.opf("vector", lambda q: q.memset(S["acc"][:], 0.0), "DVE")
            b.wait("tensor", "LD")
            NEXPERTS = int(_os.environ.get("P2_NEXP", "8"))
            for a in range(NEXPERTS):
                layers = E_SPECS[a]
                rhs = S["xs"]
                rhs_cols = NT
                hoff = 0
                for li, (wn, K, M, act) in enumerate(layers[:-1]):
                    out_tv = TV(pool16, 0, 128,
                                S["h"].off + hoff, (M // 128) * NT)
                    stream_layer(wn, K, M, rhs, rhs_cols, out_tv, act, BIA[wn])
                    rhs = out_tv
                    rhs_cols = NT
                    hoff += (M // 128) * NT
                wn, K, M, act = layers[-1]
                final_layer(wn, K, rhs, rhs_cols, a, tt)
            if dbg and tt == 0:
                b.wait("gpsimd", "DVE")
                b.op("gpsimd", lambda q: q.dma_start(out=dbgo["d_acc"], in_=S["acc"][:]), "ST", 16)
                b.wait("vector", "ST")
                b.wait("scalar", "ST")
            b.wait("vector", "DVE")
            for k in range(NCH):
                sl = slice(k * NT, (k + 1) * NT)
                b.op("vector", lambda q, sl=sl: q.tensor_scalar(
                    S["t32a"][:], S["xs"][:, sl], 0.1, None, ALU.mult))
                b.op("vector", lambda q, sl=sl: q.tensor_tensor(
                    S["o2"][:, sl], S["acc"][:, sl], S["t32a"][:], ALU.add),
                    "DVE" if k == NCH - 1 else None)
            if dbg and tt == 0:
                b.wait("gpsimd", "DVE")
                b.op("gpsimd", lambda q: q.dma_start(out=dbgo["d_fin"], in_=S["o2"][:]), "ST", 16)
                b.wait("vector", "ST")
                b.wait("scalar", "ST")
            b.wait("scalar", "DVE")
            b.wait("vector", "DVE")
            post_ln_residual(tt, xb_in, xb_out, S["o2"])


    def phase_barrier():
        for eng in ("gpsimd", "sync"):
            for sem in ("PE", "ACT", "DVE"):
                b.wait(eng, sem)

    load_consts()
    prev = 0
    if 1 in phases:
        with scope():
            attn_phase(xbuf[prev], xbuf[1], "wqkvT", "woutT", "l0")
        prev = 1
    if 2 in phases:
        phase_barrier()
        with scope():
            ffn0_phase(xbuf[prev], xbuf[2])
        prev = 2
    if 3 in phases:
        phase_barrier()
        with scope():
            l1_attn_phase(xbuf[prev], xbuf[3])
        prev = 3
    if 4 in phases:
        phase_barrier()
        with scope():
            ffn1_phase(xbuf[prev], xbuf[4])
        prev = 4
    b.wait("gpsimd", "ST")
    b.wait("gpsimd", "WT")
    b.wait("gpsimd", "LD")

    with es, nc.Block() as block:
        @block.sync
        def _(q):
            for fn in b.ops["sync"]:
                fn(q)

        @block.gpsimd
        def _(q):
            for fn in b.ops["gpsimd"]:
                fn(q)

        @block.tensor
        def _(q):
            for fn in b.ops["tensor"]:
                fn(q)

        @block.vector
        def _(q):
            for fn in b.ops["vector"]:
                fn(q)

        @block.scalar
        def _(q):
            for fn in b.ops["scalar"]:
                fn(q)

    return nc



def _host_prep(inputs, phases=(1, 2, 3, 4)):
    c1, c2 = _rope_tables()
    m0, m1 = _causal_masks()
    ident = np.eye(128, dtype=np.float32)
    x = inputs["L0_attn_x"]
    common = dict(rc1=c1, rc2=c2, mask0=m0, mask1=m1, ident=ident)
    if 2 in phases:
        gi = lambda k: np.ascontiguousarray(np.asarray(inputs["L0_ffn_" + k],
                                                       dtype=np.float32))
        for a in range(3):
            common[f"w_a{a}_l0"] = gi("e0_w1")[a]
            common[f"w_a{a}_l1"] = gi("e0_w2")[a]
            common[f"b_w_a{a}_l0"] = _vprep(gi("e0_b1")[a])
        for i, a in enumerate(range(3, 6)):
            common[f"w_a{a}_l0"] = gi("e1_w1")[i]
            common[f"w_a{a}_l1"] = gi("e1_w2")[i]
            common[f"w_a{a}_l2"] = gi("e1_w3")[i]
            common[f"b_w_a{a}_l0"] = _vprep(gi("e1_b1")[i])
            common[f"b_w_a{a}_l1"] = _vprep(gi("e1_b2")[i])
        for i, a in enumerate(range(6, 8)):
            common[f"w_a{a}_l0"] = gi("e2_w1")[i]
            common[f"w_a{a}_l1"] = gi("e2_w2")[i]
            common[f"b_w_a{a}_l0"] = _vprep(gi("e2_b1")[i])
        common["gate_w1"] = gi("gate_w1")
        common["gate_w2"] = gi("gate_w2")
        common["gate_w3p"] = np.ascontiguousarray(gi("gate_w3")[:, PERM])
        common["aff_wp"] = np.ascontiguousarray(gi("aff_w")[:, PERM])
        common["gate_b1"] = _vprep(gi("gate_b1"))
        common["ln_scale"] = _vprep(gi("ln_scale"))
        common["ln_offset"] = _vprep(gi("ln_offset"))
        common["gate_b2"] = _vprep(gi("gate_b2"))
        common["gate_b3p"] = np.ascontiguousarray(gi("gate_b3")[PERM][:, None])
        common["aff_bp"] = np.ascontiguousarray(gi("aff_b")[PERM][:, None])
        b2s = np.stack([gi("e0_b2")[0], gi("e0_b2")[1], gi("e0_b2")[2],
                        gi("e1_b3")[0], gi("e1_b3")[1], gi("e1_b3")[2],
                        gi("e2_b2")[0], gi("e2_b2")[1]])
        common["b2stack"] = np.ascontiguousarray(b2s)
        common["usage"] = np.ascontiguousarray(
            np.asarray(inputs["L0_ffn_expert_usage"], dtype=np.float32)[None, :])
        gnan = np.zeros((128, 8), np.float32)
        gnan[:, 0] = 0.125
        gnan[:, 3] = 0.125
        common["gnan"] = gnan
        common["_noise"] = _noise_constant()
    if 1 in phases:
        common["wqkvT"] = _prep_qkvT(np.asarray(inputs["L0_attn_Wqkv"]))
        common["woutT"] = np.ascontiguousarray(np.asarray(inputs["L0_attn_Wout"]).T)
    if 3 in phases:
        common["l1_wqkvT"] = np.ascontiguousarray(np.asarray(inputs["L1_attn_w_qkv"],
                                                             dtype=np.float32).T)
        common["l1_woutT"] = np.ascontiguousarray(np.asarray(inputs["L1_attn_w_out"],
                                                             dtype=np.float32).T)
    if 4 in phases:
        g4 = lambda k: np.ascontiguousarray(np.asarray(inputs["L1_ffn_" + k],
                                                       dtype=np.float32))
        common["l1_ln_w"] = _vprep(g4("ln_w"))
        common["l1_ln_b"] = _vprep(g4("ln_b"))
        common["l1_router"] = g4("router_w")
        common["l1_gate"] = g4("gate_w")
        common["l1_up"] = g4("up_w")
        common["l1_down"] = g4("down_w")
        common["l1_shg"] = g4("sh_gate")
        common["l1_shu"] = g4("sh_up")
        common["l1_shd"] = g4("sh_down")
    noise = common.pop("_noise", None)
    in_maps = []
    for c in range(NCORES):
        xc = np.asarray(x[c * B_PER_CORE:(c + 1) * B_PER_CORE])
        x_fm = np.ascontiguousarray(xc.reshape(T_CORE, D).T)
        m = dict(common)
        m["x0"] = x_fm
        if noise is not None:
            nslice = noise[c * B_PER_CORE:(c + 1) * B_PER_CORE]
            npermed = nslice.transpose(2, 0, 1).reshape(8, T_CORE)[PERM]
            m["noise_p"] = np.ascontiguousarray(npermed)
        in_maps.append(m)
    return in_maps


FUSED = False


def kernel(**inputs):
    if FUSED:
        phase_groups = [(1, 2, 3, 4)]
    else:
        phase_groups = [(1,), (2,), (3,), (4,)]
    cur = inputs
    for pg in phase_groups:
        in_maps = _host_prep(cur, pg)
        nc = build_program(phases=pg)
        res = run_bass_kernel_spmd(nc, in_maps, core_ids=list(range(NCORES)))
        outs = []
        for c in range(NCORES):
            o = np.asarray(res.results[c]["xout"])
            outs.append(np.ascontiguousarray(o.T).reshape(B_PER_CORE, T_BATCH, D))
        x = np.concatenate(outs, axis=0).astype(np.float32)
        cur = dict(inputs)
        cur["L0_attn_x"] = x
    return x


# version 03 (final)
# speedup vs baseline: 1.0075x; 1.0075x over previous
import math
from collections import defaultdict
from contextlib import ExitStack

import numpy as np
import ml_dtypes

import concourse.bass as bass
import concourse.mybir as mybir
from concourse.bass_utils import run_bass_kernel_spmd

F32 = mybir.dt.float32
BF16 = mybir.dt.bfloat16
AF = mybir.ActivationFunctionType
ALU = mybir.AluOpType
AX = mybir.AxisListType

NCORES = 8
B_PER_CORE = 2
T_BATCH = 2048
T_CORE = B_PER_CORE * T_BATCH
D = 1024
NCH = D // 128
NT = 512
NTT = T_CORE // NT
HEADS = 16
HD = 64
BLK = 256
NEG = np.float32(-1e38)

PERM = np.argsort(np.array([0, 3, 6, 1, 4, 7, 2, 5]))



def _rope_perm64():
    return np.concatenate([np.arange(0, 64, 2), np.arange(1, 64, 2)])


def _prep_qkvT(wqkv):
    p64 = _rope_perm64()
    newrows = np.arange(3072)
    for c in range(2):
        for h in range(HEADS):
            base = c * 1024 + h * 64
            newrows[base:base + 64] = base + p64
    return np.ascontiguousarray(wqkv[newrows].T)


def _vprep(v):
    v = np.asarray(v)
    return np.ascontiguousarray(v.reshape(-1, 128).T)


def _rope_tables():
    inv_freq = 1.0 / (10000.0 ** (np.arange(0, HD, 2, dtype=np.float64) / HD))
    t = np.arange(T_BATCH, dtype=np.float64)
    fr = inv_freq[:, None] * t[None, :]
    cos = np.cos(fr)
    sin = np.sin(fr)
    c1 = np.empty((128, T_BATCH), np.float64)
    c2 = np.empty((128, T_BATCH), np.float64)
    for half in range(2):
        base = half * 64
        c1[base:base + 32] = cos
        c1[base + 32:base + 64] = cos
        c2[base:base + 32] = -sin
        c2[base + 32:base + 64] = sin
    return c1.astype(np.float32), c2.astype(np.float32)


def _causal_masks():
    j = np.arange(BLK)[None, :]
    i0 = np.arange(128)[:, None]
    m0 = np.where(j <= i0, np.float32(0), NEG).astype(np.float32)
    m1 = np.where(j <= i0 + 128, np.float32(0), NEG).astype(np.float32)
    return m0, m1


def _noise_constant():
    import jax
    with jax.default_device(jax.devices("cpu")[0]):
        noise = np.asarray(jax.random.normal(jax.random.key(1), (16, T_BATCH, 8)),
                           dtype=np.float32)
    return noise



ENGS = ("sync", "gpsimd", "tensor", "vector", "scalar")


class B:
    def __init__(self, nc, sems):
        self.nc = nc
        self.sems = sems
        self.ops = {e: [] for e in ENGS}
        self.cnt = defaultdict(int)

    def op(self, eng, fn, sem=None, n=1):
        if sem is None:
            self.ops[eng].append(lambda q, fn=fn: fn(q))
        else:
            s = self.sems[sem]
            self.ops[eng].append(lambda q, fn=fn, s=s, n=n: fn(q).then_inc(s, n))
            self.cnt[sem] += n

    def opf(self, eng, fn, sem, n=1):
        self.ops[eng].append(lambda q, fn=fn: fn(q))
        s = self.sems[sem]
        self.ops[eng].append(lambda q, s=s, n=n: q.drain().then_inc(s, n))
        self.cnt[sem] += n

    def fence(self, eng):
        self.ops[eng].append(lambda q: q.drain())

    def wait(self, eng, sem, val=None):
        if val is None:
            val = self.cnt[sem]
        if val <= 0:
            return
        s = self.sems[sem]
        self.ops[eng].append(lambda q, s=s, v=val: q.wait_ge(s, v))

    def val(self, sem):
        return self.cnt[sem]



def build_program(phases=(1, 2, 3, 4), dbg=False):
    nc = bass.Bass("TRN2", target_bir_lowering=False)
    es = ExitStack()

    di = lambda nm, shape, dt=F32: nc.dram_tensor(nm, list(shape), dt, kind="ExternalInput").ap()
    do = lambda nm, shape, dt=F32: nc.dram_tensor(nm, list(shape), dt, kind="ExternalOutput").ap()

    inp = {}
    inp["x0"] = di("x0", [D, T_CORE])
    xout = do("xout", [D, T_CORE])

    last = max(phases)
    xbuf = {0: inp["x0"]}
    for p in phases:
        if p == last:
            xbuf[p] = xout
        else:
            xbuf[p] = nc.dram_tensor(f"xb{p}", [D, T_CORE], F32, kind="Internal").ap()

    dbgo = {}
    if dbg:
        for nm, shape in [("d_xln", [128, NCH * NT]), ("d_qkv", [128, 3 * NCH * NT]),
                          ("d_att", [128, NCH * NT]), ("d_o2", [128, NCH * NT]),
                          ("d_vT", [128, 4 * NCH * 128])]:
            dbgo[nm] = do(nm, shape)
        for nm, shape in [("d4_xe", [128, NCH * NT]), ("d4_hs", [128, 16 * NT]),
                          ("d4_hm", [128, 16 * NT]), ("d4_acc0", [128, NCH * NT]),
                          ("d4_xf", [128, NCH * NT]), ("d4_w", [8, T_CORE]),
                          ("d4_acc", [128, NCH * NT]), ("d4_o2", [128, NCH * NT]),
                          ("d4_hid", [128, NCH * NT]),
                          ("d_hi0", [128, 1]), ("d_cnt0", [128, 1]), ("d_dsq0", [128, 1]),
                          ("d_impg", [128, 256]),
                          ("d_vstar", [1, 1]), ("d_temp", [1, 1]), ("d_flag", [1, 1]),
                          ("d_srow", [1, NT]), ("d_imp", [1, NT]), ("d_xs", [128, NCH * NT]),
                          ("d_lg", [8, NT]), ("d_gall", [8, T_CORE]),
                          ("d_fin", [128, NCH * NT]), ("d_acc", [128, NCH * NT])]:
            dbgo[nm] = do(nm, shape)

    inp["rc1"] = di("rc1", [128, T_BATCH])
    inp["rc2"] = di("rc2", [128, T_BATCH])
    inp["mask0"] = di("mask0", [128, BLK])
    inp["mask1"] = di("mask1", [128, BLK])
    inp["ident"] = di("ident", [128, 128])

    if 1 in phases:
        inp["wqkvT"] = di("wqkvT", [D, 3 * D])
        inp["woutT"] = di("woutT", [D, D])
    if 3 in phases:
        inp["l1_wqkvT"] = di("l1_wqkvT", [D, 3 * D])
        inp["l1_woutT"] = di("l1_woutT", [D, D])

    E_SPECS = [
        [("w_a%d_l0" % a, 1024, 4096, "gelu"), ("w_a%d_l1" % a, 4096, 1024, None)]
        for a in range(3)
    ] + [
        [("w_a%d_l0" % a, 1024, 2048, "silu"), ("w_a%d_l1" % a, 2048, 2048, "silu"),
         ("w_a%d_l2" % a, 2048, 1024, None)]
        for a in range(3, 6)
    ] + [
        [("w_a%d_l0" % a, 1024, 3072, "ss"), ("w_a%d_l1" % a, 3072, 1024, None)]
        for a in range(6, 8)
    ]
    if 4 in phases:
        inp["l1_ln_w"] = di("l1_ln_w", [128, 8])
        inp["l1_ln_b"] = di("l1_ln_b", [128, 8])
        inp["l1_router"] = di("l1_router", [D, 8])
        inp["l1_gate"] = di("l1_gate", [8, D, 2 * D])
        inp["l1_up"] = di("l1_up", [8, D, 2 * D])
        inp["l1_down"] = di("l1_down", [8, 2 * D, D])
        inp["l1_shg"] = di("l1_shg", [D, 2 * D])
        inp["l1_shu"] = di("l1_shu", [D, 2 * D])
        inp["l1_shd"] = di("l1_shd", [2 * D, D])
    scr = {}
    if 2 in phases:
        for a in range(8):
            for (wn, K, M, act) in E_SPECS[a]:
                inp[wn] = di(wn, [K, M])
                if act is not None:
                    inp["b_" + wn] = di("b_" + wn, [128, M // 128])
        inp["gate_w1"] = di("gate_w1", [D, D])
        inp["gate_w2"] = di("gate_w2", [D, D // 2])
        inp["gate_w3p"] = di("gate_w3p", [D // 2, 8])
        inp["aff_wp"] = di("aff_wp", [D, 8])
        inp["gate_b1"] = di("gate_b1", [128, 8])
        inp["ln_scale"] = di("ln_scale", [128, 8])
        inp["ln_offset"] = di("ln_offset", [128, 8])
        inp["gate_b2"] = di("gate_b2", [128, 4])
        inp["gate_b3p"] = di("gate_b3p", [8, 1])
        inp["aff_bp"] = di("aff_bp", [8, 1])
        inp["b2stack"] = di("b2stack", [8, D])
        inp["usage"] = di("usage", [1, 8])
        inp["noise_p"] = di("noise_p", [8, T_CORE])
        inp["gnan"] = di("gnan", [128, 8])
        scr["xc"] = nc.dram_tensor("xc_scr", [D, T_CORE], BF16, kind="Internal").ap()
        scr["xs"] = nc.dram_tensor("xs_scr", [D, T_CORE], BF16, kind="Internal").ap()
        scr["r"] = nc.dram_tensor("r_scr", [1, T_CORE], F32, kind="Internal").ap()
        scr["s3"] = nc.dram_tensor("s3_scr", [1, T_CORE], F32, kind="Internal").ap()
        scr["lg"] = nc.dram_tensor("lg_scr", [8, T_CORE], F32, kind="Internal").ap()
        cc_in_t = nc.dram_tensor("cc_in", [16, 256], F32)
        cc_out_t = nc.dram_tensor("cc_out", [128, 256], F32)
        scr["cc_in"] = cc_in_t.ap()
        scr["cc_out"] = cc_out_t.ap()
    if 3 in phases or 4 in phases:
        scr["qf"] = nc.dram_tensor("qf_scr", [D, T_CORE], BF16, kind="Internal").ap()
        scr["hid"] = nc.dram_tensor("hid_scr", [D, T_CORE], BF16, kind="Internal").ap()
        scr["xf"] = nc.dram_tensor("xf_scr", [D, T_CORE], BF16, kind="Internal").ap()

    sems = {}
    for s in ("LD", "WT", "PE", "ACT", "DVE", "ST", "CC", "PL"):
        sems[s] = es.enter_context(nc.semaphore(s))

    b = B(nc, sems)

    sb = lambda nm, shape, dt=F32: es.enter_context(nc.sbuf_tensor(nm, list(shape), dt))

    class TV:
        def __init__(self, t, p0, npart, off, ncols):
            self.t, self.p0, self.np, self.off, self.nc = t, p0, npart, off, ncols

        def _sl(self, s, base, n):
            if isinstance(s, slice):
                a = 0 if s.start is None else s.start
                bnd = n if s.stop is None else s.stop
                assert 0 <= a <= bnd <= n, (a, bnd, n)
                return base + a, base + bnd
            raise TypeError(s)

        def __getitem__(self, idx):
            if idx == slice(None):
                return self.t[self.p0:self.p0 + self.np, self.off:self.off + self.nc]
            pa, pb = self._sl(idx[0], self.p0, self.np)
            ca, cb = self._sl(idx[1], self.off, self.nc)
            return self.t[pa:pb, ca:cb]

    POOL16_COLS = 74112
    POOL32_COLS = 14336
    pool16 = sb("pool16", [128, POOL16_COLS], BF16)
    pool32 = sb("pool32", [128, POOL32_COLS])
    _off16 = [0]
    _off32 = [0]

    def carve16(ncols, npart=128, p0=0):
        off = _off16[0]; _off16[0] += ncols
        assert _off16[0] <= POOL16_COLS, ("pool16 overflow", _off16[0])
        return TV(pool16, p0, npart, off, ncols)

    def carve32(ncols, npart=128, p0=0):
        off = _off32[0]; _off32[0] += ncols
        assert _off32[0] <= POOL32_COLS, ("pool32 overflow", _off32[0])
        return TV(pool32, p0, npart, off, ncols)

    from contextlib import contextmanager

    @contextmanager
    def scope():
        m16, m32 = _off16[0], _off32[0]
        yield
        _off16[0], _off32[0] = m16, m32

    S = {}
    S["x"] = carve32(NCH * NT)
    S["o2"] = carve32(NCH * NT)
    S["t32a"] = carve32(NT)
    S["bf_a"] = carve16(NCH * NT)
    S["bf_b"] = carve16(NCH * NT)
    S["bf_c"] = carve16(NCH * NT)

    R = {}
    for nm in ("mu", "r", "s2", "tmp1", "tmp2", "imp", "s3"):
        R[nm] = carve32(NT, npart=1)
    R["r_bf"] = carve16(NT, npart=1)
    R["mu_bf"] = carve16(NT, npart=1)

    C = {}
    C["ones128_bf"] = sb("ones128_bf", [128, 1], BF16)
    C["ones1x128_bf"] = sb("ones1x128_bf", [1, 128], BF16)
    C["ones1x128_f"] = sb("ones1x128_f", [1, 128])
    C["ones128_f"] = sb("ones128_f", [128, 1])
    C["negmax"] = sb("negmax", [128, 1])
    C["z"] = sb("z", [128, 1])
    C["rz"] = sb("rz", [128, 1])
    C["eps"] = sb("c_eps", [1, 1])

    W = {}
    W["ident"] = carve16(128)
    W["identf"] = carve32(128)

    ps = {}
    for nm in ("A", "B", "C", "D", "E", "F", "G"):
        ps[nm] = es.enter_context(nc.psum_tensor("ps" + nm, [128, NT], F32))
    ps["T"] = es.enter_context(nc.psum_tensor("psT", [128, 1024], BF16))


    def fm_slice(ap_, k, c0, n=NT):
        return bass.AP(ap_.tensor, k * 128 * T_CORE + c0, [[T_CORE, 128], [1, n]])

    def load_consts():
        b.op("vector", lambda q: q.memset(C["ones128_bf"][:], 1.0))
        b.op("vector", lambda q: q.memset(C["ones1x128_bf"][:], 1.0))
        b.op("vector", lambda q: q.memset(C["ones1x128_f"][:], 1.0))
        b.op("vector", lambda q: q.memset(C["ones128_f"][:], 1.0))
        b.opf("vector", lambda q: q.memset(C["eps"][:], 1e-5), "DVE")
        b.op("gpsimd", lambda q: q.dma_start(out=W["ident"][:], in_=inp["ident"]), "WT", 16)
        b.op("sync", lambda q: q.dma_start(out=W["identf"][:], in_=inp["ident"]), "LD", 16)

    def load_attn_weights(wq_name, wo_name, wq_tv, wo_tv):
        for k in range(NCH):
            src = bass.AP(inp[wq_name].tensor, k * 128 * (3 * D),
                          [[3 * D, 128], [1, 3 * D]])
            b.op("gpsimd", lambda q, s=src, d=wq_tv[:, k * 3 * D:(k + 1) * 3 * D]:
                 q.dma_start(out=d, in_=s), "WT", 16)
        for k in range(NCH):
            src = bass.AP(inp[wo_name].tensor, k * 128 * D, [[D, 128], [1, D]])
            b.op("gpsimd", lambda q, s=src, d=wo_tv[:, k * D:(k + 1) * D]:
                 q.dma_start(out=d, in_=s), "WT", 16)

    def stats_rows(bf_val, bf_sq, tt):
        for k in range(NCH):
            b.op("tensor", lambda q, k=k: q.matmul(
                ps["F"][0:1, :], C["ones128_bf"][:], bf_val[:, k * NT:(k + 1) * NT],
                start=(k == 0), stop=(k == NCH - 1)))
        for k in range(NCH):
            b.op("tensor", lambda q, k=k: q.matmul(
                ps["G"][0:1, :], C["ones128_bf"][:], bf_sq[:, k * NT:(k + 1) * NT],
                start=(k == 0), stop=(k == NCH - 1)),
                "PE" if k == NCH - 1 else None)
        b.wait("vector", "PE")
        mu = R["mu"][:]
        e2 = R["tmp1"][:]
        musq = R["tmp2"][:]
        var = R["s2"][:]
        b.op("vector", lambda q, mu=mu: q.tensor_scalar(mu, ps["F"][0:1, :], 1.0 / D,
                                                        None, ALU.mult))
        b.op("vector", lambda q, e2=e2: q.tensor_scalar(e2, ps["G"][0:1, :], 1.0 / D,
                                                        None, ALU.mult))
        b.fence("vector")
        b.op("vector", lambda q, musq=musq, mu=mu: q.tensor_tensor(musq, mu, mu, ALU.mult))
        b.opf("vector", lambda q, var=var, e2=e2, musq=musq:
              q.tensor_tensor(var, e2, musq, ALU.subtract), "DVE")
        b.wait("scalar", "DVE")
        sd = R["tmp1"][:]
        b.opf("scalar", lambda q, sd=sd, var=var:
              q.activation(sd, var, AF.Sqrt, bias=C["eps"][:]), "ACT")
        b.wait("vector", "ACT")
        rr = R["r"][:]
        b.op("vector", lambda q, rr=rr, sd=sd: q.reciprocal(rr, sd))
        b.fence("vector")
        b.op("vector", lambda q, rr=rr: q.tensor_copy(R["r_bf"][:], rr))
        b.opf("vector", lambda q, mu=mu:
              q.tensor_copy(R["mu_bf"][:], mu), "DVE")

    def bcast_row(row_bf_slice, bank):
        b.op("tensor", lambda q, r=row_bf_slice, p=ps[bank]:
             q.matmul(p[:], C["ones1x128_bf"][:], r, start=True, stop=True), "PE")

    def load_x_tile(tt, xb_ap):
        c0 = tt * NT
        b.wait("sync", "ST")
        for k in range(NCH):
            b.op("sync", lambda q, s=fm_slice(xb_ap, k, c0), k=k:
                 q.dma_start(out=S["x"][:, k * NT:(k + 1) * NT], in_=s), "LD", 16)

    def ln_of_x_tile(tt, out_bf):
        c0 = tt * NT
        b.wait("scalar", "LD")
        b.wait("vector", "LD")
        for k in range(NCH):
            sl = slice(k * NT, (k + 1) * NT)
            b.op("scalar", lambda q, sl=sl: q.activation(
                S["bf_b"][:, sl], S["x"][:, sl], AF.Square), "ACT")
            b.op("vector", lambda q, sl=sl:
                 q.tensor_copy(S["bf_a"][:, sl], S["x"][:, sl]), "DVE")
        b.wait("tensor", "ACT")
        b.wait("tensor", "DVE")
        stats_rows(S["bf_a"], S["bf_b"], tt)
        b.wait("tensor", "DVE")
        bcast_row(R["mu_bf"][:], "C")
        bcast_row(R["r_bf"][:], "D")
        b.wait("vector", "PE")
        for k in range(NCH):
            sl = slice(k * NT, (k + 1) * NT)
            b.op("vector", lambda q, sl=sl: q.tensor_tensor(
                S["t32a"][:], S["x"][:, sl], ps["C"][:], ALU.subtract))
            b.op("vector", lambda q, sl=sl, ob=out_bf: q.tensor_tensor(
                ob[:, sl], S["t32a"][:], ps["D"][:], ALU.mult), "DVE")

    def post_ln_residual(tt, xb_in_ap, xb_out_ap, branch32):
        c0 = tt * NT
        for k in range(NCH):
            sl = slice(k * NT, (k + 1) * NT)
            b.op("scalar", lambda q, sl=sl, br=branch32: q.activation(
                S["bf_b"][:, sl], br[:, sl], AF.Square), "ACT")
            b.op("vector", lambda q, sl=sl, br=branch32:
                 q.tensor_copy(S["bf_a"][:, sl], br[:, sl]), "DVE")
        load_x_tile(tt, xb_in_ap)
        b.wait("tensor", "ACT")
        b.wait("tensor", "DVE")
        stats_rows(S["bf_a"], S["bf_b"], tt)
        b.wait("tensor", "DVE")
        bcast_row(R["mu_bf"][:], "C")
        bcast_row(R["r_bf"][:], "D")
        b.wait("vector", "PE")
        b.wait("vector", "LD")
        for k in range(NCH):
            sl = slice(k * NT, (k + 1) * NT)
            b.op("vector", lambda q, sl=sl, br=branch32: q.tensor_tensor(
                S["t32a"][:], br[:, sl], ps["C"][:], ALU.subtract))
            b.op("vector", lambda q, sl=sl: q.tensor_tensor(
                S["t32a"][:], S["t32a"][:], ps["D"][:], ALU.mult))
            b.op("vector", lambda q, sl=sl: q.tensor_tensor(
                S["x"][:, sl], S["x"][:, sl], S["t32a"][:], ALU.add), "DVE")
        b.wait("sync", "DVE")
        for k in range(NCH):
            b.op("sync", lambda q, d=fm_slice(xb_out_ap, k, c0), k=k:
                 q.dma_start(out=d, in_=S["x"][:, k * NT:(k + 1) * NT]), "ST", 16)

    def mm_sweep_evac(nm_chunks, lhsT_fn, rhs_fn, evac_fn, nk=NCH):
        marks = []
        for m in range(nm_chunks):
            bank = "A" if m % 2 == 0 else "B"
            if m >= 2:
                b.wait("tensor", "ACT", marks[m - 2])
            for k in range(nk):
                b.op("tensor", lambda q, m=m, k=k, bank=bank: q.matmul(
                    ps[bank][:], lhsT_fn(m, k), rhs_fn(m, k),
                    start=(k == 0), stop=(k == nk - 1)),
                    "PE" if k == nk - 1 else None)
            b.wait("scalar", "PE")
            evac_fn(m, bank)
            marks.append(b.val("ACT"))


    DBG_TT = 0

    def dump(nm, tv):
        if not dbg or nm not in dbgo:
            return
        b.wait("gpsimd", "DVE")
        b.wait("gpsimd", "ACT")
        b.op("gpsimd", lambda q, nm=nm, tv=tv: q.dma_start(out=dbgo[nm], in_=tv[:]),
             "ST", 16)
        b.wait("vector", "ST")
        b.wait("scalar", "ST")

    def attn_phase(xb_in, xb_out, wq_name, wo_name, kind, S=S, W=W):
        S = dict(S)
        W = dict(W)
        S["qkv"] = carve16(3 * NCH * NT)
        S["qsw"] = carve16(NCH * NT)
        S["vT"] = carve16(4 * NCH * 128)
        S["attb"] = carve16(BLK)
        S["attT"] = carve16(BLK)
        W["qkv"] = carve16(NCH * 3 * D)
        W["out"] = carve16(NCH * D)
        W["rc1"] = carve16(T_BATCH)
        W["rc2"] = carve16(T_BATCH)
        W["mask0"] = carve32(BLK)
        W["mask1"] = carve32(BLK)
        b.op("gpsimd", lambda q, d=W["rc1"]: q.dma_start(out=d[:], in_=inp["rc1"]), "WT", 16)
        b.op("gpsimd", lambda q, d=W["rc2"]: q.dma_start(out=d[:], in_=inp["rc2"]), "WT", 16)
        b.op("sync", lambda q, d=W["mask0"]: q.dma_start(out=d[:], in_=inp["mask0"]), "LD", 16)
        b.op("sync", lambda q, d=W["mask1"]: q.dma_start(out=d[:], in_=inp["mask1"]), "LD", 16)
        load_attn_weights(wq_name, wo_name, W["qkv"], W["out"])

        def l0_attention_core(tt):
            return l0_attention_core_impl(tt, S, W)

        for tt in range(NTT):
            p0 = (tt % (T_BATCH // NT)) * NT
            load_x_tile(tt, xb_in)
            ln_of_x_tile(tt, S["bf_c"])
            if tt == DBG_TT:
                dump("d_xln", S["bf_c"])
            b.wait("tensor", "WT")
            b.wait("tensor", "DVE")
            mm_sweep_evac(
                3 * NCH,
                lambda m, k: W["qkv"][:, k * 3 * D + m * 128: k * 3 * D + (m + 1) * 128],
                lambda m, k: S["bf_c"][:, k * NT:(k + 1) * NT],
                lambda m, bank: b.op("scalar", lambda q, m=m, bank=bank: q.activation(
                    S["qkv"][:, m * NT:(m + 1) * NT], ps[bank][:], AF.Copy), "ACT"))
            b.wait("gpsimd", "ACT")
            for half in range(2):
                hoff = half * NCH * NT
                if half == 1:
                    b.wait("gpsimd", "DVE")
                for blk_i in range(4):
                    srcb = blk_i ^ 1
                    b.op("gpsimd", lambda q, d=blk_i, s=srcb, hoff=hoff: q.dma_start(
                        out=S["qsw"][d * 32:(d + 1) * 32, :],
                        in_=S["qkv"][s * 32:(s + 1) * 32, hoff:hoff + NCH * NT]),
                        "WT", 16)
                b.wait("vector", "WT")
                for m in range(NCH):
                    sl = slice(hoff + m * NT, hoff + (m + 1) * NT)
                    sw = slice(m * NT, (m + 1) * NT)
                    csl = slice(p0, p0 + NT)
                    b.op("vector", lambda q, sl=sl, csl=csl: q.tensor_tensor(
                        S["bf_a"][:, 0:NT], S["qkv"][:, sl], W["rc1"][:, csl], ALU.mult))
                    b.op("vector", lambda q, sw=sw, csl=csl: q.tensor_tensor(
                        S["bf_b"][:, 0:NT], S["qsw"][:, sw], W["rc2"][:, csl], ALU.mult))
                    b.op("vector", lambda q, sl=sl: q.tensor_tensor(
                        S["qkv"][:, sl], S["bf_a"][:, 0:NT], S["bf_b"][:, 0:NT],
                        ALU.add), "DVE")
            if tt == DBG_TT:
                b.wait("gpsimd", "DVE")
                dump("d_qkv", S["qkv"])
            if kind == "l0":
                l0_attention_core(tt)
            else:
                l1_attention_core(tt)
            if tt == DBG_TT:
                dump("d_att", S["bf_c"])
                dump("d_vT", S["vT"])
            mm_sweep_evac(
                NCH,
                lambda m, k: W["out"][:, k * D + m * 128: k * D + (m + 1) * 128],
                lambda m, k: S["bf_c"][:, k * NT:(k + 1) * NT],
                lambda m, bank: b.op("scalar", lambda q, m=m, bank=bank: q.activation(
                    S["o2"][:, m * NT:(m + 1) * NT], ps[bank][:], AF.Copy), "ACT"))
            b.wait("vector", "ACT")
            if tt == DBG_TT:
                b.wait("gpsimd", "ACT")
                dump("d_o2", S["o2"])
            post_ln_residual(tt, xb_in, xb_out, S["o2"])

    def l0_attention_core_impl(tt, S, W):
        b.wait("tensor", "DVE")
        for ch in range(NCH):
            for st in range(4):
                slot = st % 2
                b.wait("tensor", "DVE")
                b.op("tensor", lambda q, ch=ch, st=st, slot=slot: q.transpose(
                    ps["T"][:, slot * 128:(slot + 1) * 128],
                    S["qkv"][:, (2 * NCH + ch) * NT + st * 128:
                             (2 * NCH + ch) * NT + (st + 1) * 128],
                    W["ident"][:]), "PE")
                b.wait("vector", "PE")
                dst = S["vT"][:, (st * NCH + ch) * 128:(st * NCH + ch + 1) * 128]
                b.op("vector", lambda q, d=dst, slot=slot: q.tensor_copy(
                    d, ps["T"][:, slot * 128:(slot + 1) * 128]), "DVE")
        scale = float(HD ** -0.5)
        for ch in range(NCH):
            for sub in range(2):
                pbase = 64 * sub
                for blk_i in range(2):
                    for qt in range(2):
                        qcol = ch * NT + blk_i * BLK + qt * 128
                        kcol = (NCH + ch) * NT + blk_i * BLK
                        sbank = "A" if (blk_i * 2 + qt) % 2 == 0 else "B"
                        b.wait("tensor", "DVE")
                        b.op("tensor", lambda q, qcol=qcol, kcol=kcol, sbank=sbank,
                             pbase=pbase: q.matmul(
                                 ps[sbank][:, 0:BLK],
                                 S["qkv"][pbase:pbase + 64, qcol:qcol + 128],
                                 S["qkv"][pbase:pbase + 64, kcol:kcol + BLK],
                                 start=True, stop=True), "PE")
                        b.wait("vector", "PE")
                        mask = W["mask0"] if qt == 0 else W["mask1"]
                        b.op("vector", lambda q, sbank=sbank, mask=mask:
                             q.tensor_tensor(ps[sbank][:, 0:BLK], ps[sbank][:, 0:BLK],
                                             mask[:], ALU.add))
                        b.op("vector", lambda q, sbank=sbank: q.tensor_reduce(
                            C["negmax"][:], ps[sbank][:, 0:BLK], AX.X, ALU.max,
                            negate=True))
                        b.fence("vector")
                        b.opf("vector", lambda q: q.tensor_scalar(
                            C["negmax"][:], C["negmax"][:], scale, None, ALU.mult),
                            "DVE")
                        b.wait("scalar", "DVE")
                        b.opf("scalar", lambda q, sbank=sbank: q.activation(
                            S["attb"][:], ps[sbank][:, 0:BLK], AF.Exp,
                            bias=C["negmax"][:], scale=scale,
                            accum_out=C["z"][:]), "ACT")
                        b.wait("vector", "ACT")
                        b.op("vector", lambda q: q.reciprocal(C["rz"][:], C["z"][:]))
                        b.fence("vector")
                        b.opf("vector", lambda q: q.tensor_scalar(
                            S["attb"][:], S["attb"][:], C["rz"][:], None, ALU.mult),
                            "DVE")
                        b.wait("tensor", "DVE")
                        for jt in range(2):
                            b.op("tensor", lambda q, jt=jt: q.transpose(
                                ps["T"][:, jt * 128:(jt + 1) * 128],
                                S["attb"][:, jt * 128:(jt + 1) * 128],
                                W["ident"][:]), "PE" if jt == 1 else None)
                        b.wait("vector", "PE")
                        for jt in range(2):
                            b.op("vector", lambda q, jt=jt: q.tensor_copy(
                                S["attT"][:, jt * 128:(jt + 1) * 128],
                                ps["T"][:, jt * 128:(jt + 1) * 128]),
                                "DVE" if jt == 1 else None)
                        b.wait("tensor", "DVE")
                        ocol = blk_i * BLK + qt * 128
                        for jt in range(2):
                            vcol = ((blk_i * 2 + jt) * NCH + ch) * 128 + pbase
                            b.op("tensor", lambda q, vcol=vcol, jt=jt, ocol=ocol,
                                 pbase=pbase: q.matmul(
                                     ps["E"][pbase:pbase + 64, ocol:ocol + 128],
                                     S["vT"][:, vcol:vcol + 64],
                                     S["attT"][:, jt * 128:(jt + 1) * 128],
                                     start=(jt == 0), stop=(jt == 1)),
                                 "PE" if jt == 1 else None)
            b.wait("scalar", "PE")
            b.op("scalar", lambda q, ch=ch: q.activation(
                S["bf_c"][:, ch * NT:(ch + 1) * NT], ps["E"][:], AF.Copy), "ACT")
            b.wait("tensor", "ACT")


    def l1_attn_phase(xb_in, xb_out, S=S, W=W):
        S = dict(S)
        W = dict(W)
        S["qkv"] = carve16(3 * NCH * NT)
        S["vT"] = carve16(4 * NCH * 128)
        S["kT"] = carve16(4 * NCH * 128)
        S["qsw"] = S["kT"]
        S["kvbf"] = carve16(8 * 72)
        S["ksumbf"] = carve16(8)
        W["qkv"] = carve16(NCH * 3 * D)
        W["out"] = carve16(NCH * D)
        W["rc1"] = carve16(T_BATCH)
        W["rc2"] = carve16(T_BATCH)
        KV = carve32(8 * 72)
        KS = carve32(8)
        DN = carve32(NT, npart=1)
        b.op("gpsimd", lambda q, d=W["rc1"]: q.dma_start(out=d[:], in_=inp["rc1"]), "WT", 16)
        b.op("gpsimd", lambda q, d=W["rc2"]: q.dma_start(out=d[:], in_=inp["rc2"]), "WT", 16)
        load_attn_weights("l1_wqkvT", "l1_woutT", W["qkv"], W["out"])
        scale = float(HD ** -0.5)
        TPB = T_BATCH // NT
        for bb in range(B_PER_CORE):
            b.wait("vector", "PE")
            b.op("vector", lambda q: q.memset(KV[:], 0.0))
            b.opf("vector", lambda q: q.memset(KS[:], 0.0), "DVE")
            for ti in range(TPB):
                tt = bb * TPB + ti
                p0 = ti * NT
                load_x_tile(tt, xb_in)
                ln_of_x_tile(tt, S["bf_c"])
                b.wait("tensor", "WT")
                b.wait("tensor", "DVE")
                mm_sweep_evac(
                    3 * NCH,
                    lambda m, k: W["qkv"][:, k * 3 * D + m * 128: k * 3 * D + (m + 1) * 128],
                    lambda m, k: S["bf_c"][:, k * NT:(k + 1) * NT],
                    lambda m, bank: b.op("scalar", lambda q, m=m, bank=bank: q.activation(
                        S["qkv"][:, m * NT:(m + 1) * NT], ps[bank][:], AF.Copy), "ACT"))
                b.wait("gpsimd", "ACT")
                for half in range(2):
                    hoff = half * NCH * NT
                    if half == 1:
                        b.wait("gpsimd", "DVE")
                    for blk_i in range(4):
                        srcb = blk_i ^ 1
                        b.op("gpsimd", lambda q, d=blk_i, sgm=srcb, hoff=hoff: q.dma_start(
                            out=S["qsw"][d * 32:(d + 1) * 32, :],
                            in_=S["qkv"][sgm * 32:(sgm + 1) * 32, hoff:hoff + NCH * NT]),
                            "WT", 16)
                    b.wait("vector", "WT")
                    for m in range(NCH):
                        sl = slice(hoff + m * NT, hoff + (m + 1) * NT)
                        sw = slice(m * NT, (m + 1) * NT)
                        csl = slice(p0, p0 + NT)
                        b.op("vector", lambda q, sl=sl, csl=csl: q.tensor_tensor(
                            S["bf_a"][:, 0:NT], S["qkv"][:, sl], W["rc1"][:, csl],
                            ALU.mult))
                        b.op("vector", lambda q, sw=sw, csl=csl: q.tensor_tensor(
                            S["bf_b"][:, 0:NT], S["qsw"][:, sw], W["rc2"][:, csl],
                            ALU.mult))
                        b.op("vector", lambda q, sl=sl: q.tensor_tensor(
                            S["qkv"][:, sl], S["bf_a"][:, 0:NT], S["bf_b"][:, 0:NT],
                            ALU.add), "DVE")
                for half, sc in ((0, scale), (1, 1.0)):
                    ho = half * NCH * NT
                    reg = slice(ho, ho + NCH * NT)
                    b.wait("vector", "ACT")
                    b.op("vector", lambda q, reg=reg, sc=sc: q.tensor_scalar(
                        S["bf_a"][:], S["qkv"][:, reg], sc, 0.0, ALU.mult, ALU.min))
                    b.opf("vector", lambda q, reg=reg, sc=sc: q.tensor_scalar(
                        S["bf_b"][:], S["qkv"][:, reg], sc, 0.0, ALU.mult, ALU.max),
                        "DVE")
                    b.wait("scalar", "DVE")
                    b.opf("scalar", lambda q: q.activation(
                        S["bf_c"][:], S["bf_a"][:], AF.Exp), "ACT")
                    b.wait("vector", "ACT")
                    b.opf("vector", lambda q, reg=reg: q.tensor_tensor(
                        S["qkv"][:, reg], S["bf_c"][:], S["bf_b"][:], ALU.add), "DVE")
                b.wait("sync", "DVE")
                c0 = tt * NT
                for k in range(NCH):
                    dst = bass.AP(scr["qf"].tensor, k * 128 * T_CORE + c0,
                                  [[T_CORE, 128], [1, NT]])
                    b.op("sync", lambda q, d=dst, k=k:
                         q.dma_start(out=d, in_=S["qkv"][:, k * NT:(k + 1) * NT]),
                         "ST", 16)
                for k in range(NCH):
                    sl = slice(NCH * NT + k * NT, NCH * NT + (k + 1) * NT)
                    b.opf("vector", lambda q, sl=sl: q.tensor_scalar(
                        S["t32a"][:], S["qkv"][:, sl], 1.0, 0.0, ALU.mult, ALU.add,
                        accum_out=C["negmax"][:]), "DVE")
                    b.op("vector", lambda q, k=k: q.tensor_tensor(
                        KS[:, k:k + 1], KS[:, k:k + 1], C["negmax"][:], ALU.add),
                        "DVE")
                b.wait("tensor", "DVE")
                for src_half, dst_tv in ((1, S["kT"]), (2, S["vT"])):
                    for ch in range(NCH):
                        for st in range(4):
                            slot = st % 2
                            b.wait("tensor", "DVE")
                            b.op("tensor", lambda q, ch=ch, st=st, slot=slot,
                                 src_half=src_half: q.transpose(
                                     ps["T"][:, slot * 128:(slot + 1) * 128],
                                     S["qkv"][:, (src_half * NCH + ch) * NT + st * 128:
                                              (src_half * NCH + ch) * NT + (st + 1) * 128],
                                     W["ident"][:]), "PE")
                            b.wait("vector", "PE")
                            dst = dst_tv[:, (st * NCH + ch) * 128:(st * NCH + ch + 1) * 128]
                            b.op("vector", lambda q, d=dst, slot=slot: q.tensor_copy(
                                d, ps["T"][:, slot * 128:(slot + 1) * 128]), "DVE")
                for ch in range(NCH):
                    b.wait("tensor", "DVE")
                    for sub in range(2):
                        pbase = 64 * sub
                        for st in range(4):
                            col = (st * NCH + ch) * 128 + pbase
                            b.op("tensor", lambda q, col=col, st=st, pbase=pbase:
                                 q.matmul(ps["E"][pbase:pbase + 64, 0:64],
                                          S["kT"][:, col:col + 64],
                                          S["vT"][:, col:col + 64],
                                          start=(st == 0), stop=(st == 3)),
                                 "PE" if st == 3 else None)
                    b.wait("vector", "PE")
                    b.op("vector", lambda q, ch=ch: q.tensor_tensor(
                        KV[:, ch * 72:ch * 72 + 64], KV[:, ch * 72:ch * 72 + 64],
                        ps["E"][:, 0:64], ALU.add), "DVE")
                    b.wait("tensor", "DVE")
            b.op("vector", lambda q: q.tensor_copy(S["kvbf"][:], KV[:]))
            b.opf("vector", lambda q: q.tensor_copy(S["ksumbf"][:], KS[:]), "DVE")
            for ti in range(TPB):
                tt = bb * TPB + ti
                c0 = tt * NT
                b.wait("sync", "PE")
                b.wait("sync", "DVE")
                b.wait("sync", "ST")
                for k in range(NCH):
                    src = bass.AP(scr["qf"].tensor, k * 128 * T_CORE + c0,
                                  [[T_CORE, 128], [1, NT]])
                    b.op("sync", lambda q, sgm=src, k=k:
                         q.dma_start(out=S["qkv"][:, k * NT:(k + 1) * NT], in_=sgm),
                         "LD", 16)
                b.wait("tensor", "LD")
                b.wait("tensor", "DVE")
                b.wait("tensor", "ACT")
                for ch in range(NCH):
                    for sub in range(2):
                        pbase = 64 * sub
                        b.op("tensor", lambda q, ch=ch, pbase=pbase: q.matmul(
                            ps["F"][0:1, :],
                            S["ksumbf"][pbase:pbase + 64, ch:ch + 1],
                            S["qkv"][pbase:pbase + 64, ch * NT:(ch + 1) * NT],
                            start=True, stop=True), "PE")
                        b.wait("vector", "PE")
                        b.op("vector", lambda q: q.tensor_scalar(
                            DN[:], ps["F"][0:1, :], 1e-6, None, ALU.max))
                        b.fence("vector")
                        b.opf("vector", lambda q: q.reciprocal(R["tmp1"][:], DN[:]),
                              "DVE")
                        b.wait("tensor", "DVE")
                        b.op("tensor", lambda q, pbase=pbase: q.matmul(
                            ps["D"][pbase:pbase + 64, :],
                            C["ones1x128_f"][0:1, 0:64], R["tmp1"][:],
                            start=True, stop=True))
                        b.op("tensor", lambda q, ch=ch, pbase=pbase: q.matmul(
                            ps["A"][pbase:pbase + 64, :],
                            S["kvbf"][pbase:pbase + 64, ch * 72:ch * 72 + 64],
                            S["qkv"][pbase:pbase + 64, ch * NT:(ch + 1) * NT],
                            start=True, stop=True), "PE")
                    b.wait("scalar", "PE")
                    b.opf("scalar", lambda q: q.activation(
                        S["bf_a"][:, 0:NT], ps["A"][:], AF.Copy), "ACT")
                    b.wait("vector", "ACT")
                    b.op("vector", lambda q, ch=ch: q.tensor_tensor(
                        S["bf_c"][:, ch * NT:(ch + 1) * NT], S["bf_a"][:, 0:NT],
                        ps["D"][:], ALU.mult), "DVE")
                    b.wait("tensor", "DVE")
                b.wait("tensor", "DVE")
                mm_sweep_evac(
                    NCH,
                    lambda m, k: W["out"][:, k * D + m * 128: k * D + (m + 1) * 128],
                    lambda m, k: S["bf_c"][:, k * NT:(k + 1) * NT],
                    lambda m, bank: b.op("scalar", lambda q, m=m, bank=bank:
                                         q.activation(S["o2"][:, m * NT:(m + 1) * NT],
                                                      ps[bank][:], AF.Copy), "ACT"))
                b.wait("vector", "ACT")
                post_ln_residual(tt, xb_in, xb_out, S["o2"])


    def ffn1_phase(xb_in, xb_out, S=S, W=W):
        S = dict(S)
        W = dict(W)
        S["xf"] = carve16(NCH * NT)
        S["h"] = carve16(32 * NT)
        S["wb0"] = carve16(8192)
        S["wb1"] = carve16(8192)
        S["gall"] = carve16(T_CORE, npart=8)
        S["grow"] = carve16(NT, npart=1)
        S["acc"] = carve16(NCH * NT)
        S["gfin"] = carve16(8)
        W["router"] = carve16(NCH * 8)
        V = {}
        V["lgt"] = carve32(8)
        V["lg"] = carve32(NT, npart=8)
        B4 = {}
        B4["lnw"] = carve32(8)
        B4["lnb"] = carve32(8)
        T4 = {}
        T4["max1"] = sb("c4_max1", [128, 1])
        T4["score"] = sb("c4_score", [128, 1])
        for k in range(NCH):
            src = bass.AP(inp["l1_router"].tensor, k * 128 * 8, [[8, 128], [1, 8]])
            b.op("gpsimd", lambda q, sgm=src, k=k:
                 q.dma_start(out=W["router"][:, k * 8:(k + 1) * 8], in_=sgm), "WT", 16)
        b.op("sync", lambda q: q.dma_start(out=B4["lnw"][:], in_=inp["l1_ln_w"]), "LD", 16)
        b.op("sync", lambda q: q.dma_start(out=B4["lnb"][:], in_=inp["l1_ln_b"]), "LD", 16)

        for tt in range(NTT):
            c0 = tt * NT
            load_x_tile(tt, xb_in)
            ln_of_x_tile(tt, S["bf_c"])
            b.wait("sync", "DVE")
            for k in range(NCH):
                dst = bass.AP(scr["hid"].tensor, k * 128 * T_CORE + c0,
                              [[T_CORE, 128], [1, NT]])
                b.op("sync", lambda q, d=dst, k=k:
                     q.dma_start(out=d, in_=S["bf_c"][:, k * NT:(k + 1) * NT]), "ST", 16)
            b.wait("scalar", "DVE")
            for k in range(NCH):
                sl = slice(k * NT, (k + 1) * NT)
                b.op("scalar", lambda q, sl=sl: q.activation(
                    S["bf_b"][:, sl], S["bf_c"][:, sl], AF.Square), "ACT")
            b.wait("tensor", "ACT")
            b.wait("tensor", "DVE")
            stats_rows(S["bf_c"], S["bf_b"], tt)
            b.wait("tensor", "DVE")
            bcast_row(R["mu_bf"][:], "C")
            bcast_row(R["r_bf"][:], "D")
            b.wait("vector", "PE")
            for k in range(NCH):
                sl = slice(k * NT, (k + 1) * NT)
                b.op("vector", lambda q, sl=sl: q.tensor_tensor(
                    S["t32a"][:], S["bf_c"][:, sl], ps["C"][:], ALU.subtract))
                b.op("vector", lambda q, sl=sl: q.tensor_tensor(
                    S["t32a"][:], S["t32a"][:], ps["D"][:], ALU.mult))
                b.op("vector", lambda q, sl=sl, k=k: q.tensor_scalar(
                    S["xf"][:, sl], S["t32a"][:], B4["lnw"][:, k:k + 1],
                    B4["lnb"][:, k:k + 1], ALU.mult, ALU.add),
                    "DVE" if k == NCH - 1 else None)
            b.wait("sync", "DVE")
            for k in range(NCH):
                dst = bass.AP(scr["xf"].tensor, k * 128 * T_CORE + c0,
                              [[T_CORE, 128], [1, NT]])
                b.op("sync", lambda q, d=dst, k=k:
                     q.dma_start(out=d, in_=S["xf"][:, k * NT:(k + 1) * NT]), "ST", 16)
            if dbg and tt == 0:
                b.wait("gpsimd", "DVE")
                b.op("gpsimd", lambda q: q.dma_start(out=dbgo["d4_xf"], in_=S["xf"][:]), "ST", 16)
                b.op("gpsimd", lambda q: q.dma_start(out=dbgo["d4_hid"], in_=S["bf_c"][:]), "ST", 16)
                b.wait("vector", "ST")
                b.wait("scalar", "ST")
            b.wait("tensor", "WT")
            b.wait("tensor", "DVE")
            b.wait("tensor", "ACT")
            for k in range(NCH):
                b.op("tensor", lambda q, k=k: q.matmul(
                    ps["E"][0:8, :], W["router"][:, k * 8:(k + 1) * 8],
                    S["xf"][:, k * NT:(k + 1) * NT],
                    start=(k == 0), stop=(k == NCH - 1)), "PE" if k == NCH - 1 else None)
            b.wait("vector", "PE")
            b.opf("vector", lambda q: q.tensor_copy(V["lg"][:], ps["E"][0:8, :]), "DVE")
            for sub in range(4):
                scol = sub * 128
                b.wait("tensor", "DVE")
                b.op("tensor", lambda q, scol=scol: q.transpose(
                    ps["E"][:, 0:128], V["lg"][0:8, scol:scol + 128],
                    W["identf"][0:8, :]), "PE")
                b.wait("vector", "PE")
                b.op("vector", lambda q: q.tensor_copy(V["lgt"][:], ps["E"][:, 0:8]))
                b.fence("vector")
                b.op("vector", lambda q: q.tensor_reduce(
                    T4["max1"][:], V["lgt"][:], AX.X, ALU.max))
                b.fence("vector")
                b.opf("vector", lambda q: q.tensor_scalar(
                    V["lgt"][:], V["lgt"][:], T4["max1"][:], None, ALU.is_ge), "DVE")
                b.wait("scalar", "DVE")
                b.opf("scalar", lambda q: q.activation(
                    T4["score"][:], T4["max1"][:], AF.Sigmoid), "ACT")
                b.wait("vector", "ACT")
                b.op("vector", lambda q: q.tensor_scalar(
                    V["lgt"][:], V["lgt"][:], T4["score"][:], None, ALU.mult))
                b.fence("vector")
                b.opf("vector", lambda q: q.tensor_copy(S["gfin"][:], V["lgt"][:]),
                      "DVE")
                b.wait("tensor", "DVE")
                b.op("tensor", lambda q: q.transpose(
                    ps["T"][0:8, 0:128], S["gfin"][:], W["ident"][:]), "PE")
                b.wait("vector", "PE")
                b.opf("vector", lambda q, c0=c0, scol=scol: q.tensor_copy(
                    S["gall"][0:8, c0 + scol:c0 + scol + 128], ps["T"][0:8, 0:128]),
                    "DVE")

        if dbg:
            b.wait("gpsimd", "DVE")
            b.op("gpsimd", lambda q: q.dma_start(out=dbgo["d4_w"], in_=S["gall"][:]), "ST", 16)
            b.wait("vector", "ST")
        SER = _os.environ.get("P4_SERIAL", "0") == "1"

        def stream_mm(base_off, K, M, rhs_tv, evac_fn):
            nk = K // 128
            mg = min(1024, (8192 // nk) // 128 * 128)
            b.wait("tensor", "ACT")
            b.wait("tensor", "DVE")
            b.wait("gpsimd", "PE")
            pe_marks = {}
            evac_marks = []
            for gi in range(M // mg):
                buf = S["wb0"] if gi % 2 == 0 else S["wb1"]
                if SER:
                    b.wait("gpsimd", "PE")
                    b.wait("gpsimd", "ACT")
                    b.wait("gpsimd", "DVE")
                elif gi >= 2:
                    b.wait("gpsimd", "PE", pe_marks[gi - 2])
                src = bass.AP(inp[stream_mm.wname].tensor, base_off + gi * mg,
                              [[M, 128], [128 * M, nk], [1, mg]])
                b.op("gpsimd", lambda q, sgm=src, buf=buf, nk=nk, mg=mg:
                     q.dma_start(out=buf[:, 0:nk * mg], in_=sgm), "WT", 16)
                b.wait("tensor", "WT")
                for mc in range(mg // 128):
                    mi = (gi * mg) // 128 + mc
                    bank = "A" if mi % 2 == 0 else "B"
                    if SER:
                        b.wait("tensor", "ACT")
                        b.wait("tensor", "DVE")
                    elif len(evac_marks) >= 2:
                        b.wait("tensor", evac_marks[-2][0], evac_marks[-2][1])
                    for k in range(nk):
                        b.op("tensor", lambda q, k=k, mc=mc, bank=bank, buf=buf, mg=mg:
                             q.matmul(ps[bank][:],
                                      buf[:, k * mg + mc * 128:k * mg + (mc + 1) * 128],
                                      rhs_tv[:, k * NT:(k + 1) * NT],
                                      start=(k == 0), stop=(k == nk - 1)),
                             "PE" if k == nk - 1 else None)
                    evac_marks.append(evac_fn(mi, bank))
                pe_marks[gi] = b.val("PE")

        for tt in range(NTT):
            c0 = tt * NT
            b.wait("sync", "PE")
            b.wait("sync", "DVE")
            b.wait("sync", "ST")
            for k in range(NCH):
                src = bass.AP(scr["xf"].tensor, k * 128 * T_CORE + c0,
                              [[T_CORE, 128], [1, NT]])
                b.op("sync", lambda q, sgm=src, k=k:
                     q.dma_start(out=S["xf"][:, k * NT:(k + 1) * NT], in_=sgm), "LD", 16)
            b.wait("vector", "LD")
            b.wait("vector", "DVE")
            b.opf("vector", lambda q: q.memset(S["acc"][:], 0.0), "DVE")
            b.wait("tensor", "LD")

            def evac_silu(mi, bank):
                b.wait("scalar", "PE")
                b.op("scalar", lambda q, mi=mi, bank=bank: q.activation(
                    S["h"][:, mi * NT:(mi + 1) * NT], ps[bank][:], AF.Silu), "ACT")
                return ("ACT", b.val("ACT"))

            def evac_hmid(mi, bank):
                b.wait("vector", "PE")
                b.op("vector", lambda q, mi=mi, bank=bank: q.tensor_tensor(
                    S["h"][:, (16 + mi) * NT:(16 + mi + 1) * NT],
                    S["h"][:, mi * NT:(mi + 1) * NT], ps[bank][:], ALU.mult), "DVE")
                return ("DVE", b.val("DVE"))

            def evac_acc(mi, bank):
                b.wait("vector", "PE")
                b.op("vector", lambda q, mi=mi, bank=bank: q.tensor_tensor(
                    S["acc"][:, mi * NT:(mi + 1) * NT],
                    S["acc"][:, mi * NT:(mi + 1) * NT], ps[bank][:], ALU.add), "DVE")
                return ("DVE", b.val("DVE"))

            hmid_tv = TV(pool16, 0, 128, S["h"].off + 16 * NT, 16 * NT)

            for e in range(8):
                b.wait("gpsimd", "PE")
                b.wait("gpsimd", "DVE")
                b.op("gpsimd", lambda q, e=e, c0=c0: q.dma_start(
                    out=S["grow"][:], in_=S["gall"][e:e + 1, c0:c0 + NT]), "WT", 16)
                b.wait("tensor", "WT")
                b.wait("tensor", "DVE")
                b.wait("tensor", "ACT")
                b.op("tensor", lambda q: q.matmul(
                    ps["C"][:], C["ones1x128_bf"][:], S["grow"][:],
                    start=True, stop=True), "PE")
                b.wait("vector", "PE")
                for k in range(NCH):
                    sl = slice(k * NT, (k + 1) * NT)
                    b.op("vector", lambda q, sl=sl: q.tensor_tensor(
                        S["bf_a"][:, sl], S["xf"][:, sl], ps["C"][:], ALU.mult),
                        "DVE" if k == NCH - 1 else None)
                if dbg and tt == 0 and e == 0:
                    b.wait("gpsimd", "DVE")
                    b.op("gpsimd", lambda q: q.dma_start(out=dbgo["d4_xe"], in_=S["bf_a"][:]), "ST", 16)
                    b.wait("vector", "ST")
                stream_mm.wname = "l1_gate"
                stream_mm(e * D * 2 * D, D, 2 * D, S["bf_a"], evac_silu)
                stream_mm.wname = "l1_up"
                stream_mm(e * D * 2 * D, D, 2 * D, S["bf_a"], evac_hmid)
                if dbg and tt == 0 and e == 0:
                    b.wait("gpsimd", "ACT")
                    b.wait("gpsimd", "DVE")
                    b.op("gpsimd", lambda q: q.dma_start(out=dbgo["d4_hs"], in_=S["h"][:, 0:16 * NT]), "ST", 16)
                    b.op("gpsimd", lambda q: q.dma_start(out=dbgo["d4_hm"], in_=S["h"][:, 16 * NT:32 * NT]), "ST", 16)
                    b.wait("vector", "ST")
                    b.wait("scalar", "ST")
                stream_mm.wname = "l1_down"
                stream_mm(e * 2 * D * D, 2 * D, D, hmid_tv, evac_acc)
                if dbg and tt == 0 and e == 0:
                    b.wait("gpsimd", "DVE")
                    b.op("gpsimd", lambda q: q.dma_start(out=dbgo["d4_acc0"], in_=S["acc"][:]), "ST", 16)
                    b.wait("vector", "ST")
            stream_mm.wname = "l1_shg"
            stream_mm(0, D, 2 * D, S["xf"], evac_silu)
            stream_mm.wname = "l1_shu"
            stream_mm(0, D, 2 * D, S["xf"], evac_hmid)
            stream_mm.wname = "l1_shd"
            stream_mm(0, 2 * D, D, hmid_tv, evac_acc)
            if dbg and tt == 0:
                b.wait("gpsimd", "DVE")
                b.op("gpsimd", lambda q: q.dma_start(out=dbgo["d4_acc"], in_=S["acc"][:]), "ST", 16)
                b.wait("vector", "ST")
                b.wait("scalar", "ST")
            b.wait("sync", "ST")
            b.wait("sync", "DVE")
            for k in range(NCH):
                src = bass.AP(scr["hid"].tensor, k * 128 * T_CORE + c0,
                              [[T_CORE, 128], [1, NT]])
                b.op("sync", lambda q, sgm=src, k=k:
                     q.dma_start(out=S["bf_c"][:, k * NT:(k + 1) * NT], in_=sgm),
                     "LD", 16)
            b.wait("vector", "LD")
            b.wait("vector", "DVE")
            for k in range(NCH):
                sl = slice(k * NT, (k + 1) * NT)
                b.op("vector", lambda q, sl=sl: q.tensor_tensor(
                    S["o2"][:, sl], S["bf_c"][:, sl], S["acc"][:, sl], ALU.add),
                    "DVE" if k == NCH - 1 else None)
            b.wait("scalar", "DVE")
            post_ln_residual(tt, xb_in, xb_out, S["o2"])



    import os as _os
    P2_STAGES = _os.environ.get("P2_STAGES", "ABCDE")

    def ffn0_phase(xb_in, xb_out, S=S, W=W):
        S = dict(S)
        W = dict(W)
        S["xs"] = carve16(NCH * NT)
        S["h"] = carve16(32 * NT)
        S["wb0"] = carve16(8192)
        S["wb1"] = carve16(8192)
        W["g1"] = carve16(NCH * D)
        W["g2"] = carve16(NCH * (D // 2))
        W["g3"] = carve16(4 * 8)
        W["aff"] = carve16(NCH * 8)
        W["b2s"] = carve16(D, npart=8)
        S["gall"] = carve16(T_CORE, npart=8)
        S["srow"] = carve16(NT, npart=1)
        S["gb"] = carve16(NT)
        S["acc"] = carve16(NCH * NT)
        S["gfin"] = carve16(8)
        S["grow"] = carve16(NT, npart=1)
        V32 = {}
        V32["noise"] = carve32(NT, npart=8)
        V32["lg"] = carve32(NT, npart=8)
        V32["impg"] = carve32(256)
        V32["scrap"] = carve32(256)
        V32["gnan"] = carve32(8)
        V32["xsum"] = carve32(16)
        V32["ge"] = carve32(8)
        V32["lgt"] = carve32(8)
        V32["t8a"] = carve32(8)
        V32["t8b"] = carve32(8)
        BIA = {}
        for a in range(8):
            for (wn, K, M, act) in E_SPECS[a]:
                if act is not None:
                    BIA[wn] = carve32(M // 128)
        for nm in ("gate_b1", "ln_scale", "ln_offset"):
            BIA[nm] = carve32(8)
        BIA["gate_b2"] = carve32(4)
        T1 = {}
        for nm in ("lo", "hi", "mid", "ff", "cnt", "omf", "vstar_c", "dsq"):
            T1[nm] = sb("c2_" + nm, [128, 1])
        for nm in ("vstar", "temp", "flag", "ns", "gm", "sval"):
            T1[nm] = sb("c2r_" + nm, [1, 1])
        T1["u8"] = sb("c2_u8", [1, 8])
        T1["u8b"] = sb("c2_u8b", [1, 8])
        T1["r128"] = sb("c2_r128", [1, 128])
        T1["b3"] = sb("c2_b3", [8, 1])
        T1["affb"] = sb("c2_affb", [8, 1])
        T1["aff03"] = sb("c2_aff03", [8, 2])
        T1["ns8"] = sb("c2_ns8", [8, 1])
        T1["fl8"] = sb("c2_fl8", [8, 1])
        T1["ones128f"] = sb("c2_o128f", [128, 1])
        T1["ones1x8f"] = sb("c2_o1x8f", [1, 8])

        b.op("vector", lambda q: q.memset(T1["ones128f"][:], 1.0))
        b.op("vector", lambda q: q.memset(T1["ones1x8f"][:], 1.0))
        b.opf("vector", lambda q: q.memset(V32["xsum"][:], 0.0), "DVE")
        for k in range(NCH):
            src = bass.AP(inp["gate_w1"].tensor, k * 128 * D, [[D, 128], [1, D]])
            b.op("gpsimd", lambda q, s=src, k=k:
                 q.dma_start(out=W["g1"][:, k * D:(k + 1) * D], in_=s), "WT", 16)
        for k in range(NCH):
            src = bass.AP(inp["gate_w2"].tensor, k * 128 * (D // 2),
                          [[D // 2, 128], [1, D // 2]])
            b.op("gpsimd", lambda q, s=src, k=k:
                 q.dma_start(out=W["g2"][:, k * (D // 2):(k + 1) * (D // 2)], in_=s),
                 "WT", 16)
        for k in range(4):
            src = bass.AP(inp["gate_w3p"].tensor, k * 128 * 8, [[8, 128], [1, 8]])
            b.op("gpsimd", lambda q, s=src, k=k:
                 q.dma_start(out=W["g3"][:, k * 8:(k + 1) * 8], in_=s), "WT", 16)
        for k in range(NCH):
            src = bass.AP(inp["aff_wp"].tensor, k * 128 * 8, [[8, 128], [1, 8]])
            b.op("gpsimd", lambda q, s=src, k=k:
                 q.dma_start(out=W["aff"][:, k * 8:(k + 1) * 8], in_=s), "WT", 16)
        b.op("gpsimd", lambda q: q.dma_start(out=W["b2s"][:], in_=inp["b2stack"]), "WT", 16)
        for nm in ("gate_b1", "ln_scale", "ln_offset", "gate_b2"):
            b.op("sync", lambda q, nm=nm: q.dma_start(out=BIA[nm][:], in_=inp[nm]),
                 "LD", 16)
        for a in range(8):
            for (wn, K, M, act) in E_SPECS[a]:
                if act is not None:
                    b.op("sync", lambda q, wn=wn:
                         q.dma_start(out=BIA[wn][:], in_=inp["b_" + wn]), "LD", 16)
        b.op("sync", lambda q: q.dma_start(out=T1["b3"][:], in_=inp["gate_b3p"]), "LD", 16)
        b.op("sync", lambda q: q.dma_start(out=T1["affb"][:], in_=inp["aff_bp"]), "LD", 16)
        b.op("sync", lambda q: q.dma_start(out=T1["u8"][:], in_=inp["usage"]), "LD", 16)
        b.op("sync", lambda q: q.dma_start(out=V32["gnan"][:], in_=inp["gnan"]), "LD", 16)

        row_dma = lambda row_tv, dram_ap, c0, n=NT: None

        for tt in range(NTT):
            c0 = tt * NT
            load_x_tile(tt, xb_in)
            b.wait("scalar", "LD")
            b.wait("scalar", "DVE")
            for k in range(NCH):
                sl = slice(k * NT, (k + 1) * NT)
                b.op("scalar", lambda q, sl=sl: q.activation(
                    S["o2"][:, sl], S["x"][:, sl], AF.Square), "ACT")
            b.wait("tensor", "ACT")
            b.wait("tensor", "DVE")
            b.wait("tensor", "LD")
            for k in range(NCH):
                b.op("tensor", lambda q, k=k: q.matmul(
                    ps["F"][0:1, :], C["ones128_f"][:], S["x"][:, k * NT:(k + 1) * NT],
                    start=(k == 0), stop=(k == NCH - 1)))
            for k in range(NCH):
                b.op("tensor", lambda q, k=k: q.matmul(
                    ps["G"][0:1, :], C["ones128_f"][:], S["o2"][:, k * NT:(k + 1) * NT],
                    start=(k == 0), stop=(k == NCH - 1)),
                    "PE" if k == NCH - 1 else None)
            b.wait("vector", "PE")
            b.op("vector", lambda q: q.tensor_scalar(
                R["mu"][:], ps["F"][0:1, :], 1.0 / D, None, ALU.mult))
            b.op("vector", lambda q: q.tensor_scalar(
                R["tmp1"][:], ps["G"][0:1, :], 1.0 / D, None, ALU.mult))
            b.fence("vector")
            b.op("vector", lambda q: q.tensor_tensor(
                R["tmp2"][:], R["mu"][:], R["mu"][:], ALU.mult))
            b.fence("vector")
            b.op("vector", lambda q: q.tensor_tensor(
                R["s2"][:], R["tmp1"][:], R["tmp2"][:], ALU.subtract))
            b.fence("vector")
            b.opf("vector", lambda q: q.tensor_scalar(
                R["tmp1"][:], R["s2"][:], 1e-5, None, ALU.add), "DVE")
            b.wait("scalar", "DVE")
            b.opf("scalar", lambda q: q.activation(
                R["tmp2"][:], R["tmp1"][:], AF.Sqrt), "ACT")
            b.wait("vector", "ACT")
            b.op("vector", lambda q: q.reciprocal(R["r"][:], R["tmp2"][:]))
            b.fence("vector")
            b.op("vector", lambda q: q.tensor_tensor(
                R["tmp2"][:], R["r"][:], R["r"][:], ALU.mult))
            b.fence("vector")
            b.op("vector", lambda q: q.tensor_tensor(
                R["tmp2"][:], R["tmp2"][:], R["tmp1"][:], ALU.mult))
            b.fence("vector")
            b.op("vector", lambda q: q.tensor_scalar(
                R["tmp2"][:], R["tmp2"][:], -0.5, 1.5, ALU.mult, ALU.add))
            b.fence("vector")
            b.opf("vector", lambda q: q.tensor_tensor(
                R["r"][:], R["r"][:], R["tmp2"][:], ALU.mult), "DVE")
            b.wait("tensor", "DVE")
            b.op("tensor", lambda q: q.matmul(
                ps["C"][:], C["ones1x128_f"][:], R["mu"][:], start=True, stop=True))
            b.op("tensor", lambda q: q.matmul(
                ps["D"][:], C["ones1x128_f"][:], R["r"][:], start=True, stop=True), "PE")
            b.wait("vector", "PE")
            for k in range(NCH):
                sl = slice(k * NT, (k + 1) * NT)
                b.op("vector", lambda q, sl=sl: q.tensor_tensor(
                    S["t32a"][:], S["x"][:, sl], ps["C"][:], ALU.subtract))
                b.op("vector", lambda q, sl=sl: q.tensor_copy(
                    S["xs"][:, sl], S["t32a"][:]))
                b.op("vector", lambda q, sl=sl: q.tensor_tensor(
                    S["x"][:, sl], S["t32a"][:], ps["D"][:], ALU.mult),
                    "DVE" if k == NCH - 1 else None)
            b.wait("scalar", "DVE")
            for k in range(NCH):
                sl = slice(k * NT, (k + 1) * NT)
                b.op("scalar", lambda q, sl=sl: q.activation(
                    S["o2"][:, sl], S["x"][:, sl], AF.Square), "ACT")
                b.op("scalar", lambda q, sl=sl: q.activation(
                    S["bf_b"][:, sl], S["xs"][:, sl], AF.Abs),
                    "ACT" if k == NCH - 1 else None)
            b.wait("tensor", "ACT")
            for k in range(NCH):
                b.op("tensor", lambda q, k=k: q.matmul(
                    ps["E"][0:1, :], C["ones128_f"][:], S["o2"][:, k * NT:(k + 1) * NT],
                    start=(k == 0), stop=(k == NCH - 1)))
            for k in range(NCH):
                b.op("tensor", lambda q, k=k: q.matmul(
                    ps["F"][0:1, :], C["ones128_bf"][:], S["bf_b"][:, k * NT:(k + 1) * NT],
                    start=(k == 0), stop=(k == NCH - 1)),
                    "PE" if k == NCH - 1 else None)
            b.wait("vector", "PE")
            b.opf("vector", lambda q: q.tensor_copy(R["s3"][:], ps["F"][0:1, :]), "DVE")
            b.wait("scalar", "PE")
            b.wait("scalar", "DVE")
            b.opf("scalar", lambda q: q.activation(
                R["imp"][:], ps["E"][0:1, :], AF.Sqrt), "ACT")
            b.wait("vector", "ACT")
            b.fence("vector")
            b.wait("sync", "DVE")
            b.wait("sync", "ACT")
            b.wait("sync", "PE")
            for k in range(NCH):
                dst = bass.AP(scr["xc"].tensor, k * 128 * T_CORE + c0,
                              [[T_CORE, 128], [1, NT]])
                b.op("sync", lambda q, d=dst, k=k:
                     q.dma_start(out=d, in_=S["xs"][:, k * NT:(k + 1) * NT]), "ST", 16)
            b.op("sync", lambda q, tt=tt: q.dma_start(
                out=bass.AP(scr["cc_in"].tensor, tt * NT, [[NT, 1], [1, NT]]),
                in_=R["imp"][:]), "ST", 16)
            b.op("sync", lambda q, c0=c0: q.dma_start(
                out=bass.AP(scr["r"].tensor, c0, [[NT, 1], [1, NT]]),
                in_=R["r"][:]), "ST", 16)
            b.op("sync", lambda q, c0=c0: q.dma_start(
                out=bass.AP(scr["s3"].tensor, c0, [[NT, 1], [1, NT]]),
                in_=R["s3"][:]), "ST", 16)

        if "B" not in P2_STAGES:
            return
        b.wait("gpsimd", "ST")
        b.op("gpsimd", lambda q: q.collective_compute(
            "AllGather", ALU.bypass, replica_groups=[list(range(NCORES))],
            ins=[cc_in_t.ap().opt()], outs=[cc_out_t.ap().opt()]), "CC", 1)
        b.wait("sync", "CC")
        b.op("sync", lambda q: q.dma_start(out=V32["impg"][:], in_=scr["cc_out"]),
             "LD", 16)
        b.wait("vector", "LD")

        b.opf("vector", lambda q: q.tensor_scalar(
            V32["scrap"][:], V32["impg"][:], 1.0 / 32768.0, 0.0, ALU.mult,
            ALU.add, accum_out=T1["dsq"][:]), "DVE")
        b.wait("tensor", "DVE")
        b.op("tensor", lambda q: q.matmul(ps["G"][0:1, 0:1], T1["ones128f"][:],
                                          T1["dsq"][:], start=True, stop=True), "PE")
        b.wait("vector", "PE")
        b.opf("vector", lambda q: q.tensor_copy(T1["gm"][:], ps["G"][0:1, 0:1]), "DVE")
        b.wait("tensor", "DVE")
        b.op("tensor", lambda q: q.matmul(ps["G"][:, 0:1], C["ones1x128_f"][:],
                                          T1["gm"][:], start=True, stop=True), "PE")
        b.wait("vector", "PE")
        b.opf("vector", lambda q: q.tensor_copy(T1["vstar_c"][:], ps["G"][:, 0:1]), "DVE")
        b.opf("vector", lambda q: q.tensor_scalar(
            V32["scrap"][:], V32["impg"][:], T1["vstar_c"][:], None, ALU.subtract),
            "DVE")
        b.wait("scalar", "DVE")
        b.opf("scalar", lambda q: q.activation(
            V32["scrap"][:], V32["scrap"][:], AF.Square, accum_out=T1["dsq"][:]), "ACT")
        b.wait("tensor", "ACT")
        b.op("tensor", lambda q: q.matmul(ps["G"][0:1, 0:1], T1["ones128f"][:],
                                          T1["dsq"][:], start=True, stop=True), "PE")
        b.wait("vector", "PE")
        b.opf("vector", lambda q: q.tensor_scalar(
            T1["gm"][:], ps["G"][0:1, 0:1], 1.0 / 32768.0, None, ALU.mult), "DVE")
        b.wait("scalar", "DVE")
        b.opf("scalar", lambda q: q.activation(T1["sval"][:], T1["gm"][:], AF.Sqrt),
              "ACT")
        b.wait("vector", "ACT")
        b.opf("vector", lambda q: q.tensor_scalar(
            T1["temp"][:], T1["sval"][:], 0.1, 1.0, ALU.mult, ALU.add), "DVE")

        b.op("vector", lambda q: q.memset(T1["lo"][:], 0.0))
        b.op("vector", lambda q: q.tensor_reduce(
            T1["hi"][:], V32["impg"][:], AX.X, ALU.max))
        b.fence("vector")
        b.opf("vector", lambda q: q.tensor_copy(T1["mid"][:], T1["hi"][:]), "DVE")
        b.wait("tensor", "DVE")
        b.op("tensor", lambda q: q.transpose(
            ps["E"][0:1, 0:128], T1["mid"][:], W["identf"][:]), "PE")
        b.wait("vector", "PE")
        b.op("vector", lambda q: q.tensor_reduce(
            T1["gm"][:], ps["E"][0:1, 0:128], AX.X, ALU.max))
        b.fence("vector")
        b.opf("vector", lambda q: q.tensor_copy(T1["r128"][0:1, 0:1], T1["gm"][:]),
              "DVE")
        b.wait("tensor", "DVE")
        b.op("tensor", lambda q: q.matmul(ps["G"][:, 0:1], C["ones1x128_f"][:],
                                          T1["gm"][:], start=True, stop=True), "PE")
        b.wait("vector", "PE")
        b.opf("vector", lambda q: q.tensor_copy(T1["hi"][:], ps["G"][:, 0:1]), "DVE")
        if dbg:
            b.wait("gpsimd", "DVE")
            b.op("gpsimd", lambda q: q.dma_start(out=dbgo["d_hi0"], in_=T1["hi"][:]), "ST", 16)
            b.wait("vector", "ST")
        K_SEL = float(int(0.9 * (32768 - 1)) + 1)
        for it in range(60):
            b.op("vector", lambda q: q.tensor_tensor(
                T1["mid"][:], T1["lo"][:], T1["hi"][:], ALU.add))
            b.fence("vector")
            b.op("vector", lambda q: q.tensor_scalar(
                T1["mid"][:], T1["mid"][:], 0.5, None, ALU.mult))
            b.fence("vector")
            b.opf("vector", lambda q: q.tensor_scalar(
                V32["scrap"][:], V32["impg"][:], T1["mid"][:], 0.0, ALU.is_le,
                ALU.add, accum_out=T1["dsq"][:]), "DVE")
            b.wait("tensor", "DVE")
            b.op("tensor", lambda q: q.matmul(ps["G"][0:1, 0:1], T1["ones128f"][:],
                                              T1["dsq"][:], start=True, stop=True),
                 "PE")
            b.wait("vector", "PE")
            b.opf("vector", lambda q: q.tensor_copy(T1["gm"][:], ps["G"][0:1, 0:1]),
                  "DVE")
            b.wait("tensor", "DVE")
            b.op("tensor", lambda q: q.matmul(ps["G"][:, 0:1], C["ones1x128_f"][:],
                                              T1["gm"][:], start=True, stop=True), "PE")
            b.wait("vector", "PE")
            if dbg and it == 0:
                b.op("vector", lambda q: q.tensor_copy(T1["vstar_c"][:], ps["G"][:, 0:1]))
                b.fence("vector")
                b.opf("vector", lambda q: q.tensor_copy(T1["dsq"][:], T1["dsq"][:]), "DVE")
                b.wait("gpsimd", "DVE")
                b.op("gpsimd", lambda q: q.dma_start(out=dbgo["d_cnt0"], in_=T1["vstar_c"][:]), "ST", 16)
                b.op("gpsimd", lambda q: q.dma_start(out=dbgo["d_dsq0"], in_=T1["dsq"][:]), "ST", 16)
                b.wait("vector", "ST")
            b.op("vector", lambda q: q.tensor_scalar(
                T1["ff"][:], ps["G"][:, 0:1], K_SEL, None, ALU.is_ge))
            b.fence("vector")
            b.op("vector", lambda q: q.tensor_tensor(
                T1["cnt"][:], T1["mid"][:], T1["hi"][:], ALU.subtract))
            b.fence("vector")
            b.op("vector", lambda q: q.tensor_tensor(
                T1["cnt"][:], T1["cnt"][:], T1["ff"][:], ALU.mult))
            b.fence("vector")
            b.op("vector", lambda q: q.tensor_tensor(
                T1["hi"][:], T1["hi"][:], T1["cnt"][:], ALU.add))
            b.op("vector", lambda q: q.tensor_scalar(
                T1["omf"][:], T1["ff"][:], -1.0, 1.0, ALU.mult, ALU.add))
            b.fence("vector")
            b.op("vector", lambda q: q.tensor_tensor(
                T1["cnt"][:], T1["mid"][:], T1["lo"][:], ALU.subtract))
            b.fence("vector")
            b.op("vector", lambda q: q.tensor_tensor(
                T1["cnt"][:], T1["cnt"][:], T1["omf"][:], ALU.mult))
            b.fence("vector")
            b.opf("vector", lambda q: q.tensor_tensor(
                T1["lo"][:], T1["lo"][:], T1["cnt"][:], ALU.add), "DVE")
        b.op("vector", lambda q: q.tensor_scalar(
            V32["scrap"][:], V32["impg"][:], T1["lo"][:], None, ALU.is_le))
        b.fence("vector")
        b.op("vector", lambda q: q.scalar_tensor_tensor(
            V32["scrap"][:], V32["scrap"][:], 1e38, V32["impg"][:], ALU.mult, ALU.add))
        b.fence("vector")
        b.op("vector", lambda q: q.tensor_reduce(
            T1["mid"][:], V32["scrap"][:], AX.X, ALU.min))
        b.fence("vector")
        b.opf("vector", lambda q: q.tensor_copy(T1["cnt"][:], T1["mid"][:]), "DVE")
        b.wait("tensor", "DVE")
        b.op("tensor", lambda q: q.transpose(
            ps["E"][0:1, 0:128], T1["mid"][:], W["identf"][:]), "PE")
        b.wait("vector", "PE")
        b.op("vector", lambda q: q.tensor_reduce(
            T1["vstar"][:], ps["E"][0:1, 0:128], AX.X, ALU.min))
        b.fence("vector")
        b.opf("vector", lambda q: q.tensor_copy(T1["r128"][0:1, 0:1], T1["vstar"][:]),
              "DVE")

        b.op("vector", lambda q: q.tensor_scalar(
            T1["u8b"][:], T1["u8"][:], 1e-8, None, ALU.add))
        b.fence("vector")
        b.op("vector", lambda q: q.tensor_reduce(
            T1["sval"][:], T1["u8b"][:], AX.X, ALU.min))
        b.fence("vector")
        b.op("vector", lambda q: q.tensor_scalar(
            T1["flag"][:], T1["sval"][:], 0.0, None, ALU.is_lt))
        b.fence("vector")
        b.opf("vector", lambda q: q.tensor_scalar(
            T1["u8b"][:], T1["u8b"][:], 1e-30, None, ALU.max), "DVE")
        b.wait("scalar", "DVE")
        b.opf("scalar", lambda q: q.activation(T1["u8b"][:], T1["u8b"][:], AF.Ln),
              "ACT")
        b.wait("vector", "ACT")
        b.op("vector", lambda q: q.tensor_tensor(
            T1["u8b"][:], T1["u8"][:], T1["u8b"][:], ALU.mult))
        b.fence("vector")
        b.op("vector", lambda q: q.tensor_reduce(
            T1["sval"][:], T1["u8b"][:], AX.X, ALU.add, negate=True))
        b.fence("vector")
        b.op("vector", lambda q: q.tensor_scalar(
            T1["sval"][:], T1["sval"][:], -0.01, 0.02, ALU.mult, ALU.add))
        b.fence("vector")
        b.op("vector", lambda q: q.tensor_scalar(
            T1["sval"][:], T1["sval"][:], 1e-4, None, ALU.max))
        b.fence("vector")
        b.op("vector", lambda q: q.tensor_scalar(
            T1["gm"][:], T1["flag"][:], -1.0, 1.0, ALU.mult, ALU.add))
        b.fence("vector")
        b.opf("vector", lambda q: q.tensor_tensor(
            T1["ns"][:], T1["sval"][:], T1["gm"][:], ALU.mult), "DVE")
        b.wait("tensor", "DVE")
        b.op("tensor", lambda q: q.matmul(ps["G"][0:8, 0:1], T1["ones1x8f"][:],
                                          T1["ns"][:], start=True, stop=True), "PE")
        b.wait("vector", "PE")
        b.opf("vector", lambda q: q.tensor_copy(T1["ns8"][:], ps["G"][0:8, 0:1]), "DVE")
        b.wait("tensor", "DVE")
        b.op("tensor", lambda q: q.matmul(ps["G"][0:8, 0:1], T1["ones1x8f"][:],
                                          T1["flag"][:], start=True, stop=True), "PE")
        b.wait("vector", "PE")
        b.opf("vector", lambda q: q.tensor_copy(T1["fl8"][:], ps["G"][0:8, 0:1]), "DVE")
        b.wait("tensor", "DVE")
        b.op("tensor", lambda q: q.matmul(ps["G"][:, 0:1], C["ones1x128_f"][:],
                                          T1["flag"][:], start=True, stop=True), "PE")
        b.wait("vector", "PE")
        b.op("vector", lambda q: q.tensor_copy(T1["ff"][:], ps["G"][:, 0:1]))
        b.fence("vector")
        b.opf("vector", lambda q: q.tensor_scalar(
            T1["omf"][:], T1["ff"][:], -1.0, 1.0, ALU.mult, ALU.add), "DVE")

        if "C" not in P2_STAGES:
            return
        if dbg:
            b.wait("gpsimd", "DVE")
            b.op("gpsimd", lambda q: q.dma_start(out=dbgo["d_vstar"], in_=T1["vstar"][:]), "ST", 16)
            b.op("gpsimd", lambda q: q.dma_start(out=dbgo["d_impg"], in_=V32["impg"][:]), "ST", 16)
            b.op("gpsimd", lambda q: q.dma_start(out=dbgo["d_temp"], in_=T1["temp"][:]), "ST", 16)
            b.op("gpsimd", lambda q: q.dma_start(out=dbgo["d_flag"], in_=T1["flag"][:]), "ST", 16)
        for tt in range(NTT):
            c0 = tt * NT
            b.wait("sync", "PE")
            b.wait("sync", "DVE")
            b.wait("sync", "ST")
            for k in range(NCH):
                src = bass.AP(scr["xc"].tensor, k * 128 * T_CORE + c0,
                              [[T_CORE, 128], [1, NT]])
                b.op("sync", lambda q, s=src, k=k:
                     q.dma_start(out=S["xs"][:, k * NT:(k + 1) * NT], in_=s), "LD", 16)
            b.op("sync", lambda q, tt=tt: q.dma_start(
                out=R["imp"][:],
                in_=bass.AP(scr["cc_in"].tensor, tt * NT, [[NT, 1], [1, NT]])), "LD", 16)
            b.op("sync", lambda q, c0=c0: q.dma_start(
                out=R["r"][:], in_=bass.AP(scr["r"].tensor, c0, [[NT, 1], [1, NT]])),
                "LD", 16)
            b.op("sync", lambda q, c0=c0: q.dma_start(
                out=R["s3"][:], in_=bass.AP(scr["s3"].tensor, c0, [[NT, 1], [1, NT]])),
                "LD", 16)
            b.wait("vector", "LD")
            b.op("vector", lambda q: q.tensor_scalar(
                R["tmp1"][:], R["imp"][:], T1["vstar"][:], None, ALU.is_gt))
            b.op("vector", lambda q: q.tensor_tensor(
                R["tmp2"][:], R["r"][:], R["s3"][:], ALU.mult))
            b.fence("vector")
            b.op("vector", lambda q: q.tensor_tensor(
                R["tmp2"][:], R["tmp2"][:], R["tmp1"][:], ALU.mult))
            b.fence("vector")
            b.op("vector", lambda q: q.tensor_scalar(
                R["tmp2"][:], R["tmp2"][:], 1e-8, None, ALU.add))
            b.fence("vector")
            b.op("vector", lambda q: q.tensor_scalar(
                R["tmp2"][:], R["tmp2"][:], T1["temp"][:], None, ALU.mult))
            b.fence("vector")
            b.op("vector", lambda q: q.reciprocal(R["s2"][:], R["tmp2"][:]))
            b.fence("vector")
            b.op("vector", lambda q: q.tensor_tensor(
                R["s2"][:], R["s2"][:], R["r"][:], ALU.mult))
            b.fence("vector")
            b.op("vector", lambda q: q.tensor_tensor(
                R["s2"][:], R["s2"][:], R["tmp1"][:], ALU.mult))
            b.fence("vector")
            b.opf("vector", lambda q: q.tensor_copy(S["srow"][:], R["s2"][:]), "DVE")
            b.wait("tensor", "DVE")
            bcast_row(S["srow"][:], "D")
            b.wait("vector", "PE")
            for k in range(NCH):
                sl = slice(k * NT, (k + 1) * NT)
                b.op("vector", lambda q, sl=sl: q.tensor_tensor(
                    S["xs"][:, sl], S["xs"][:, sl], ps["D"][:], ALU.mult),
                    "DVE" if k == NCH - 1 else None)
            if dbg and tt == 0:
                b.wait("gpsimd", "DVE")
                b.op("gpsimd", lambda q: q.dma_start(out=dbgo["d_xs"], in_=S["xs"][:]), "ST", 16)
                b.op("gpsimd", lambda q: q.dma_start(out=dbgo["d_srow"], in_=S["srow"][:]), "ST", 16)
                b.op("gpsimd", lambda q: q.dma_start(out=dbgo["d_imp"], in_=R["imp"][:]), "ST", 16)
                b.wait("vector", "ST")
            b.wait("sync", "DVE")
            for k in range(NCH):
                dst = bass.AP(scr["xs"].tensor, k * 128 * T_CORE + c0,
                              [[T_CORE, 128], [1, NT]])
                b.op("sync", lambda q, d=dst, k=k:
                     q.dma_start(out=d, in_=S["xs"][:, k * NT:(k + 1) * NT]), "ST", 16)
            bb = tt // (T_BATCH // NT)
            for k in range(NCH):
                sl = slice(k * NT, (k + 1) * NT)
                b.opf("vector", lambda q, sl=sl: q.tensor_scalar(
                    S["t32a"][:], S["xs"][:, sl], 1.0 / T_BATCH, 0.0, ALU.mult,
                    ALU.add, accum_out=T1["dsq"][:]), "DVE")
                col = bb * 8 + k
                b.op("vector", lambda q, col=col: q.tensor_tensor(
                    V32["xsum"][:, col:col + 1], V32["xsum"][:, col:col + 1],
                    T1["dsq"][:], ALU.add), "DVE")
            b.wait("tensor", "DVE")
            mm_sweep_evac(
                NCH,
                lambda m, k: W["g1"][:, k * D + m * 128:k * D + (m + 1) * 128],
                lambda m, k: S["xs"][:, k * NT:(k + 1) * NT],
                lambda m, bank: b.op("scalar", lambda q, m=m, bank=bank: q.activation(
                    S["bf_a"][:, m * NT:(m + 1) * NT], ps[bank][:], AF.Silu,
                    bias=BIA["gate_b1"][:, m:m + 1]), "ACT"))
            b.wait("scalar", "ACT")
            for k in range(NCH):
                sl = slice(k * NT, (k + 1) * NT)
                b.op("scalar", lambda q, sl=sl: q.activation(
                    S["bf_b"][:, sl], S["bf_a"][:, sl], AF.Square), "ACT")
            b.wait("tensor", "ACT")
            b.wait("tensor", "DVE")
            stats_rows(S["bf_a"], S["bf_b"], tt)
            b.wait("tensor", "DVE")
            bcast_row(R["mu_bf"][:], "C")
            bcast_row(R["r_bf"][:], "D")
            b.wait("vector", "PE")
            for k in range(NCH):
                sl = slice(k * NT, (k + 1) * NT)
                b.op("vector", lambda q, sl=sl: q.tensor_tensor(
                    S["t32a"][:], S["bf_a"][:, sl], ps["C"][:], ALU.subtract))
                b.op("vector", lambda q, sl=sl: q.tensor_tensor(
                    S["t32a"][:], S["t32a"][:], ps["D"][:], ALU.mult))
                b.op("vector", lambda q, sl=sl, k=k: q.tensor_scalar(
                    S["bf_a"][:, sl], S["t32a"][:], BIA["ln_scale"][:, k:k + 1],
                    BIA["ln_offset"][:, k:k + 1], ALU.mult, ALU.add),
                    "DVE" if k == NCH - 1 else None)
            b.wait("tensor", "DVE")
            mm_sweep_evac(
                4,
                lambda m, k: W["g2"][:, k * (D // 2) + m * 128:k * (D // 2) + (m + 1) * 128],
                lambda m, k: S["bf_a"][:, k * NT:(k + 1) * NT],
                lambda m, bank: b.op("scalar", lambda q, m=m, bank=bank: q.activation(
                    S["bf_c"][:, m * NT:(m + 1) * NT], ps[bank][:], AF.Silu,
                    bias=BIA["gate_b2"][:, m:m + 1]), "ACT"))
            b.wait("tensor", "ACT")
            b.wait("tensor", "DVE")
            for k in range(4):
                b.op("tensor", lambda q, k=k: q.matmul(
                    ps["E"][0:8, :], W["g3"][:, k * 8:(k + 1) * 8],
                    S["bf_c"][:, k * NT:(k + 1) * NT],
                    start=(k == 0), stop=(k == 3)), "PE" if k == 3 else None)
            b.wait("vector", "PE")
            b.opf("vector", lambda q: q.tensor_scalar(
                V32["lg"][:], ps["E"][0:8, :], T1["b3"][:], None, ALU.add), "DVE")
            b.wait("sync", "DVE")
            b.op("sync", lambda q, c0=c0: q.dma_start(
                out=bass.AP(scr["lg"].tensor, c0, [[T_CORE, 8], [1, NT]]),
                in_=V32["lg"][:]), "ST", 16)
            if dbg and tt == 0:
                b.wait("gpsimd", "DVE")
                b.op("gpsimd", lambda q: q.dma_start(out=dbgo["d_lg"], in_=V32["lg"][:]), "ST", 16)
                b.wait("vector", "ST")

        if "D" not in P2_STAGES:
            return
        b.wait("vector", "DVE")
        for bb in range(2):
            b.opf("vector", lambda q, bb=bb: q.tensor_copy(
                S["gfin"][:], V32["xsum"][:, bb * 8:bb * 8 + 8]), "DVE")
            b.wait("tensor", "DVE")
            for k in range(NCH):
                b.op("tensor", lambda q, k=k: q.matmul(
                    ps["F"][0:8, 0:1], W["aff"][:, k * 8:(k + 1) * 8],
                    S["gfin"][:, k:k + 1], start=(k == 0), stop=(k == NCH - 1)),
                    "PE" if k == NCH - 1 else None)
            b.wait("vector", "PE")
            b.opf("vector", lambda q, bb=bb: q.tensor_scalar(
                T1["aff03"][:, bb:bb + 1], ps["F"][0:8, 0:1], T1["affb"][:], 0.3,
                ALU.add, ALU.mult), "DVE")

        for tt in range(NTT):
            c0 = tt * NT
            bb = tt // (T_BATCH // NT)
            b.wait("sync", "ST")
            b.op("sync", lambda q, c0=c0: q.dma_start(
                out=V32["lg"][:],
                in_=bass.AP(scr["lg"].tensor, c0, [[T_CORE, 8], [1, NT]])), "LD", 16)
            b.op("sync", lambda q, c0=c0: q.dma_start(
                out=V32["noise"][:],
                in_=bass.AP(inp["noise_p"].tensor, c0, [[T_CORE, 8], [1, NT]])),
                "LD", 16)
            b.wait("vector", "LD")
            b.op("vector", lambda q, bb=bb: q.tensor_scalar(
                V32["lg"][:], V32["lg"][:], T1["aff03"][:, bb:bb + 1], None, ALU.add))
            b.fence("vector")
            b.op("vector", lambda q: q.scalar_tensor_tensor(
                V32["lg"][:], V32["noise"][:], T1["ns8"][:], V32["lg"][:],
                ALU.mult, ALU.add))
            b.fence("vector")
            b.opf("vector", lambda q: q.tensor_scalar(
                V32["lg"][:], V32["lg"][:], 1.0 / 1.5, None, ALU.mult), "DVE")
            for sub in range(4):
                scol = sub * 128
                b.wait("tensor", "DVE")
                b.wait("tensor", "ACT")
                b.op("tensor", lambda q, scol=scol: q.transpose(
                    ps["E"][:, 0:128], V32["lg"][0:8, scol:scol + 128],
                    W["identf"][0:8, :]), "PE")
                b.wait("vector", "PE")
                b.op("vector", lambda q: q.tensor_copy(
                    V32["lgt"][:], ps["E"][:, 0:8]))
                b.fence("vector")
                b.op("vector", lambda q: q.tensor_reduce(
                    T1["mid"][:], V32["lgt"][:], AX.X, ALU.max))
                b.fence("vector")
                b.op("vector", lambda q: q.tensor_scalar(
                    V32["t8a"][:], V32["lgt"][:], T1["mid"][:], None, ALU.is_ge))
                b.fence("vector")
                b.op("vector", lambda q: q.scalar_tensor_tensor(
                    V32["t8b"][:], V32["t8a"][:], -1e38, V32["lgt"][:],
                    ALU.mult, ALU.add))
                b.fence("vector")
                b.op("vector", lambda q: q.tensor_reduce(
                    T1["mid"][:], V32["t8b"][:], AX.X, ALU.max))
                b.fence("vector")
                b.op("vector", lambda q: q.tensor_scalar(
                    V32["t8a"][:], V32["lgt"][:], T1["mid"][:], None, ALU.is_ge))
                b.fence("vector")
                b.op("vector", lambda q: q.tensor_tensor(
                    V32["t8b"][:], V32["lgt"][:], V32["t8a"][:], ALU.mult))
                b.op("vector", lambda q: q.tensor_scalar(
                    V32["t8a"][:], V32["t8a"][:], 1.0, 1e9, ALU.subtract, ALU.mult))
                b.fence("vector")
                b.op("vector", lambda q: q.tensor_tensor(
                    V32["t8b"][:], V32["t8b"][:], V32["t8a"][:], ALU.add))
                b.fence("vector")
                b.op("vector", lambda q: q.tensor_reduce(
                    T1["mid"][:], V32["t8b"][:], AX.X, ALU.max, negate=True))
                b.fence("vector")
                b.opf("vector", lambda q: q.tensor_copy(T1["cnt"][:], T1["mid"][:]),
                      "DVE")
                b.wait("scalar", "DVE")
                b.opf("scalar", lambda q: q.activation(
                    V32["ge"][:], V32["t8b"][:], AF.Exp, bias=T1["mid"][:],
                    accum_out=T1["dsq"][:]), "ACT")
                b.wait("vector", "ACT")
                b.op("vector", lambda q: q.reciprocal(T1["cnt"][:], T1["dsq"][:]))
                b.fence("vector")
                b.op("vector", lambda q: q.tensor_scalar(
                    V32["ge"][:], V32["ge"][:], T1["cnt"][:], None, ALU.mult))
                b.fence("vector")
                b.op("vector", lambda q: q.tensor_scalar(
                    V32["ge"][:], V32["ge"][:], T1["omf"][:], None, ALU.mult))
                b.op("vector", lambda q: q.tensor_scalar(
                    V32["t8a"][:], V32["gnan"][:], T1["ff"][:], None, ALU.mult))
                b.fence("vector")
                b.op("vector", lambda q: q.tensor_tensor(
                    V32["ge"][:], V32["ge"][:], V32["t8a"][:], ALU.add))
                b.fence("vector")
                b.opf("vector", lambda q: q.tensor_copy(S["gfin"][:], V32["ge"][:]),
                      "DVE")
                b.wait("tensor", "DVE")
                b.op("tensor", lambda q: q.transpose(
                    ps["T"][0:8, 0:128], S["gfin"][:], W["ident"][:]), "PE")
                b.wait("vector", "PE")
                b.opf("vector", lambda q, c0=c0, scol=scol: q.tensor_copy(
                    S["gall"][0:8, c0 + scol:c0 + scol + 128], ps["T"][0:8, 0:128]),
                    "DVE")

        if dbg:
            b.wait("gpsimd", "DVE")
            b.op("gpsimd", lambda q: q.dma_start(out=dbgo["d_gall"], in_=S["gall"][:]), "ST", 16)
        if "E" not in P2_STAGES:
            return
        def stream_layer(wn, K, M, rhs_tv, rhs_cols, out_tv, act, bias_tv):
            nk = K // 128
            mg = min(1024, (8192 // (K // 128)) // 128 * 128)
            wsrc = inp[wn]
            b.wait("tensor", "ACT")
            b.wait("tensor", "DVE")
            b.wait("gpsimd", "PE")
            pe_marks = {}
            evac_marks = []
            nmg = M // mg
            for gi in range(nmg):
                buf = S["wb0"] if gi % 2 == 0 else S["wb1"]
                if gi >= 2:
                    b.wait("gpsimd", "PE", pe_marks[gi - 2])
                src = bass.AP(wsrc.tensor, gi * mg, [[M, 128], [128 * M, nk], [1, mg]])
                b.op("gpsimd", lambda q, s=src, buf=buf, nk=nk, mg=mg:
                     q.dma_start(out=buf[:, 0:nk * mg], in_=s), "WT", 16)
                b.wait("tensor", "WT")
                for mc in range(mg // 128):
                    m_abs = gi * mg + mc * 128
                    mi = m_abs // 128
                    bank = "A" if mi % 2 == 0 else "B"
                    if len(evac_marks) >= 2:
                        b.wait("tensor", "ACT", evac_marks[-2])
                    for k in range(nk):
                        b.op("tensor", lambda q, k=k, mc=mc, bank=bank, buf=buf, mg=mg:
                             q.matmul(ps[bank][:],
                                      buf[:, k * mg + mc * 128:k * mg + (mc + 1) * 128],
                                      rhs_tv[:, k * rhs_cols:k * rhs_cols + NT],
                                      start=(k == 0), stop=(k == nk - 1)),
                             "PE" if k == nk - 1 else None)
                    b.wait("scalar", "PE")
                    af = {"gelu": AF.Gelu, "silu": AF.Silu}.get(act)
                    if act == "ss":
                        b.wait("scalar", "DVE")
                        b.op("scalar", lambda q, mi=mi, bank=bank: q.activation(
                            S["bf_b"][:, 0:NT], ps[bank][:], AF.Silu,
                            bias=bias_tv[:, mi:mi + 1]), "ACT")
                        b.op("scalar", lambda q, mi=mi, bank=bank: q.activation(
                            S["bf_c"][:, 0:NT], ps[bank][:], AF.Sigmoid,
                            bias=bias_tv[:, mi:mi + 1]), "ACT")
                        b.wait("vector", "ACT")
                        b.op("vector", lambda q, mi=mi: q.tensor_tensor(
                            out_tv[:, mi * NT:(mi + 1) * NT], S["bf_b"][:, 0:NT],
                            S["bf_c"][:, 0:NT], ALU.mult), "DVE")
                    else:
                        b.op("scalar", lambda q, mi=mi, bank=bank, af=af: q.activation(
                            out_tv[:, mi * NT:(mi + 1) * NT], ps[bank][:], af,
                            bias=bias_tv[:, mi:mi + 1]), "ACT")
                    evac_marks.append(b.val("ACT"))
                pe_marks[gi] = b.val("PE")

        def final_layer(wn, K, rhs_tv, rhs_cols, a, tt):
            nk = K // 128
            mg = min(1024, (8192 // (K // 128)) // 128 * 128)
            wsrc = inp[wn]
            b.wait("gpsimd", "PE")
            c0 = tt * NT
            b.wait("gpsimd", "PE")
            b.op("gpsimd", lambda q, a=a, c0=c0: q.dma_start(
                out=S["grow"][:], in_=S["gall"][a:a + 1, c0:c0 + NT]), "WT", 16)
            b.wait("tensor", "ACT")
            b.wait("tensor", "DVE")
            b.wait("tensor", "WT")
            b.op("tensor", lambda q: q.matmul(
                ps["C"][:], C["ones1x128_bf"][:], S["grow"][:],
                start=True, stop=True), "PE")
            b.wait("scalar", "PE")
            b.opf("scalar", lambda q: q.activation(S["gb"][:, 0:NT], ps["C"][:],
                                                   AF.Copy), "ACT")
            pe_marks = {}
            evac_marks = []
            nmg = D // mg
            for gi in range(nmg):
                buf = S["wb0"] if gi % 2 == 0 else S["wb1"]
                if gi >= 2:
                    b.wait("gpsimd", "PE", pe_marks[gi - 2])
                src = bass.AP(wsrc.tensor, gi * mg, [[D, 128], [128 * D, nk], [1, mg]])
                b.op("gpsimd", lambda q, s=src, buf=buf, nk=nk, mg=mg:
                     q.dma_start(out=buf[:, 0:nk * mg], in_=s), "WT", 16)
                b.wait("tensor", "WT")
                for mc in range(mg // 128):
                    m_abs = gi * mg + mc * 128
                    mi = m_abs // 128
                    bank = "A" if mi % 2 == 0 else "B"
                    if len(evac_marks) >= 2:
                        b.wait("tensor", "DVE", evac_marks[-2])
                    for k in range(nk):
                        b.op("tensor", lambda q, k=k, mc=mc, bank=bank, buf=buf, mg=mg,
                             last=(k == nk - 1 and a != 7): q.matmul(
                                 ps[bank][:],
                                 buf[:, k * mg + mc * 128:k * mg + (mc + 1) * 128],
                                 rhs_tv[:, k * rhs_cols:k * rhs_cols + NT],
                                 start=(k == 0), stop=last),
                             "PE" if (k == nk - 1 and a != 7) else None)
                    if a == 7:
                        b.op("tensor", lambda q, mi=mi, bank=bank, c0=c0: q.matmul(
                            ps[bank][:], W["b2s"][0:8, mi * 128:(mi + 1) * 128],
                            S["gall"][0:8, c0:c0 + NT], start=False, stop=True), "PE")
                    b.wait("vector", "PE")
                    b.op("vector", lambda q, bank=bank: q.tensor_tensor(
                        S["t32a"][:], ps[bank][:], S["gb"][:, 0:NT], ALU.mult))
                    b.op("vector", lambda q, mi=mi: q.tensor_tensor(
                        S["acc"][:, mi * NT:(mi + 1) * NT],
                        S["acc"][:, mi * NT:(mi + 1) * NT], S["t32a"][:], ALU.add),
                        "DVE")
                    evac_marks.append(b.val("DVE"))
                pe_marks[gi] = b.val("PE")

        for tt in range(NTT):
            c0 = tt * NT
            b.wait("sync", "PE")
            b.wait("sync", "DVE")
            b.wait("sync", "ST")
            for k in range(NCH):
                src = bass.AP(scr["xs"].tensor, k * 128 * T_CORE + c0,
                              [[T_CORE, 128], [1, NT]])
                b.op("sync", lambda q, s=src, k=k:
                     q.dma_start(out=S["xs"][:, k * NT:(k + 1) * NT], in_=s), "LD", 16)
            b.wait("vector", "LD")
            b.wait("vector", "DVE")
            b.opf("vector", lambda q: q.memset(S["acc"][:], 0.0), "DVE")
            b.wait("tensor", "LD")
            NEXPERTS = int(_os.environ.get("P2_NEXP", "8"))
            for a in range(NEXPERTS):
                layers = E_SPECS[a]
                rhs = S["xs"]
                rhs_cols = NT
                hoff = 0
                for li, (wn, K, M, act) in enumerate(layers[:-1]):
                    out_tv = TV(pool16, 0, 128,
                                S["h"].off + hoff, (M // 128) * NT)
                    stream_layer(wn, K, M, rhs, rhs_cols, out_tv, act, BIA[wn])
                    rhs = out_tv
                    rhs_cols = NT
                    hoff += (M // 128) * NT
                wn, K, M, act = layers[-1]
                final_layer(wn, K, rhs, rhs_cols, a, tt)
            if dbg and tt == 0:
                b.wait("gpsimd", "DVE")
                b.op("gpsimd", lambda q: q.dma_start(out=dbgo["d_acc"], in_=S["acc"][:]), "ST", 16)
                b.wait("vector", "ST")
                b.wait("scalar", "ST")
            b.wait("vector", "DVE")
            for k in range(NCH):
                sl = slice(k * NT, (k + 1) * NT)
                b.op("vector", lambda q, sl=sl: q.tensor_scalar(
                    S["t32a"][:], S["xs"][:, sl], 0.1, None, ALU.mult))
                b.op("vector", lambda q, sl=sl: q.tensor_tensor(
                    S["o2"][:, sl], S["acc"][:, sl], S["t32a"][:], ALU.add),
                    "DVE" if k == NCH - 1 else None)
            if dbg and tt == 0:
                b.wait("gpsimd", "DVE")
                b.op("gpsimd", lambda q: q.dma_start(out=dbgo["d_fin"], in_=S["o2"][:]), "ST", 16)
                b.wait("vector", "ST")
                b.wait("scalar", "ST")
            b.wait("scalar", "DVE")
            b.wait("vector", "DVE")
            post_ln_residual(tt, xb_in, xb_out, S["o2"])


    def phase_barrier():
        for eng in ("gpsimd", "sync"):
            for sem in ("PE", "ACT", "DVE"):
                b.wait(eng, sem)

    load_consts()
    prev = 0
    if 1 in phases:
        with scope():
            attn_phase(xbuf[prev], xbuf[1], "wqkvT", "woutT", "l0")
        prev = 1
    if 2 in phases:
        phase_barrier()
        with scope():
            ffn0_phase(xbuf[prev], xbuf[2])
        prev = 2
    if 3 in phases:
        phase_barrier()
        with scope():
            l1_attn_phase(xbuf[prev], xbuf[3])
        prev = 3
    if 4 in phases:
        phase_barrier()
        with scope():
            ffn1_phase(xbuf[prev], xbuf[4])
        prev = 4
    b.wait("gpsimd", "ST")
    b.wait("gpsimd", "WT")
    b.wait("gpsimd", "LD")

    with es, nc.Block() as block:
        @block.sync
        def _(q):
            for fn in b.ops["sync"]:
                fn(q)

        @block.gpsimd
        def _(q):
            for fn in b.ops["gpsimd"]:
                fn(q)

        @block.tensor
        def _(q):
            for fn in b.ops["tensor"]:
                fn(q)

        @block.vector
        def _(q):
            for fn in b.ops["vector"]:
                fn(q)

        @block.scalar
        def _(q):
            for fn in b.ops["scalar"]:
                fn(q)

    return nc



def _host_prep(inputs, phases=(1, 2, 3, 4)):
    c1, c2 = _rope_tables()
    m0, m1 = _causal_masks()
    ident = np.eye(128, dtype=np.float32)
    x = inputs["L0_attn_x"]
    common = dict(rc1=c1, rc2=c2, mask0=m0, mask1=m1, ident=ident)
    if 2 in phases:
        gi = lambda k: np.ascontiguousarray(np.asarray(inputs["L0_ffn_" + k],
                                                       dtype=np.float32))
        for a in range(3):
            common[f"w_a{a}_l0"] = gi("e0_w1")[a]
            common[f"w_a{a}_l1"] = gi("e0_w2")[a]
            common[f"b_w_a{a}_l0"] = _vprep(gi("e0_b1")[a])
        for i, a in enumerate(range(3, 6)):
            common[f"w_a{a}_l0"] = gi("e1_w1")[i]
            common[f"w_a{a}_l1"] = gi("e1_w2")[i]
            common[f"w_a{a}_l2"] = gi("e1_w3")[i]
            common[f"b_w_a{a}_l0"] = _vprep(gi("e1_b1")[i])
            common[f"b_w_a{a}_l1"] = _vprep(gi("e1_b2")[i])
        for i, a in enumerate(range(6, 8)):
            common[f"w_a{a}_l0"] = gi("e2_w1")[i]
            common[f"w_a{a}_l1"] = gi("e2_w2")[i]
            common[f"b_w_a{a}_l0"] = _vprep(gi("e2_b1")[i])
        common["gate_w1"] = gi("gate_w1")
        common["gate_w2"] = gi("gate_w2")
        common["gate_w3p"] = np.ascontiguousarray(gi("gate_w3")[:, PERM])
        common["aff_wp"] = np.ascontiguousarray(gi("aff_w")[:, PERM])
        common["gate_b1"] = _vprep(gi("gate_b1"))
        common["ln_scale"] = _vprep(gi("ln_scale"))
        common["ln_offset"] = _vprep(gi("ln_offset"))
        common["gate_b2"] = _vprep(gi("gate_b2"))
        common["gate_b3p"] = np.ascontiguousarray(gi("gate_b3")[PERM][:, None])
        common["aff_bp"] = np.ascontiguousarray(gi("aff_b")[PERM][:, None])
        b2s = np.stack([gi("e0_b2")[0], gi("e0_b2")[1], gi("e0_b2")[2],
                        gi("e1_b3")[0], gi("e1_b3")[1], gi("e1_b3")[2],
                        gi("e2_b2")[0], gi("e2_b2")[1]])
        common["b2stack"] = np.ascontiguousarray(b2s)
        common["usage"] = np.ascontiguousarray(
            np.asarray(inputs["L0_ffn_expert_usage"], dtype=np.float32)[None, :])
        gnan = np.zeros((128, 8), np.float32)
        gnan[:, 0] = 0.125
        gnan[:, 3] = 0.125
        common["gnan"] = gnan
        common["_noise"] = _noise_constant()
    if 1 in phases:
        common["wqkvT"] = _prep_qkvT(np.asarray(inputs["L0_attn_Wqkv"]))
        common["woutT"] = np.ascontiguousarray(np.asarray(inputs["L0_attn_Wout"]).T)
    if 3 in phases:
        common["l1_wqkvT"] = np.ascontiguousarray(np.asarray(inputs["L1_attn_w_qkv"],
                                                             dtype=np.float32).T)
        common["l1_woutT"] = np.ascontiguousarray(np.asarray(inputs["L1_attn_w_out"],
                                                             dtype=np.float32).T)
    if 4 in phases:
        g4 = lambda k: np.ascontiguousarray(np.asarray(inputs["L1_ffn_" + k],
                                                       dtype=np.float32))
        common["l1_ln_w"] = _vprep(g4("ln_w"))
        common["l1_ln_b"] = _vprep(g4("ln_b"))
        common["l1_router"] = g4("router_w")
        common["l1_gate"] = g4("gate_w")
        common["l1_up"] = g4("up_w")
        common["l1_down"] = g4("down_w")
        common["l1_shg"] = g4("sh_gate")
        common["l1_shu"] = g4("sh_up")
        common["l1_shd"] = g4("sh_down")
    noise = common.pop("_noise", None)
    in_maps = []
    for c in range(NCORES):
        xc = np.asarray(x[c * B_PER_CORE:(c + 1) * B_PER_CORE])
        x_fm = np.ascontiguousarray(xc.reshape(T_CORE, D).T)
        m = dict(common)
        m["x0"] = x_fm
        if noise is not None:
            nslice = noise[c * B_PER_CORE:(c + 1) * B_PER_CORE]
            npermed = nslice.transpose(2, 0, 1).reshape(8, T_CORE)[PERM]
            m["noise_p"] = np.ascontiguousarray(npermed)
        in_maps.append(m)
    return in_maps


FUSED = True


def kernel(**inputs):
    if FUSED:
        phase_groups = [(1, 2, 3, 4)]
    else:
        phase_groups = [(1,), (2,), (3,), (4,)]
    cur = inputs
    for pg in phase_groups:
        in_maps = _host_prep(cur, pg)
        nc = build_program(phases=pg)
        res = run_bass_kernel_spmd(nc, in_maps, core_ids=list(range(NCORES)))
        outs = []
        for c in range(NCORES):
            o = np.asarray(res.results[c]["xout"])
            outs.append(np.ascontiguousarray(o.T).reshape(B_PER_CORE, T_BATCH, D))
        x = np.concatenate(outs, axis=0).astype(np.float32)
        cur = dict(inputs)
        cur["L0_attn_x"] = x
    return x
